# version 01
# baseline (speedup 1.0000x reference)
import os
from collections import defaultdict
from contextlib import ExitStack

import numpy as np
import ml_dtypes

import concourse.bass as bass
import concourse.mybir as mybir
from concourse.bass_utils import run_bass_kernel_spmd

F32 = mybir.dt.float32
BF16 = mybir.dt.bfloat16
AF = mybir.ActivationFunctionType
ALU = mybir.AluOpType
AX = mybir.AxisListType
BF = ml_dtypes.bfloat16

NCH = 16
NT = 4
NH = 16
KCE = [16, 32, 48, 64]
SLOPES = [0.5 ** ((i + 1) * 0.5) for i in range(NH)]
WH = [128, 256, 256] + [512] * 13
NEG = -1.0e9
INV_SQRT_HD = np.float32(1.0 / np.sqrt(128.0))
FP = np.float32

def sc_of(j):
    return [j, 7 - j, 8 + j, 15 - j]

_SC_MAP = {}
for _j in range(4):
    for _u, _s in enumerate(sc_of(_j)):
        _SC_MAP[_s] = (_j, _u)

def flash_blocks(layer):
    col = 0
    out = []
    acol = {}
    for h in range(NH):
        ws = 512 if layer == "a" else WH[h]
        for u in range(4):
            for i in range(512 // ws):
                npos, ndiag = KCE[u], ws // 128
                if layer == "a":
                    if (u, i) not in acol:
                        acol[(u, i)] = col
                        col += npos + ndiag
                    b0 = acol[(u, i)]
                else:
                    b0 = col
                    col += npos + ndiag
                out.append(dict(h=h, u=u, i=i, W=ws, npos=npos, ndiag=ndiag, bias0=b0))
    return out, col

FB_A, BIASA_COLS = flash_blocks("a")
FB_B, BIASB_COLS = flash_blocks("b")

MASK_OFF = {}
_c = 0
for _wd in ((512, 0), (512, 1), (512, 2), (512, 3), (256, 0), (256, 1), (128, 0)):
    MASK_OFF[_wd] = _c
    _c += _wd[0]
MASK_COLS = _c

WSPEC = {
    "wqkv": (18, 2048), "wfc": (16, 2048), "wq": (16, 2048), "wk": (16, 2048),
    "wv": (16, 2048), "wo": (16, 2048), "sw1": (32, 4096), "sw2": (32, 4096),
    "sw3": (32, 4096), "w12": (64, 4096), "w3": (32, 4096), "router": (1, 128),
}


def _mk_masks():
    m = np.zeros((128, MASK_COLS), dtype=BF)
    for (w, d), off in MASK_OFF.items():
        k = np.arange(128)[:, None]
        q = np.arange(w)[None, :]
        m[:, off:off + w] = (128 * d + k <= q).astype(BF)
    return m

def _mk_bias(layer, j):
    fb, ncols = (FB_A, BIASA_COLS) if layer == "a" else (FB_B, BIASB_COLS)
    tab = np.zeros((128, ncols), dtype=np.float32)
    p = np.arange(128, dtype=np.float32)
    for b in fb:
        sc = sc_of(j)[b["u"]]
        q0 = sc * 512 + b["i"] * b["W"]
        slope = 0.0 if layer == "a" else SLOPES[b["h"]]
        R = q0 + b["W"] / 2.0
        for kc in range(b["npos"]):
            c = b["bias0"] + kc
            if (kc + 1) * 128 <= q0:
                tab[:, c] = slope * (128.0 * kc + p - R)
            else:
                tab[:, c] = NEG
        for dd in range(b["ndiag"]):
            c = b["bias0"] + b["npos"] + dd
            kc = q0 // 128 + dd
            tab[:, c] = slope * (128.0 * kc + p - R)
    return tab

def _wblocks(Wt, kcs, mblk):
    K, M = Wt.shape
    nb = M // mblk
    a = Wt.reshape(kcs, 128, nb, mblk).transpose(2, 1, 0, 3)
    return np.ascontiguousarray(a.reshape(nb, 128, kcs * mblk), dtype=np.float32)

def _host_weight_arrays(inp):
    w = {}
    w["wqkv"] = _wblocks(np.asarray(inp["L0_attn_a_w_qkv"], np.float32), 16, 128)
    w["wfc"] = _wblocks(np.asarray(inp["L0_attn_a_w_fc"], np.float32), 16, 128)
    for n, k in (("wq", "L0_attn_b_Wq"), ("wk", "L0_attn_b_Wk"),
                 ("wv", "L0_attn_b_Wv"), ("wo", "L0_attn_b_Wo")):
        w[n] = _wblocks(np.ascontiguousarray(np.asarray(inp[k], np.float32).T), 16, 128)
    w["sw1"] = _wblocks(np.ascontiguousarray(np.asarray(inp["L0_ffn_sw1"], np.float32).T), 16, 256)
    w["sw2"] = _wblocks(np.ascontiguousarray(np.asarray(inp["L0_ffn_sw2"], np.float32).T), 16, 256)
    s3 = np.ascontiguousarray(np.asarray(inp["L0_ffn_sw3"], np.float32).T)
    a = s3.reshape(64, 128, 16, 128).transpose(2, 0, 1, 3)
    a = a.reshape(16, 2, 32, 128, 128).transpose(0, 1, 3, 2, 4)
    w["sw3"] = np.ascontiguousarray(a.reshape(32, 128, 32 * 128), dtype=np.float32)
    w12t = np.asarray(inp["L0_ffn_w12"], np.float32).transpose(0, 2, 1)
    a = w12t.reshape(8, 16, 128, 8, 256).transpose(0, 3, 2, 1, 4)
    w["w12"] = np.ascontiguousarray(a.reshape(64, 128, 16 * 256), dtype=np.float32)
    w3t = np.asarray(inp["L0_ffn_w3"], np.float32).transpose(0, 2, 1)
    a = w3t.reshape(8, 8, 128, 16, 128).transpose(3, 0, 1, 2, 4)
    a = a.reshape(16, 2, 4, 8, 128, 128).transpose(0, 1, 4, 2, 3, 5)
    w["w3"] = np.ascontiguousarray(a.reshape(32, 128, 4 * 8 * 128), dtype=np.float32)
    rt = np.ascontiguousarray(np.asarray(inp["L0_ffn_router_w"], np.float32).T)
    a = rt.reshape(16, 128, 1, 8).transpose(2, 1, 0, 3)
    w["router"] = np.ascontiguousarray(a.reshape(1, 128, 16 * 8), dtype=np.float32)
    return w

def _host_x(x0, core):
    g, j = core // 4, core % 4
    toks = np.concatenate([s * 512 + np.arange(512) for s in sc_of(j)])
    xs = np.asarray(x0, np.float32)[g][toks]
    arr = xs.T.reshape(16, 128, 2048).transpose(1, 0, 2)
    return np.ascontiguousarray(arr, dtype=np.float32)

def make_in_maps(inputs):
    w = _host_weight_arrays(inputs)
    masks = _mk_masks().astype(np.float32)
    ident = np.eye(128, dtype=np.float32)
    ones = np.ones((128, 128), dtype=np.float32)
    onehot = np.zeros((8, 1024), dtype=np.float32)
    for e in range(8):
        onehot[e, e * 128:(e + 1) * 128] = 1
    bqkv = np.ascontiguousarray(np.asarray(inputs["L0_attn_a_b_qkv"], np.float32).reshape(18, 128).T)
    bfc = np.ascontiguousarray(np.asarray(inputs["L0_attn_a_b_fc"], np.float32).reshape(16, 128).T)
    ebias = np.ascontiguousarray(np.asarray(inputs["L0_ffn_expert_bias"], np.float32).reshape(8, 1))
    in_maps = []
    for core in range(8):
        j = core % 4
        m = {
            "x_in": _host_x(inputs["L0_attn_a_x"], core),
            "bqkv": bqkv, "bfc": bfc, "ebias": ebias,
            "ident": ident, "onesin": ones, "onehot": onehot, "masks": masks,
            "biasa": _mk_bias("a", j), "biasb": _mk_bias("b", j),
        }
        for n in WSPEC:
            m["wf_" + n] = w[n]
        in_maps.append(m)
    return in_maps

def reassemble(results):
    out = np.empty((2, 8192, 2048), dtype=np.float32)
    for core in range(8):
        g, j = core // 4, core % 4
        toks = np.concatenate([s * 512 + np.arange(512) for s in sc_of(j)])
        shard = results[core]["out"]
        xs = shard.transpose(1, 0, 2).reshape(2048, 2048)
        out[g, toks, :] = xs.T
    return out


class Prog:
    def __init__(self, nc, ctx):
        self.nc = nc
        self.sems = {}
        self.cnt = defaultdict(int)
        self.lastw = defaultdict(dict)
        names = ["z", "lw0", "lw1", "lw2", "lw3", "lx", "lq0", "lq1", "lq2", "lq3",
                 "ka0", "ka1", "ls", "lg", "mm", "ac", "dv", "ag"]
        names += ["cv_" + n for n in WSPEC]
        for s in names:
            self.sems[s] = ctx.enter_context(nc.semaphore(s))
        self.dma_sems = {n for n in names if n not in ("mm", "ac", "dv")}
        self._fence_a = nc.dram_tensor("fence_a", [1, 16], F32, kind="Internal").ap()
        self._fence_b = nc.dram_tensor("fence_b", [1, 16], F32, kind="Internal").ap()
        mynums = sorted(h.num for h in self.sems.values())
        myrange = range(mynums[0], mynums[-1] + 1)
        nc.gpsimd.dma_reset(myrange)
        for s in self.sems.values():
            nc.gpsimd.sem_clear(s)
        for _ in range(4):
            nc.gpsimd.nop()
        nc.gpsimd.dma_start(out=self._fence_b, in_=self._fence_a).then_inc(self.sems["z"], 16)
        self.cnt["z"] = 16
        for eng in ("sync", "tensor", "scalar", "vector"):
            e = getattr(nc, eng)
            for _ in range(24):
                e.nop()
            e.wait_ge(self.sems["z"], 16)
            self.lastw[eng]["z"] = 16
        nc.gpsimd.wait_ge(self.sems["z"], 16)
        self.lastw["gpsimd"]["z"] = 16

    def inc(self, inst, sem, n=1):
        inst.then_inc(self.sems[sem], n)
        self.cnt[sem] += n
        return (sem, self.cnt[sem])

    def ev(self, sem):
        return (sem, self.cnt[sem])

    def barrier(self):
        pass

    def wait(self, engname, ev):
        if ev is None:
            return
        if isinstance(ev, list):
            for e in ev:
                self.wait(engname, e)
            return
        sem, v = ev
        if sem in self.dma_sems:
            v = self.cnt[sem]
        if v <= 0 or self.lastw[engname].get(sem, 0) >= v:
            return
        getattr(self.nc, engname).wait_ge(self.sems[sem], v)
        self.lastw[engname][sem] = v


class Rot:
    def __init__(self, n):
        self.n = n
        self.free_ev = [None] * n
        self.i = 0

    def acquire(self):
        idx = self.i
        self.i = (self.i + 1) % self.n
        return idx, self.free_ev[idx]

    def release(self, idx, ev):
        self.free_ev[idx] = ev


def _ceil32(x):
    return (x + 31) // 32 * 32


SERIAL = int(os.environ.get("ATHENA_SERIAL", "0"))

class Builder:
    def __init__(self, upto="full"):
        self.upto = upto
        self.taps = []
        nc = bass.Bass("TRN2", target_bir_lowering=False, debug=False)
        self.nc = nc
        self.ctx = ExitStack()
        self.P = Prog(nc, self.ctx)
        self._declare_dram()
        self._plan_sbuf()
        self._psum()

    def _declare_dram(self):
        nc = self.nc
        self.d_in = {}
        self.d_in["x"] = nc.dram_tensor("x_in", [128, 16, 2048], F32, kind="ExternalInput").ap()
        for n, (nb, bl) in WSPEC.items():
            self.d_in["wf_" + n] = nc.dram_tensor("wf_" + n, [nb, 128, bl], F32, kind="ExternalInput").ap()
        for nm, shp, dt in (("bqkv", [128, 18], F32), ("bfc", [128, 16], F32),
                            ("ebias", [8, 1], F32), ("ident", [128, 128], F32),
                            ("onesin", [128, 128], F32), ("onehot", [8, 1024], F32),
                            ("masks", [128, MASK_COLS], F32),
                            ("biasa", [128, BIASA_COLS], F32),
                            ("biasb", [128, BIASB_COLS], F32)):
            self.d_in[nm] = nc.dram_tensor(nm, shp, dt, kind="ExternalInput").ap()
        self.out = nc.dram_tensor("out", [128, 16, 2048], F32, kind="ExternalOutput").ap()

        self.wb = {}
        for n, (nb, bl) in WSPEC.items():
            self.wb[n] = nc.dram_tensor("wb_" + n, [nb, 128, bl], BF16, kind="Internal").ap()
        self.qa_d = nc.dram_tensor("qa_d", [128, 16, 2048], BF16, kind="Internal").ap()
        self.qb_d = nc.dram_tensor("qb_d", [128, 16, 2048], BF16, kind="Internal").ap()
        self.aoa_d = nc.dram_tensor("aoa_d", [128, 16, 2048], BF16, kind="Internal").ap()
        self.aob_d = nc.dram_tensor("aob_d", [128, 16, 2048], BF16, kind="Internal").ap()
        self.xr_d = nc.dram_tensor("xr_d", [128, 16, 2048], F32, kind="Internal").ap()
        self.x2_d = nc.dram_tensor("x2_d", [128, 16, 2048], F32, kind="Internal").ap()
        self.ag1_in = nc.dram_tensor("ag1_in", [256, 2048], BF16, kind="Internal").ap()
        self.ag1_out = nc.dram_tensor("ag1_out", [1024, 2048], BF16, kind="Internal").ap()
        self.ag2_in = nc.dram_tensor("ag2_in", [4096, 2048], BF16, kind="Internal").ap()
        self.ag2_out = nc.dram_tensor("ag2_out", [16384, 2048], BF16, kind="Internal").ap()

    def _plan_sbuf(self):
        nc = self.nc
        self._sb_virt = []
        self._off = 0

        def A(name, shape, dtype):
            per = 1
            for s in shape[1:]:
                per *= s
            per *= mybir.dt.size(dtype)
            h = nc.virtual_sbuf_tensor(name, shape, dtype)
            self._sb_virt.append((h, self._off))
            self._off = _ceil32(self._off + per)
            return h

        self.c_ident = A("c_ident", [128, 128], BF16)
        self.c_identf = A("c_identf", [128, 128], F32)
        self.c_ones = A("c_ones", [128, 128], BF16)
        self.c_onehot = A("c_onehot", [8, 1024], BF16)
        self.c_masks = A("c_masks", [128, MASK_COLS], BF16)
        self.c_biasa = A("c_biasa", [128, BIASA_COLS], F32)
        self.c_bqkv = A("c_bqkv", [128, 18], F32)
        self.c_bqs = A("c_bqs", [128, 16], F32)
        self.c_bfc = A("c_bfc", [128, 16], F32)
        self.c_ebias = A("c_ebias", [8, 1], F32)
        self.c_router = A("c_router", [128, 128], BF16)
        self.s_xc = A("s_xc", [128, 4, 512], F32)
        self.s_st = A("s_st", [1, 4, 512], F32)
        self.s_st16 = A("s_st16", [1, 2, 512], BF16)
        self.s_ev = A("s_ev", [128, 6, 512], BF16)
        self.s_t1 = A("s_t1", [128, 6, 512], BF16)
        self.s_mb = A("s_mb", [128, 512], BF16)
        self.s_rb = A("s_rb", [128, 512], BF16)

        g1 = self._off
        self.s_y = A("s_y", [128, 16, 512], BF16)
        self.s_xb = A("s_xb", [128, 16, 512], BF16)
        self.s_mid = A("s_mid", [128, 64, 512], BF16)
        g1a = self._off
        self.s_ao = A("s_ao", [128, 16, 512], BF16)
        end_g1a = self._off
        self._off = g1a
        self.s_oacc = A("s_oacc", [128, 16, 512], BF16)
        self._off = max(self._off, end_g1a)
        self.s_w = A("s_w", [128, 4, 4096], BF16)
        self.s_gb = A("s_gb", [128, 8, 512], BF16)
        self.s_gr = A("s_gr", [8, 2, 512], F32)
        self.s_gr16 = A("s_gr16", [8, 2, 512], BF16)
        self.s_gf = A("s_gf", [8, 512], BF16)
        self.s_gtm = A("s_gtm", [128, 8, 8], F32)
        self.s_dbg = A("s_dbg", [128, 16, 8], BF16)
        self.s_dbg2 = A("s_dbg2", [128, 4, 512], BF16)
        self.s_gtm16 = A("s_gtm16", [128, 8, 8], BF16)
        end_g1 = self._off

        self._off = g1
        self.c_biasb = A("c_biasb", [128, BIASB_COLS], F32)
        self.s_ka = A("s_ka", [128, 2, 80, 128], BF16)
        self.s_va = A("s_va", [128, 2, 80, 128], BF16)
        self.s_vt = A("s_vt", [128, 2, 80, 128], BF16)
        self.s_q = A("s_q", [128, 4, 512], BF16)
        self.s_pt = A("s_pt", [128, 4, 512], BF16)
        self.s_zr = A("s_zr", [1, 4, 512], BF16)
        self.s_rbb = A("s_rbb", [128, 2, 512], BF16)
        self.s_aos = A("s_aos", [128, 4, 512], BF16)
        end_g2 = self._off

        self._off = max(end_g1, end_g2)
        assert self._off <= 207 * 1024, f"SBUF over budget: {self._off}"
        self.sbuf_total = self._off

    def _materialize_sbuf(self):
        nc = self.nc
        self.arena = self.ctx.enter_context(
            nc.sbuf_tensor("arena", [128, self._off], mybir.dt.uint8))
        base = nc.lookup_mloc(self.arena).addr
        for h, off in self._sb_virt:
            nc._tensor(h.name, list(h.shape), h.dtype, type="SB")
            mloc = nc.lookup_mloc(h)
            mloc.allocated = not nc.target_bir_lowering
            mloc.addr = base + off

    def _psum(self):
        nc = self.nc
        self.psA = self.ctx.enter_context(nc.psum_tensor("psA", [128, 4, 512], F32))
        self.psB = self.ctx.enter_context(nc.psum_tensor("psB", [128, 2, 512], F32))
        self.psZ = self.ctx.enter_context(nc.psum_tensor("psZ", [128, 1, 512], F32))
        self.psT = self.ctx.enter_context(nc.psum_tensor("psT", [128, 8, 128], BF16))
        self.rA = Rot(4)
        self.rB = Rot(2)
        self.rT = Rot(8)
        self._psZ_free = None
        self._psT_free = None
        self._vt_guard = [None, None]
        self._ao_free = None
        self._mid_free = None
        self._oacc_free = None
        self._gb_free = None
        self._gb_ready = None
        self._y_free = None
        self._xout_free = None
        self.wrot = Rot(4)

    def load_consts(self):
        nc, P = self.nc, self.P
        for sb, dr in ((self.c_biasa, "biasa"),
                       (self.c_bqkv, "bqkv"), (self.c_bfc, "bfc"),
                       (self.c_ebias, "ebias"), (self.c_identf, "ident")):
            P.inc(nc.sync.dma_start(out=sb[:], in_=self.d_in[dr]), "lx", 16)
        ev_f32 = P.ev("lx")
        for sb, dr in ((self.c_ident, "ident"), (self.c_ones, "onesin"),
                       (self.c_onehot, "onehot"), (self.c_masks, "masks")):
            P.inc(nc.gpsimd.dma_start(out=sb[:], in_=self.d_in[dr]), "lg", 16)
        self.ev_consts = [ev_f32, P.ev("lg")]
        P.wait("vector", self.ev_consts)
        i = nc.vector.tensor_scalar_mul(self.c_bqs[:, :], self.c_bqkv[:, 0:16], float(INV_SQRT_HD))
        self.ev_bqs = P.inc(i, "dv")

    def convert_weights(self):
        nc, P = self.nc, self.P
        self.ev_conv = {}
        order = ["wqkv", "wfc", "sw1", "sw2", "sw3", "w12", "w3", "router",
                 "wq", "wk", "wv", "wo"]
        for n in order:
            nb, bl = WSPEC[n]
            step = max(1, min(nb, (8 * 1024 * 1024) // (128 * bl * 4)))
            ev = None
            for b0 in range(0, nb, step):
                b1 = min(nb, b0 + step)
                i = nc.gpsimd.dma_start(out=self.wb[n][b0:b1], in_=self.d_in["wf_" + n][b0:b1])
                ev = P.inc(i, "cv_" + n, 16)
            self.ev_conv[n] = ev

    def wload(self, n, b, slot, prior_ev):
        nc, P = self.nc, self.P
        nb, bl = WSPEC[n]
        P.wait("sync", self.ev_conv[n])
        P.wait("sync", prior_ev)
        i = nc.sync.dma_start(out=self.s_w[:, slot, 0:bl], in_=self.wb[n][b])
        return P.inc(i, "lw%d" % slot, 16)

    def ln_stats_start(self):
        P = self.P
        sB, evB = self.rB.acquire()
        P.wait("tensor", evB)
        P.wait("tensor", self._psZ_free)
        return sB

    def ln_stat_chunk(self, c, sB, f32_ap, ev_ready):
        nc, P = self.nc, self.P
        P.wait("vector", ev_ready)
        i = nc.vector.tensor_copy(self.s_xb[:, c, :], f32_ap)
        ev_cast = P.inc(i, "dv")
        qslot = c % 4
        P.wait("scalar", ev_ready)
        P.wait("scalar", self._sq_free[qslot])
        i = nc.scalar.activation(self.s_t1[:, qslot, :], f32_ap, AF.Square)
        ev_sq = P.inc(i, "ac")
        P.wait("tensor", ev_cast)
        nc.tensor.matmul(self.psZ[0:1, 0, :], self.c_ones[:, 0:1], self.s_xb[:, c, :],
                         start=(c == 0), stop=(c == 15))
        P.wait("tensor", ev_sq)
        i = nc.tensor.matmul(self.psB[0:1, sB, :], self.c_ones[:, 0:1],
                             self.s_t1[:, qslot, :], start=(c == 0), stop=(c == 15))
        ev_mm = P.inc(i, "mm")
        self._sq_free[qslot] = ev_mm
        return [ev_cast, ev_sq]

    def ln_finish_stats(self, sB):
        nc, P = self.nc, self.P
        ev_mm = P.ev("mm")
        P.wait("vector", ev_mm)
        nc.vector.tensor_copy(self.s_st[:, 0, :], self.psZ[0:1, 0, :])
        i = nc.vector.tensor_copy(self.s_st[:, 1, :], self.psB[0:1, sB, :])
        ev_stats = P.inc(i, "dv")
        self.rB.release(sB, ev_stats)
        self._psZ_free = ev_stats
        return ev_stats

    def ln_apply(self, ev_stats, y_free_ev=None):
        nc, P = self.nc, self.P
        st, st16 = self.s_st, self.s_st16
        P.wait("vector", ev_stats)
        nc.vector.tensor_scalar_mul(st[:, 2, :], st[:, 0, :], 1.0 / 2048.0)
        nc.vector.tensor_scalar_mul(st[:, 3, :], st[:, 1, :], 1.0 / 2048.0)
        nc.vector.tensor_mul(st[:, 0, :], st[:, 2, :], st[:, 2, :])
        i = nc.vector.tensor_sub(st[:, 1, :], st[:, 3, :], st[:, 0, :])
        ev_var = P.inc(i, "dv")
        P.wait("scalar", ev_var)
        nc.scalar.activation(st[:, 0, :], st[:, 1, :], AF.Ln, bias=FP(1e-5))
        i = nc.scalar.activation(st16[:, 1, :], st[:, 0, :], AF.Exp, scale=FP(-0.5))
        ev_rstd = P.inc(i, "ac")
        P.wait("vector", ev_rstd)
        i = nc.vector.tensor_copy(st16[:, 0, :], st[:, 2, :])
        ev_m16 = P.inc(i, "dv")
        sA, evA = self.rA.acquire()
        sA2, evA2 = self.rA.acquire()
        P.wait("tensor", [ev_m16, ev_rstd, evA, evA2])
        nc.tensor.matmul(self.psA[:, sA, :], self.c_ones[0:1, :], st16[:, 0, :],
                         start=True, stop=True)
        i = nc.tensor.matmul(self.psA[:, sA2, :], self.c_ones[0:1, :], st16[:, 1, :],
                             start=True, stop=True)
        ev_bc = P.inc(i, "mm")
        P.wait("vector", ev_bc)
        nc.vector.tensor_copy(self.s_mb[:, :], self.psA[:, sA, :])
        i = nc.vector.tensor_copy(self.s_rb[:, :], self.psA[:, sA2, :])
        ev_bcs = P.inc(i, "dv")
        self.rA.release(sA, ev_bcs)
        self.rA.release(sA2, ev_bcs)
        P.wait("vector", y_free_ev)
        ev = None
        for c in range(16):
            sl = 4 + (c % 2)
            nc.vector.tensor_sub(self.s_t1[:, sl, :], self.s_xb[:, c, :], self.s_mb[:, :])
            i = nc.vector.tensor_mul(self.s_y[:, c, :], self.s_t1[:, sl, :], self.s_rb[:, :])
            if c == 15:
                ev = P.inc(i, "dv")
        return ev

    def ln_from_dram(self, src, ts, y_free_ev=None):
        nc, P = self.nc, self.P
        self._sq_free = [[P.ev("mm"), P.ev("dv")]] * 4
        sB = self.ln_stats_start()
        xrot = Rot(4)
        pend = {}
        for c in range(16):
            slot, evfree = xrot.acquire()
            P.wait("sync", evfree)
            i = nc.sync.dma_start(out=self.s_xc[:, slot, :], in_=src[:, c, ts])
            pend[c] = (slot, P.inc(i, "lx", 16))
            if c >= 3:
                cc = c - 3
                sl, ev = pend.pop(cc)
                rel = self.ln_stat_chunk(cc, sB, self.s_xc[:, sl, :], ev)
                xrot.release(sl, rel)
        for cc in range(13, 16):
            sl, ev = pend.pop(cc)
            rel = self.ln_stat_chunk(cc, sB, self.s_xc[:, sl, :], ev)
            xrot.release(sl, rel)
        ev_stats = self.ln_finish_stats(sB)
        self.P.barrier()
        ev = self.ln_apply(ev_stats, y_free_ev)
        self.P.barrier()
        return ev

    def proj(self, wname, y_ap_fn, mlist, evac_fn, extra_rhs_ev=None):
        nc, P = self.nc, self.P
        wrot = self.wrot
        pend = {}
        for idx in range(min(2, len(mlist))):
            slot, evfree = wrot.acquire()
            pend[idx] = (slot, self.wload(wname, mlist[idx], slot, evfree))
        nxt = min(2, len(mlist))
        for idx, m in enumerate(mlist):
            slot, ev_w = pend.pop(idx)
            sB, evB = self.rB.acquire()
            P.wait("tensor", [ev_w, evB, extra_rhs_ev])
            wv = self.s_w[:, slot, 0:2048].rearrange("p (k m) -> p k m", m=128)
            last = None
            for k in range(16):
                last = nc.tensor.matmul(self.psB[:, sB, :], wv[:, k, :], y_ap_fn(k),
                                        start=(k == 0), stop=(k == 15))
            ev_mm = P.inc(last, "mm")
            wrot.release(slot, ev_mm)
            if nxt < len(mlist):
                s2, evf2 = wrot.acquire()
                pend[nxt] = (s2, self.wload(wname, mlist[nxt], s2, evf2))
                nxt += 1
            ev_done = evac_fn(idx, m, self.psB[:, sB, :], ev_mm)
            self.rB.release(sB, ev_done)
            self.P.barrier()

    def phase0(self):
        nc, P = self.nc, self.P
        P.wait("tensor", self.ev_consts)
        evrot = Rot(6)
        y_free = None
        for t in range(NT):
            ts = slice(t * 512, (t + 1) * 512)
            ev_y = self.ln_from_dram(self.d_in["x"], ts, y_free)

            def evac(idx, m, ps, ev_mm, ts=ts):
                slot, evfree = evrot.acquire()
                P.wait("scalar", [ev_mm, evfree, self.ev_bqs])
                if m < 16:
                    i = nc.scalar.activation(self.s_ev[:, slot, :], ps, AF.Identity,
                                             bias=self.c_bqs[:, m:m + 1], scale=INV_SQRT_HD)
                else:
                    i = nc.scalar.activation(self.s_ev[:, slot, :], ps, AF.Identity,
                                             bias=self.c_bqkv[:, m:m + 1], scale=FP(1.0))
                ev_a = P.inc(i, "ac")
                if m < 16:
                    dst = self.qa_d[:, m, ts]
                elif m == 16:
                    dst = self.ag1_in[0:128, ts]
                else:
                    dst = self.ag1_in[128:256, ts]
                i = nc.scalar.dma_start(out=dst, in_=self.s_ev[:, slot, :])
                ev_st = P.inc(i, "ls", 16)
                evrot.release(slot, ev_st)
                return ev_a

            P.wait("tensor", ev_y)
            self.proj("wqkv", lambda k: self.s_y[:, k, :], list(range(18)), evac)
            y_free = P.ev("mm")

    def _span_loads(self, layer, h, buf, guard=None):
        nc, P = self.nc, self.P
        P.wait("sync", guard)
        if layer == "a":
            agout, agin, stride, krow, vrow = self.ag1_out, self.ag1_in, 256, 0, 128
        else:
            agout, agin, stride = self.ag2_out, self.ag2_in, 4096
            krow, vrow = h * 128, 2048 + h * 128
        for dst_t, row in ((self.s_ka, krow), (self.s_va, vrow)):
            for sgc in range(16):
                rho, sig = _SC_MAP[sgc]
                src = agout[rho * stride + row: rho * stride + row + 128,
                            sig * 512:(sig + 1) * 512]
                i = nc.sync.dma_start(
                    out=dst_t[:, buf, 4 * sgc:4 * sgc + 4, :],
                    in_=src.rearrange("p (c m) -> p c m", m=128))
                P.inc(i, "ka%d" % buf, 16)
            for u in range(4):
                src = agin[row:row + 128, u * 512:(u + 1) * 512]
                i = nc.sync.dma_start(
                    out=dst_t[:, buf, 64 + 4 * u:64 + 4 * u + 4, :],
                    in_=src.rearrange("p (c m) -> p c m", m=128))
                P.inc(i, "ka%d" % buf, 16)
        return P.ev("ka%d" % buf)

    def _emit_transpose_quad(self, buf, q):
        nc, P = self.nc, self.P
        P.wait("tensor", self._psT_free)
        P.wait("tensor", self._vt_guard[buf])
        P.wait("tensor", P.ev("ka%d" % buf))
        last = None
        with nc.allow_low_precision(reason="bf16 transpose staging, no accumulation"):
            for d in range(4):
                last = nc.tensor.transpose(self.psT[:, d, :], self.s_va[:, buf, 4 * q + d, :],
                                           self.c_ident[:, :])
        ev_t = P.inc(last, "mm")
        P.wait("vector", ev_t)
        i = nc.vector.tensor_copy(
            self.s_vt[:, buf, 4 * q:4 * q + 4, :],
            self.psT[:, 0:4, :])
        self._psT_free = P.inc(i, "dv")

    def flash_layer(self, layer):
        nc, P = self.nc, self.P
        P.wait("tensor", self.ev_consts)
        P.wait("sync", P.ev("ls"))
        if layer == "b":
            i = nc.sync.dma_start(out=self.c_biasb[:], in_=self.d_in["biasb"])
            P.inc(i, "lx", 16)
        self._bias_ev = P.ev("lx")
        fb = FB_A if layer == "a" else FB_B
        qsrc = self.qa_d if layer == "a" else self.qb_d
        aod = self.aoa_d if layer == "a" else self.aob_d
        btab = self.c_biasa if layer == "a" else self.c_biasb
        qrot = Rot(4)
        ptrot = Rot(4)
        zrot = Rot(4)
        rbrot = Rot(2)
        aorot = Rot(4)
        by_head = defaultdict(list)
        for b in fb:
            by_head[b["h"]].append(b)

        def load_q(b):
            slot, evfree = qrot.acquire()
            P.wait("sync", evfree)
            i = nc.sync.dma_start(
                out=self.s_q[:, slot, 0:b["W"]],
                in_=qsrc[:, b["h"], b["u"] * 512 + b["i"] * b["W"]:
                         b["u"] * 512 + (b["i"] + 1) * b["W"]])
            P.inc(i, "lq%d" % slot, 16)
            return slot

        if layer == "a":
            self._span_loads("a", 0, 0, guard=self._vt_guard[0])
            for q in range(20):
                self._emit_transpose_quad(0, q)
            head_buf = {h: 0 for h in range(NH)}
        else:
            self._span_loads("b", 0, 0, guard=self._vt_guard[0])
            for q in range(20):
                self._emit_transpose_quad(0, q)
            head_buf = {h: h % 2 for h in range(NH)}

        pend_tq = []
        for h in range(NH):
            buf = head_buf[h]
            pieces = by_head[h]
            if layer == "b" and h + 1 < NH:
                self._span_loads("b", h + 1, (h + 1) % 2,
                                 guard=self._vt_guard[(h + 1) % 2])
                pend_tq = [((h + 1) % 2, q) for q in range(20)]
            nq = max(1, (len(pend_tq) + len(pieces) - 1) // max(1, len(pieces)))
            qslots = {}
            for idx in range(min(2, len(pieces))):
                qslots[idx] = load_q(pieces[idx])
            for idx, b in enumerate(pieces):
                qslot = qslots.pop(idx)
                if idx + 2 < len(pieces):
                    qslots[idx + 2] = load_q(pieces[idx + 2])
                self._flash_piece(layer, b, buf, qslot, qrot, ptrot, zrot, rbrot,
                                  aorot, btab, aod)
                for _ in range(nq):
                    if pend_tq:
                        tbuf, tq = pend_tq.pop(0)
                        self._emit_transpose_quad(tbuf, tq)
            while pend_tq:
                tbuf, tq = pend_tq.pop(0)
                self._emit_transpose_quad(tbuf, tq)
            self._vt_guard[buf] = P.ev("mm")

    def _flash_piece(self, layer, b, buf, qslot, qrot, ptrot, zrot, rbrot, aorot,
                     btab, aod):
        nc, P = self.nc, self.P
        W = b["W"]
        n_all = b["npos"] + b["ndiag"]
        wdiv = W // 128
        sO, evO = self.rB.acquire()
        P.wait("tensor", [evO, self._psZ_free, P.ev("ka%d" % buf),
                          P.ev("lq%d" % qslot)])
        state = []
        for ci in range(n_all):
            if ci < b["npos"]:
                pos = ci
                bias_c = b["bias0"] + ci
                diag = None
            else:
                dd = ci - b["npos"]
                pos = 64 + 4 * b["u"] + b["i"] * wdiv + dd
                bias_c = b["bias0"] + b["npos"] + dd
                diag = dd
            sA, evA = self.rA.acquire()
            P.wait("tensor", evA)
            i = nc.tensor.matmul(self.psA[:, sA, 0:W], self.s_ka[:, buf, pos, :],
                                 self.s_q[:, qslot, 0:W], start=True, stop=True)
            ev_qk = P.inc(i, "mm")
            pt, evp = ptrot.acquire()
            P.wait("scalar", [ev_qk, evp, self._bias_ev])
            i = nc.scalar.activation(self.s_pt[:, pt, 0:W], self.psA[:, sA, 0:W],
                                     AF.Exp, bias=btab[:, bias_c:bias_c + 1],
                                     scale=FP(1.0))
            ev_exp = P.inc(i, "ac")
            self.rA.release(sA, ev_exp)
            ev_ready = ev_exp
            if diag is not None:
                mo = MASK_OFF[(W, diag)]
                P.wait("vector", ev_exp)
                i = nc.vector.tensor_mul(self.s_pt[:, pt, 0:W], self.s_pt[:, pt, 0:W],
                                         self.c_masks[:, mo:mo + W])
                ev_ready = P.inc(i, "dv")
            state.append((ci, pt, ev_ready))
            if len(state) > 1:
                self._flash_zav(state.pop(0), b, buf, sO, W, n_all, ptrot)
        self._flash_zav(state.pop(0), b, buf, sO, W, n_all, ptrot)
        ev_mm = P.ev("mm")
        zs, evz = zrot.acquire()
        P.wait("vector", [ev_mm, evz])
        with nc.allow_low_precision(reason="softmax normalizer to bf16 for broadcast"):
            i = nc.vector.reciprocal(self.s_zr[0:1, zs, 0:W], self.psZ[0:1, 0, 0:W])
        ev_zr = P.inc(i, "dv")
        self._psZ_free = ev_zr
        sA, evA = self.rA.acquire()
        P.wait("tensor", [ev_zr, evA])
        i = nc.tensor.matmul(self.psA[:, sA, 0:W], self.c_ones[0:1, :],
                             self.s_zr[0:1, zs, 0:W], start=True, stop=True)
        ev_bc = P.inc(i, "mm")
        zrot.release(zs, ev_bc)
        rb, evrb = rbrot.acquire()
        P.wait("vector", [ev_bc, evrb])
        i = nc.vector.tensor_copy(self.s_rbb[:, rb, 0:W], self.psA[:, sA, 0:W])
        ev_rbc = P.inc(i, "dv")
        self.rA.release(sA, ev_rbc)
        ao, evao = aorot.acquire()
        P.wait("vector", evao)
        i = nc.vector.tensor_mul(self.s_aos[:, ao, 0:W], self.psB[:, sO, 0:W],
                                 self.s_rbb[:, rb, 0:W])
        ev_n = P.inc(i, "dv")
        rbrot.release(rb, ev_n)
        self.rB.release(sO, ev_n)
        qrot.release(qslot, P.ev("mm"))
        P.wait("scalar", ev_n)
        c0 = b["u"] * 512 + b["i"] * W
        i = nc.scalar.dma_start(out=aod[:, b["h"], c0:c0 + W],
                                in_=self.s_aos[:, ao, 0:W])
        ev_st = P.inc(i, "ls", 16)
        aorot.release(ao, ev_st)

    def _flash_zav(self, st, b, buf, sO, W, n_all, ptrot):
        nc, P = self.nc, self.P
        ci, pt, ev_ready = st
        if ci < b["npos"]:
            pos = ci
        else:
            pos = 64 + 4 * b["u"] + b["i"] * (W // 128) + (ci - b["npos"])
        P.wait("tensor", ev_ready)
        nc.tensor.matmul(self.psZ[0:1, 0, 0:W], self.c_ones[:, 0:1],
                         self.s_pt[:, pt, 0:W], start=(ci == 0), stop=(ci == n_all - 1))
        i = nc.tensor.matmul(self.psB[:, sO, 0:W], self.s_vt[:, buf, pos, :],
                             self.s_pt[:, pt, 0:W], start=(ci == 0),
                             stop=(ci == n_all - 1))
        ev_av = P.inc(i, "mm")
        ptrot.release(pt, ev_av)

    def _ln_stats_begin(self, pool):
        P = self.P
        self._sq_free = [[P.ev("mm"), P.ev("dv")]] * 4
        if pool == "B":
            sl, ev = self.rB.acquire()
            ap = self.psB[0:1, sl, :]
        else:
            sl, ev = self.rA.acquire()
            ap = self.psA[0:1, sl, :]
        P.wait("tensor", [ev, self._psZ_free])
        return (pool, sl, ap)

    def _ln_stats_finish(self, hnd):
        nc, P = self.nc, self.P
        pool, sl, ap = hnd
        ev_mm = P.ev("mm")
        P.wait("vector", ev_mm)
        nc.vector.tensor_copy(self.s_st[:, 0, :], self.psZ[0:1, 0, :])
        i = nc.vector.tensor_copy(self.s_st[:, 1, :], ap)
        ev_stats = P.inc(i, "dv")
        (self.rB if pool == "B" else self.rA).release(sl, ev_stats)
        self._psZ_free = ev_stats
        return ev_stats

    def _ln_stat_chunk2(self, c, hnd, f32_ap, ev_ready):
        nc, P = self.nc, self.P
        pool, sl, ap = hnd
        P.wait("vector", ev_ready)
        i = nc.vector.tensor_copy(self.s_xb[:, c, :], f32_ap)
        ev_cast = P.inc(i, "dv")
        qslot = c % 4
        P.wait("scalar", [ev_ready, self._sq_free[qslot]])
        i = nc.scalar.activation(self.s_t1[:, qslot, :], f32_ap, AF.Square)
        ev_sq = P.inc(i, "ac")
        P.wait("tensor", ev_cast)
        nc.tensor.matmul(self.psZ[0:1, 0, :], self.c_ones[:, 0:1], self.s_xb[:, c, :],
                         start=(c == 0), stop=(c == 15))
        P.wait("tensor", ev_sq)
        i = nc.tensor.matmul(ap, self.c_ones[:, 0:1],
                             self.s_t1[:, qslot, :], start=(c == 0), stop=(c == 15))
        ev_mm = P.inc(i, "mm")
        self._sq_free[qslot] = ev_mm
        return [ev_cast, ev_sq]

    def phase_mid(self, layer):
        nc, P = self.nc, self.P
        for t in range(NT):
            ts = slice(t * 512, (t + 1) * 512)
            self._mid_tile(layer, t, ts)

    def _mid_tile(self, layer, t, ts):
        nc, P = self.nc, self.P
        ao_d = self.aoa_d if layer == 1 else self.aob_d
        res_src = self.d_in["x"] if layer == 1 else self.x2_d
        wname = "wfc" if layer == 1 else "wo"
        P.wait("sync", self._ao_free)
        P.wait("sync", P.ev("ls"))
        i = nc.sync.dma_start(out=self.s_ao[:, :, :], in_=ao_d[:, :, ts])
        P.inc(i, "lx", 16)
        ev_ao = P.ev("lx")
        hnd = self._ln_stats_begin("A")
        xin_rot = Rot(2)

        def evac(idx, m, ps, ev_mm, ts=ts, layer=layer):
            xs, evx = xin_rot.acquire()
            P.wait("sync", evx)
            P.wait("sync", P.ev("ls"))
            i = nc.sync.dma_start(out=self.s_xc[:, xs, :], in_=res_src[:, m, ts])
            P.inc(i, "lx", 16)
            P.wait("vector", [ev_mm, self._xout_free])
            P.wait("vector", P.ev("lx"))
            if layer == 1:
                i = nc.vector.scalar_tensor_tensor(
                    self.s_xc[:, 3, :], ps, self.c_bfc[:, m:m + 1],
                    self.s_xc[:, xs, :], ALU.add, ALU.add)
            else:
                i = nc.vector.scalar_tensor_tensor(
                    self.s_xc[:, 3, :], ps, 1.0,
                    self.s_xc[:, xs, :], ALU.mult, ALU.add)
            ev_x1 = P.inc(i, "dv")
            xin_rot.release(xs, ev_x1)
            rel = self._ln_stat_chunk2(m, hnd, self.s_xc[:, 3, :], ev_x1)
            P.wait("scalar", ev_x1)
            i = nc.scalar.dma_start(out=self.xr_d[:, m, ts], in_=self.s_xc[:, 3, :])
            ev_sp = P.inc(i, "ls", 16)
            self._xout_free = [rel[0], rel[1], ev_sp]
            return ev_x1

        self.proj(wname, lambda k: self.s_ao[:, k, :], list(range(16)), evac,
                  extra_rhs_ev=ev_ao)
        ev_stats = self._ln_stats_finish(hnd)
        ev_y = self.ln_apply(ev_stats, self._y_free)
        self._ffn_tile(t, ts, ev_y, layer)
        if layer == 1:
            ev_stats2 = self._ln_stats_finish(self._hnd2)
            ev_y2 = self.ln_apply(ev_stats2, [P.ev("mm"), P.ev("dv")])
            for wn, kind in (("wq", "q"), ("wk", "k"), ("wv", "v")):
                evr = Rot(3)

                def evac3(idx, m, ps, ev_mm, wn=wn, kind=kind, ts=ts, evr=evr):
                    slot, evfree = evr.acquire()
                    P.wait("scalar", [ev_mm, evfree])
                    sc = INV_SQRT_HD if kind == "q" else FP(1.0)
                    i = nc.scalar.activation(self.s_ev[:, slot, :], ps, AF.Identity,
                                             scale=sc)
                    ev_a = P.inc(i, "ac")
                    P.wait("scalar", ev_a)
                    if kind == "q":
                        dst = self.qb_d[:, m, ts]
                    elif kind == "k":
                        dst = self.ag2_in[m * 128:(m + 1) * 128, ts]
                    else:
                        dst = self.ag2_in[2048 + m * 128:2048 + (m + 1) * 128, ts]
                    i = nc.scalar.dma_start(out=dst, in_=self.s_ev[:, slot, :])
                    ev_st = P.inc(i, "ls", 16)
                    evr.release(slot, ev_st)
                    return ev_a

                P.wait("tensor", ev_y2)
                self.proj(wn, lambda k: self.s_y[:, k, :], list(range(16)), evac3)
            self._y_free = P.ev("mm")
        else:
            self._y_free = P.ev("mm")

    def _ffn_tile(self, t, ts, ev_y, layer):
        nc, P = self.nc, self.P
        P.wait("tensor", ev_y)
        wrot = self.wrot
        t1rot = Rot(4)
        t1rot.free_ev = [[P.ev("mm"), P.ev("dv")]] * 4

        mid_guard = self._mid_free
        pend = {}
        wload4 = self.wload

        for bi in range(2):
            for wi, n in enumerate(("sw1", "sw2")):
                slot, evf = wrot.acquire()
                pend[(bi, wi)] = (slot, wload4(n, bi, slot, evf))
        nxt = 2
        for bi in range(32):
            s1, ev1 = pend.pop((bi, 0))
            s2, ev2 = pend.pop((bi, 1))
            wv1 = self.s_w[:, s1, :].rearrange("p (k m) -> p k m", m=256)
            wv2 = self.s_w[:, s2, :].rearrange("p (k m) -> p k m", m=256)
            for c2 in range(2):
                hc = bi * 2 + c2
                sg, evg = self.rA.acquire()
                su, evu = self.rA.acquire()
                P.wait("tensor", [ev1, ev2, evg, evu])
                for k in range(16):
                    nc.tensor.matmul(self.psA[:, sg, :], wv1[:, k, 128 * c2:128 * c2 + 128],
                                     self.s_y[:, k, :], start=(k == 0), stop=(k == 15))
                last = None
                for k in range(16):
                    last = nc.tensor.matmul(self.psA[:, su, :], wv2[:, k, 128 * c2:128 * c2 + 128],
                                            self.s_y[:, k, :], start=(k == 0), stop=(k == 15))
                ev_u = P.inc(last, "mm")
                tslot, evt = t1rot.acquire()
                P.wait("scalar", [ev_u, evt])
                i = nc.scalar.activation(self.s_t1[:, tslot, :], self.psA[:, sg, :], AF.Silu)
                ev_si = P.inc(i, "ac")
                P.wait("vector", [ev_si, mid_guard])
                if hc == 0:
                    nc.vector.tensor_copy(self.s_dbg2[:, 0, :], self.psA[:, sg, :])
                    nc.vector.tensor_copy(self.s_dbg2[:, 1, :], self.psA[:, su, :])
                    nc.vector.tensor_copy(self.s_dbg2[:, 2, :], self.s_t1[:, tslot, :])
                    nc.vector.tensor_copy(self.s_dbg2[:, 3, :], self.s_y[:, 0, :])
                self.rA.release(sg, ev_si)
                i = nc.vector.tensor_mul(self.s_mid[:, hc, :], self.s_t1[:, tslot, :],
                                         self.psA[:, su, :])
                ev_m = P.inc(i, "dv")
                self.rA.release(su, ev_m)
                t1rot.release(tslot, ev_m)
            ev_wdone = P.ev("mm")
            wrot.release(s1, ev_wdone)
            wrot.release(s2, ev_wdone)
            if nxt < 32:
                for wi, n in enumerate(("sw1", "sw2")):
                    slot, evf = wrot.acquire()
                    pend[(nxt, wi)] = (slot, wload4(n, nxt, slot, evf))
                nxt += 1

        self._router_gating(ev_y)

        ev_mid_a = P.ev("dv")
        pend = {}
        for ii in range(2):
            slot, evf = wrot.acquire()
            pend[ii] = (slot, wload4("sw3", ii, slot, evf))
        nxt = 2
        for m in range(16):
            outs = []
            for half in range(2):
                ii = m * 2 + half
                slot, ev_w = pend.pop(ii)
                wv = self.s_w[:, slot, :].rearrange("p (k m) -> p k m", m=128)
                if half == 0:
                    sB, evB = self.rB.acquire()
                    P.wait("tensor", [evB, ev_mid_a])
                P.wait("tensor", ev_w)
                last = None
                for kc in range(32):
                    last = nc.tensor.matmul(self.psB[:, sB, :], wv[:, kc, :],
                                            self.s_mid[:, half * 32 + kc, :],
                                            start=(half == 0 and kc == 0),
                                            stop=(half == 1 and kc == 31))
                ev_mm = P.inc(last, "mm")
                wrot.release(slot, ev_mm)
                if nxt < 32:
                    slot2, evf2 = wrot.acquire()
                    pend[nxt] = (slot2, wload4("sw3", nxt, slot2, evf2))
                    nxt += 1
            P.wait("vector", [ev_mm, self._oacc_free])
            i = nc.vector.tensor_copy(self.s_oacc[:, m, :], self.psB[:, sB, :])
            ev_oc = P.inc(i, "dv")
            self.rB.release(sB, ev_oc)

        ev_mid_rd = P.ev("mm")
        pend = {}
        order = []
        for e in range(8):
            for bp in range(4):
                order.append((e, bp))
        for ii in range(2):
            e, bp = order[ii]
            sl1, evf1 = wrot.acquire()
            sl2, evf2 = wrot.acquire()
            pend[ii] = ((sl1, wload4("w12", e * 8 + bp, sl1, evf1)),
                        (sl2, wload4("w12", e * 8 + 4 + bp, sl2, evf2)))
        nxt = 2
        for ii, (e, bp) in enumerate(order):
            (sl1, ev1), (sl2, ev2) = pend.pop(ii)
            wv1 = self.s_w[:, sl1, :].rearrange("p (k m) -> p k m", m=256)
            wv2 = self.s_w[:, sl2, :].rearrange("p (k m) -> p k m", m=256)
            for c2 in range(2):
                hc = 2 * bp + c2
                s1a, ev1a = self.rA.acquire()
                s2a, ev2a = self.rA.acquire()
                P.wait("tensor", [ev1, ev2, ev1a, ev2a])
                for k in range(16):
                    nc.tensor.matmul(self.psA[:, s1a, :], wv1[:, k, 128 * c2:128 * c2 + 128],
                                     self.s_y[:, k, :], start=(k == 0), stop=(k == 15))
                last = None
                for k in range(16):
                    last = nc.tensor.matmul(self.psA[:, s2a, :], wv2[:, k, 128 * c2:128 * c2 + 128],
                                            self.s_y[:, k, :], start=(k == 0), stop=(k == 15))
                ev_u = P.inc(last, "mm")
                tslot, evt = t1rot.acquire()
                P.wait("scalar", [ev_u, evt])
                i = nc.scalar.activation(self.s_t1[:, tslot, :], self.psA[:, s1a, :], AF.Silu)
                ev_si = P.inc(i, "ac")
                self.rA.release(s1a, ev_si)
                t2slot, evt2 = t1rot.acquire()
                P.wait("vector", [ev_si, evt2, ev_mid_rd, self._gb_ready])
                i = nc.vector.tensor_mul(self.s_t1[:, t2slot, :], self.s_t1[:, tslot, :],
                                         self.psA[:, s2a, :])
                i = nc.vector.tensor_mul(self.s_mid[:, e * 8 + hc, :],
                                         self.s_t1[:, t2slot, :], self.s_gb[:, e, :])
                ev_m = P.inc(i, "dv")
                self.rA.release(s2a, ev_m)
                t1rot.release(tslot, ev_m)
                t1rot.release(t2slot, ev_m)
            ev_wdone = P.ev("mm")
            wrot.release(sl1, ev_wdone)
            wrot.release(sl2, ev_wdone)
            if nxt < 32:
                e2, bp2 = order[nxt]
                sl1, evf1 = wrot.acquire()
                sl2, evf2 = wrot.acquire()
                pend[nxt] = ((sl1, wload4("w12", e2 * 8 + bp2, sl1, evf1)),
                             (sl2, wload4("w12", e2 * 8 + 4 + bp2, sl2, evf2)))
                nxt += 1

        if layer == 1:
            self._hnd2 = self._ln_stats_begin("A")
        ev_mid_d = P.ev("dv")
        pend = {}
        for ii in range(2):
            slot, evf = wrot.acquire()
            pend[ii] = (slot, wload4("w3", ii, slot, evf))
        nxt = 2
        xin_rot = Rot(2)
        for m in range(16):
            for half in range(2):
                ii = m * 2 + half
                slot, ev_w = pend.pop(ii)
                wv = self.s_w[:, slot, :].rearrange("p (k m) -> p k m", m=128)
                if half == 0:
                    sB, evB = self.rB.acquire()
                    P.wait("tensor", [evB, ev_mid_d])
                P.wait("tensor", ev_w)
                last = None
                for idx2 in range(32):
                    e4, kc = idx2 // 8, idx2 % 8
                    e = half * 4 + e4
                    last = nc.tensor.matmul(self.psB[:, sB, :], wv[:, idx2, :],
                                            self.s_mid[:, e * 8 + kc, :],
                                            start=(half == 0 and idx2 == 0),
                                            stop=(half == 1 and idx2 == 31))
                ev_mm = P.inc(last, "mm")
                wrot.release(slot, ev_mm)
                if nxt < 32:
                    slot2, evf2 = wrot.acquire()
                    pend[nxt] = (slot2, wload4("w3", nxt, slot2, evf2))
                    nxt += 1
            xs, evx = xin_rot.acquire()
            P.wait("sync", evx)
            P.wait("sync", P.ev("ls"))
            i = nc.sync.dma_start(out=self.s_xc[:, xs, :], in_=self.xr_d[:, m, ts])
            P.inc(i, "lx", 16)
            P.wait("vector", [ev_mm, P.ev("lx"), self._xout_free])
            i = nc.vector.scalar_tensor_tensor(
                self.s_xc[:, 2, :], self.psB[:, sB, :], 1.0,
                self.s_oacc[:, m, :], ALU.mult, ALU.add)
            i = nc.vector.tensor_add(self.s_xc[:, 3, :], self.s_xc[:, 2, :],
                                     self.s_xc[:, xs, :])
            ev_x2 = P.inc(i, "dv")
            self.rB.release(sB, ev_x2)
            xin_rot.release(xs, ev_x2)
            relx = [ev_x2]
            if layer == 1:
                relx = self._ln_stat_chunk2(m, self._hnd2, self.s_xc[:, 3, :], ev_x2)
            P.wait("scalar", ev_x2)
            dst = self.x2_d[:, m, ts] if layer == 1 else self.out[:, m, ts]
            i = nc.scalar.dma_start(out=dst, in_=self.s_xc[:, 3, :])
            ev_sp = P.inc(i, "ls", 16)
            self._xout_free = relx + [ev_sp]
        self._oacc_free = [P.ev("mm"), P.ev("dv")]
        self._mid_free = P.ev("mm")
        self._ao_free = [P.ev("mm"), P.ev("dv")]

    def _router_gating(self, ev_y):
        nc, P = self.nc, self.P
        P.wait("tensor", [self._psZ_free, self.ev_router, ev_y])
        last = None
        for k in range(16):
            last = nc.tensor.matmul(self.psZ[0:8, 0, :], self.c_router[:, k * 8:(k + 1) * 8],
                                    self.s_y[:, k, :], start=(k == 0), stop=(k == 15))
        ev_mm = P.inc(last, "mm")
        P.wait("scalar", ev_mm)
        i = nc.scalar.activation(self.s_gr[:, 1, :], self.psZ[0:8, 0, :], AF.Tanh,
                                 scale=FP(0.5))
        ev_t = P.inc(i, "ac")
        P.wait("vector", ev_t)
        i = nc.vector.tensor_scalar(self.s_gr[:, 0, :], self.s_gr[:, 1, :], 0.5, 0.5,
                                    ALU.mult, ALU.add)
        ev_sc = P.inc(i, "dv")
        self._psZ_free = ev_sc
        i = nc.vector.tensor_scalar_add(self.s_gr[:, 1, :], self.s_gr[:, 0, :],
                                        self.c_ebias[:, 0:1])
        nc.vector.tensor_copy(self.s_gr16[:, 0, :], self.s_gr[:, 0, :])
        i = nc.vector.tensor_copy(self.s_gr16[:, 1, :], self.s_gr[:, 1, :])
        ev_16 = P.inc(i, "dv")
        P.wait("tensor", [ev_16, self._psT_free])
        with nc.allow_low_precision(reason="router transpose staging"):
            for d in range(4):
                nc.tensor.transpose(self.psT[:, d, 0:8], self.s_gr16[0:8, 1, 128 * d:128 * (d + 1)],
                                    self.c_ident[0:8, 0:8])
            last = None
            for d in range(4):
                last = nc.tensor.transpose(self.psT[:, 4 + d, 0:8],
                                           self.s_gr16[0:8, 0, 128 * d:128 * (d + 1)],
                                           self.c_ident[0:8, 0:8])
        ev_tr = P.inc(last, "mm")
        P.wait("vector", ev_tr)
        i = nc.vector.tensor_copy(self.s_gtm16[:, 0:8, :], self.psT[:, 0:8, 0:8])
        ev_tm = P.inc(i, "dv")
        self._psT_free = ev_tm
        nc.vector.tensor_copy(self.s_dbg[:, 0:8, :], self.s_gtm16[:, 0:8, :])
        g = self.s_gtm

        def _sync(i):
            ev = P.inc(i, "dv")
            P.wait("vector", ev)
            return ev

        for d in range(4):
            sel = self.s_gtm16[:, d, :]
            sco = self.s_gtm16[:, 4 + d, :]
            _sync(nc.vector.reduce_max(g[:, 0, 0:1], sel, AX.X))
            _sync(nc.vector.tensor_scalar(g[:, 1, :], sel, g[:, 0, 0:1], None, ALU.is_ge))
            _sync(nc.vector.scalar_tensor_tensor(g[:, 2, :], g[:, 1, :], -1e30, sel,
                                                 ALU.mult, ALU.add))
            _sync(nc.vector.reduce_max(g[:, 3, 0:1], g[:, 2, :], AX.X))
            _sync(nc.vector.tensor_scalar(g[:, 4, :], sel, g[:, 3, 0:1], None, ALU.is_ge))
            _sync(nc.vector.tensor_mul(g[:, 5, :], g[:, 4, :], sco))
            _sync(nc.vector.reduce_sum(g[:, 6, 0:1], g[:, 5, :], AX.X))
            _sync(nc.vector.tensor_scalar_add(g[:, 6, 0:1], g[:, 6, 0:1], 1e-9))
            _sync(nc.vector.reciprocal(g[:, 7, 0:1], g[:, 6, 0:1]))
            ev_g = _sync(nc.vector.tensor_scalar(self.s_gtm16[:, d, :], g[:, 5, :],
                                                 g[:, 7, 0:1], None, ALU.mult))
            if d == 0:
                _sync(nc.vector.tensor_copy(self.s_dbg[:, 8:16, :], g[:, :, :]))
        nc.vector.tensor_copy(self.s_dbg[:, 8:12, :], self.s_gtm16[:, 0:4, :])
        P.wait("tensor", [ev_g, self._psT_free])
        with nc.allow_low_precision(reason="router transpose staging"):
            last = None
            for d in range(4):
                last = nc.tensor.transpose(self.psT[0:8, d, :],
                                           self.s_gtm16[:, d, :],
                                           self.c_ident[:, :])
        ev_tr2 = P.inc(last, "mm")
        P.wait("vector", ev_tr2)
        last = None
        for d in range(4):
            last = nc.vector.tensor_copy(self.s_gf[0:8, 128 * d:128 * (d + 1)],
                                         self.psT[0:8, d, 0:128])
        ev_gf = P.inc(last, "dv")
        self._psT_free = ev_gf
        for e in range(8):
            sA, evA = self.rA.acquire()
            P.wait("tensor", [ev_gf, evA])
            i = nc.tensor.matmul(self.psA[:, sA, :], self.c_onehot[0:8, e * 128:(e + 1) * 128],
                                 self.s_gf[0:8, :], start=True, stop=True)
            ev_b = P.inc(i, "mm")
            P.wait("vector", [ev_b, self._gb_free])
            i = nc.vector.tensor_copy(self.s_gb[:, e, :], self.psA[:, sA, :])
            ev_c = P.inc(i, "dv")
            self.rA.release(sA, ev_c)
        self._gb_ready = P.ev("dv")
        self._gb_free = None

    def allgather(self, which):
        nc, P = self.nc, self.P
        groups = [[0, 1, 2, 3], [4, 5, 6, 7]]
        ain, aout = (self.ag1_in, self.ag1_out) if which == 1 else (self.ag2_in, self.ag2_out)
        P.wait("gpsimd", P.ev("ls"))
        i = nc.gpsimd.collective_compute(
            "AllGather", ALU.bypass, replica_groups=groups,
            ins=[ain.opt()], outs=[aout.opt()])
        return P.inc(i, "ag", 1)

    def tap(self, name, src, shape, dt):
        self.taps.append((name, src, shape, dt))

    def finish(self):
        nc, P = self.nc, self.P
        for sname in list(P.sems):
            if sname != "z":
                P.wait("gpsimd", P.ev(sname))
        for name, src, shape, dt in self.taps:
            o = nc.dram_tensor("tap_" + name, shape, dt, kind="ExternalOutput").ap()
            i = nc.gpsimd.dma_start(out=o, in_=src)
            P.inc(i, "lg", 16)
        P.wait("gpsimd", P.ev("lg"))
        for sh in P.sems.values():
            nc.gpsimd.sem_clear(sh)
        self.ctx.close()
        return self.nc

    def build(self):
        self._materialize_sbuf()
        self.load_consts()
        self.convert_weights()
        nc, P = self.nc, self.P
        P.wait("sync", self.ev_conv["router"])
        i = nc.sync.dma_start(out=self.c_router[:], in_=self.wb["router"][0])
        P.inc(i, "lx", 16)
        self.ev_router = P.ev("lx")
        self.phase0()
        ev_ag1 = self.allgather(1)
        if self.upto == "p0":
            self.tap("qa", self.qa_d, [128, 16, 2048], BF16)
            self.tap("ag1", self.ag1_out, [1024, 2048], BF16)
            return self.finish()
        P = self.P
        for e in ("sync", "tensor", "scalar", "vector"):
            P.wait(e, ev_ag1)
        self.flash_layer("a")
        if self.upto == "p1a":
            self.tap("aoa", self.aoa_d, [128, 16, 2048], BF16)
            return self.finish()
        self.phase_mid(1)
        if self.upto == "p1b":
            self.tap("x2", self.x2_d, [128, 16, 2048], F32)
            self.tap("qb", self.qb_d, [128, 16, 2048], BF16)
            self.tap("ag2in", self.ag2_in, [4096, 2048], BF16)
            self.tap("xr", self.xr_d, [128, 16, 2048], F32)
            self.tap("mid", self.s_mid[:, :, :], [128, 64, 512], BF16)
            self.tap("oacc", self.s_oacc[:, :, :], [128, 16, 512], BF16)
            self.tap("gb", self.s_gb[:, :, :], [128, 8, 512], BF16)
            self.tap("ylast", self.s_y[:, :, :], [128, 16, 512], BF16)
            self.tap("gf", self.s_gf[:, :], [8, 512], BF16)
            self.tap("gtm16", self.s_gtm16[:, :, :], [128, 8, 8], BF16)
            self.tap("gr", self.s_gr[:, :, :], [8, 2, 512], F32)
            self.tap("gr16", self.s_gr16[:, :, :], [8, 2, 512], BF16)
            self.tap("dbg", self.s_dbg[:, :, :], [128, 16, 8], BF16)
            self.tap("dbg2", self.s_dbg2[:, :, :], [128, 4, 512], BF16)
            self.tap("wsw1b0", self.wb["sw1"][0:1], [1, 128, 4096], BF16)
            self.tap("wsw3b0", self.wb["sw3"][0:1], [1, 128, 4096], BF16)
            return self.finish()
        ev_ag2 = self.allgather(2)
        for e in ("sync", "tensor", "scalar", "vector"):
            P.wait(e, ev_ag2)
        self.flash_layer("b")
        if self.upto == "p2a":
            self.tap("aob", self.aob_d, [128, 16, 2048], BF16)
            return self.finish()
        self.phase_mid(2)
        return self.finish()


_CACHE = {}

def _get_nc(upto):
    if upto not in _CACHE:
        _CACHE[upto] = Builder(upto=upto).build()
    return _CACHE[upto]


def kernel(**inputs):
    upto = os.environ.get("ATHENA_UPTO", "full")
    nc = _get_nc(upto)
    in_maps = make_in_maps(inputs)
    res = run_bass_kernel_spmd(nc, in_maps, core_ids=list(range(8)))
    return reassemble(res.results)


# version 02
# speedup vs baseline: 1.1129x; 1.1129x over previous
import os
from collections import defaultdict
from contextlib import ExitStack

import numpy as np
import ml_dtypes

import concourse.bass as bass
import concourse.mybir as mybir
from concourse.bass_utils import run_bass_kernel_spmd

F32 = mybir.dt.float32
BF16 = mybir.dt.bfloat16
AF = mybir.ActivationFunctionType
ALU = mybir.AluOpType
AX = mybir.AxisListType
BF = ml_dtypes.bfloat16

NCH = 16
NT = 4
NH = 16
KCE = [16, 32, 48, 64]
SLOPES = [0.5 ** ((i + 1) * 0.5) for i in range(NH)]
WH = [128, 256, 256] + [512] * 13
NEG = -1.0e9
INV_SQRT_HD = np.float32(1.0 / np.sqrt(128.0))
FP = np.float32

def sc_of(j):
    return [j, 7 - j, 8 + j, 15 - j]

_SC_MAP = {}
for _j in range(4):
    for _u, _s in enumerate(sc_of(_j)):
        _SC_MAP[_s] = (_j, _u)

def flash_blocks(layer):
    col = 0
    out = []
    acol = {}
    for h in range(NH):
        ws = 512 if layer == "a" else WH[h]
        for u in range(4):
            for i in range(512 // ws):
                npos, ndiag = KCE[u], ws // 128
                if layer == "a":
                    if (u, i) not in acol:
                        acol[(u, i)] = col
                        col += npos + ndiag
                    b0 = acol[(u, i)]
                else:
                    b0 = col
                    col += npos + ndiag
                out.append(dict(h=h, u=u, i=i, W=ws, npos=npos, ndiag=ndiag, bias0=b0))
    return out, col

FB_A, BIASA_COLS = flash_blocks("a")
FB_B, BIASB_COLS = flash_blocks("b")

MASK_OFF = {}
_c = 0
for _wd in ((512, 0), (512, 1), (512, 2), (512, 3), (256, 0), (256, 1), (128, 0)):
    MASK_OFF[_wd] = _c
    _c += _wd[0]
MASK_COLS = _c

WSPEC = {
    "wqkv": (18, 2048), "wfc": (16, 2048), "wq": (16, 2048), "wk": (16, 2048),
    "wv": (16, 2048), "wo": (16, 2048), "sw1": (32, 4096), "sw2": (32, 4096),
    "sw3": (32, 4096), "w12": (64, 4096), "w3": (32, 4096), "router": (1, 128),
}


def _mk_masks():
    m = np.zeros((128, MASK_COLS), dtype=BF)
    for (w, d), off in MASK_OFF.items():
        k = np.arange(128)[:, None]
        q = np.arange(w)[None, :]
        m[:, off:off + w] = (128 * d + k <= q).astype(BF)
    return m

def _mk_bias(layer, j):
    fb, ncols = (FB_A, BIASA_COLS) if layer == "a" else (FB_B, BIASB_COLS)
    tab = np.zeros((128, ncols), dtype=np.float32)
    p = np.arange(128, dtype=np.float32)
    for b in fb:
        sc = sc_of(j)[b["u"]]
        q0 = sc * 512 + b["i"] * b["W"]
        slope = 0.0 if layer == "a" else SLOPES[b["h"]]
        R = q0 + b["W"] / 2.0
        for kc in range(b["npos"]):
            c = b["bias0"] + kc
            if (kc + 1) * 128 <= q0:
                tab[:, c] = slope * (128.0 * kc + p - R)
            else:
                tab[:, c] = NEG
        for dd in range(b["ndiag"]):
            c = b["bias0"] + b["npos"] + dd
            kc = q0 // 128 + dd
            tab[:, c] = slope * (128.0 * kc + p - R)
    return tab

def _wblocks(Wt, kcs, mblk):
    K, M = Wt.shape
    nb = M // mblk
    a = Wt.reshape(kcs, 128, nb, mblk).transpose(2, 1, 0, 3)
    return np.ascontiguousarray(a.reshape(nb, 128, kcs * mblk), dtype=np.float32)

def _host_weight_arrays(inp):
    w = {}
    w["wqkv"] = _wblocks(np.asarray(inp["L0_attn_a_w_qkv"], np.float32), 16, 128)
    w["wfc"] = _wblocks(np.asarray(inp["L0_attn_a_w_fc"], np.float32), 16, 128)
    for n, k in (("wq", "L0_attn_b_Wq"), ("wk", "L0_attn_b_Wk"),
                 ("wv", "L0_attn_b_Wv"), ("wo", "L0_attn_b_Wo")):
        w[n] = _wblocks(np.ascontiguousarray(np.asarray(inp[k], np.float32).T), 16, 128)
    w["sw1"] = _wblocks(np.ascontiguousarray(np.asarray(inp["L0_ffn_sw1"], np.float32).T), 16, 256)
    w["sw2"] = _wblocks(np.ascontiguousarray(np.asarray(inp["L0_ffn_sw2"], np.float32).T), 16, 256)
    s3 = np.ascontiguousarray(np.asarray(inp["L0_ffn_sw3"], np.float32).T)
    a = s3.reshape(64, 128, 16, 128).transpose(2, 0, 1, 3)
    a = a.reshape(16, 2, 32, 128, 128).transpose(0, 1, 3, 2, 4)
    w["sw3"] = np.ascontiguousarray(a.reshape(32, 128, 32 * 128), dtype=np.float32)
    w12t = np.asarray(inp["L0_ffn_w12"], np.float32).transpose(0, 2, 1)
    a = w12t.reshape(8, 16, 128, 8, 256).transpose(0, 3, 2, 1, 4)
    w["w12"] = np.ascontiguousarray(a.reshape(64, 128, 16 * 256), dtype=np.float32)
    w3t = np.asarray(inp["L0_ffn_w3"], np.float32).transpose(0, 2, 1)
    a = w3t.reshape(8, 8, 128, 16, 128).transpose(3, 0, 1, 2, 4)
    a = a.reshape(16, 2, 4, 8, 128, 128).transpose(0, 1, 4, 2, 3, 5)
    w["w3"] = np.ascontiguousarray(a.reshape(32, 128, 4 * 8 * 128), dtype=np.float32)
    rt = np.ascontiguousarray(np.asarray(inp["L0_ffn_router_w"], np.float32).T)
    a = rt.reshape(16, 128, 1, 8).transpose(2, 1, 0, 3)
    w["router"] = np.ascontiguousarray(a.reshape(1, 128, 16 * 8), dtype=np.float32)
    return w

def _host_x(x0, core):
    g, j = core // 4, core % 4
    toks = np.concatenate([s * 512 + np.arange(512) for s in sc_of(j)])
    xs = np.asarray(x0, np.float32)[g][toks]
    arr = xs.T.reshape(16, 128, 2048).transpose(1, 0, 2)
    return np.ascontiguousarray(arr, dtype=np.float32)

def make_in_maps(inputs):
    w = _host_weight_arrays(inputs)
    masks = _mk_masks().astype(np.float32)
    ident = np.eye(128, dtype=np.float32)
    ones = np.ones((128, 128), dtype=np.float32)
    onehot = np.zeros((8, 1024), dtype=np.float32)
    for e in range(8):
        onehot[e, e * 128:(e + 1) * 128] = 1
    bqkv = np.ascontiguousarray(np.asarray(inputs["L0_attn_a_b_qkv"], np.float32).reshape(18, 128).T)
    bfc = np.ascontiguousarray(np.asarray(inputs["L0_attn_a_b_fc"], np.float32).reshape(16, 128).T)
    ebias = np.ascontiguousarray(np.asarray(inputs["L0_ffn_expert_bias"], np.float32).reshape(8, 1))
    in_maps = []
    for core in range(8):
        j = core % 4
        m = {
            "x_in": _host_x(inputs["L0_attn_a_x"], core),
            "bqkv": bqkv, "bfc": bfc, "ebias": ebias,
            "ident": ident, "onesin": ones, "onehot": onehot, "masks": masks,
            "biasa": _mk_bias("a", j), "biasb": _mk_bias("b", j),
        }
        for n in WSPEC:
            m["wf_" + n] = w[n]
        in_maps.append(m)
    return in_maps

def reassemble(results):
    out = np.empty((2, 8192, 2048), dtype=np.float32)
    for core in range(8):
        g, j = core // 4, core % 4
        toks = np.concatenate([s * 512 + np.arange(512) for s in sc_of(j)])
        shard = results[core]["out"]
        xs = shard.transpose(1, 0, 2).reshape(2048, 2048)
        out[g, toks, :] = xs.T
    return out


class Prog:
    def __init__(self, nc, ctx):
        self.nc = nc
        self.sems = {}
        self.cnt = defaultdict(int)
        self.lastw = defaultdict(dict)
        names = ["z", "lw0", "lw1", "lw2", "lw3", "lx", "lq0", "lq1", "lq2", "lq3",
                 "ka0", "ka1", "ls", "lg", "mm", "ac", "dv", "ag"]
        names += ["cv_" + n for n in WSPEC]
        for s in names:
            self.sems[s] = ctx.enter_context(nc.semaphore(s))
        self.dma_sems = {n for n in names if n not in ("mm", "ac", "dv")}
        self._fence_a = nc.dram_tensor("fence_a", [1, 16], F32, kind="Internal").ap()
        self._fence_b = nc.dram_tensor("fence_b", [1, 16], F32, kind="Internal").ap()
        mynums = sorted(h.num for h in self.sems.values())
        myrange = range(mynums[0], mynums[-1] + 1)
        nc.gpsimd.dma_reset(myrange)
        for s in self.sems.values():
            nc.gpsimd.sem_clear(s)
        for _ in range(4):
            nc.gpsimd.nop()
        nc.gpsimd.dma_start(out=self._fence_b, in_=self._fence_a).then_inc(self.sems["z"], 16)
        self.cnt["z"] = 16
        for eng in ("sync", "tensor", "scalar", "vector"):
            e = getattr(nc, eng)
            for _ in range(24):
                e.nop()
            e.wait_ge(self.sems["z"], 16)
            self.lastw[eng]["z"] = 16
        nc.gpsimd.wait_ge(self.sems["z"], 16)
        self.lastw["gpsimd"]["z"] = 16

    def inc(self, inst, sem, n=1):
        inst.then_inc(self.sems[sem], n)
        self.cnt[sem] += n
        return (sem, self.cnt[sem])

    def ev(self, sem):
        return (sem, self.cnt[sem])

    def barrier(self):
        pass

    def wait(self, engname, ev):
        if ev is None:
            return
        if isinstance(ev, list):
            for e in ev:
                self.wait(engname, e)
            return
        sem, v = ev
        if sem in self.dma_sems:
            v = self.cnt[sem]
        if v <= 0 or self.lastw[engname].get(sem, 0) >= v:
            return
        getattr(self.nc, engname).wait_ge(self.sems[sem], v)
        self.lastw[engname][sem] = v


class Rot:
    def __init__(self, n):
        self.n = n
        self.free_ev = [None] * n
        self.i = 0

    def acquire(self):
        idx = self.i
        self.i = (self.i + 1) % self.n
        return idx, self.free_ev[idx]

    def release(self, idx, ev):
        self.free_ev[idx] = ev


def _ceil32(x):
    return (x + 31) // 32 * 32


SERIAL = int(os.environ.get("ATHENA_SERIAL", "0"))

class Builder:
    def __init__(self, upto="full"):
        self.upto = upto
        self.taps = []
        nc = bass.Bass("TRN2", target_bir_lowering=False, debug=False)
        self.nc = nc
        self.ctx = ExitStack()
        self.P = Prog(nc, self.ctx)
        self._declare_dram()
        self._plan_sbuf()
        self._psum()

    def _declare_dram(self):
        nc = self.nc
        self.d_in = {}
        self.d_in["x"] = nc.dram_tensor("x_in", [128, 16, 2048], F32, kind="ExternalInput").ap()
        for n, (nb, bl) in WSPEC.items():
            self.d_in["wf_" + n] = nc.dram_tensor("wf_" + n, [nb, 128, bl], F32, kind="ExternalInput").ap()
        for nm, shp, dt in (("bqkv", [128, 18], F32), ("bfc", [128, 16], F32),
                            ("ebias", [8, 1], F32), ("ident", [128, 128], F32),
                            ("onesin", [128, 128], F32), ("onehot", [8, 1024], F32),
                            ("masks", [128, MASK_COLS], F32),
                            ("biasa", [128, BIASA_COLS], F32),
                            ("biasb", [128, BIASB_COLS], F32)):
            self.d_in[nm] = nc.dram_tensor(nm, shp, dt, kind="ExternalInput").ap()
        self.out = nc.dram_tensor("out", [128, 16, 2048], F32, kind="ExternalOutput").ap()

        self.wb = {}
        for n, (nb, bl) in WSPEC.items():
            self.wb[n] = nc.dram_tensor("wb_" + n, [nb, 128, bl], BF16, kind="Internal").ap()
        self.qa_d = nc.dram_tensor("qa_d", [128, 16, 2048], BF16, kind="Internal").ap()
        self.qb_d = nc.dram_tensor("qb_d", [128, 16, 2048], BF16, kind="Internal").ap()
        self.aoa_d = nc.dram_tensor("aoa_d", [128, 16, 2048], BF16, kind="Internal").ap()
        self.aob_d = nc.dram_tensor("aob_d", [128, 16, 2048], BF16, kind="Internal").ap()
        self.xr_d = nc.dram_tensor("xr_d", [128, 16, 2048], F32, kind="Internal").ap()
        self.x2_d = nc.dram_tensor("x2_d", [128, 16, 2048], F32, kind="Internal").ap()
        self.ag1_in = nc.dram_tensor("ag1_in", [256, 2048], BF16, kind="Internal").ap()
        self.ag1_out = nc.dram_tensor("ag1_out", [1024, 2048], BF16, kind="Internal").ap()
        self.ag2_in = nc.dram_tensor("ag2_in", [4096, 2048], BF16, kind="Internal").ap()
        self.ag2_out = nc.dram_tensor("ag2_out", [16384, 2048], BF16, kind="Internal").ap()

    def _plan_sbuf(self):
        nc = self.nc
        self._sb_virt = []
        self._off = 0

        def A(name, shape, dtype):
            per = 1
            for s in shape[1:]:
                per *= s
            per *= mybir.dt.size(dtype)
            h = nc.virtual_sbuf_tensor(name, shape, dtype)
            self._sb_virt.append((h, self._off))
            self._off = _ceil32(self._off + per)
            return h

        self.c_ident = A("c_ident", [128, 128], BF16)
        self.c_identf = A("c_identf", [128, 128], F32)
        self.c_ones = A("c_ones", [128, 128], BF16)
        self.c_onehot = A("c_onehot", [8, 1024], BF16)
        self.c_masks = A("c_masks", [128, MASK_COLS], BF16)
        self.c_biasa = A("c_biasa", [128, BIASA_COLS], F32)
        self.c_bqkv = A("c_bqkv", [128, 18], F32)
        self.c_bqs = A("c_bqs", [128, 16], F32)
        self.c_bfc = A("c_bfc", [128, 16], F32)
        self.c_ebias = A("c_ebias", [8, 1], F32)
        self.c_router = A("c_router", [128, 128], BF16)
        self.s_xc = A("s_xc", [128, 4, 512], F32)
        self.s_st = A("s_st", [1, 4, 512], F32)
        self.s_st16 = A("s_st16", [1, 2, 512], BF16)
        self.s_ev = A("s_ev", [128, 6, 512], BF16)
        self.s_t1 = A("s_t1", [128, 6, 512], BF16)
        self.s_mb = A("s_mb", [128, 512], BF16)
        self.s_rb = A("s_rb", [128, 512], BF16)

        g1 = self._off
        self.s_y = A("s_y", [128, 16, 512], BF16)
        self.s_xb = A("s_xb", [128, 16, 512], BF16)
        self.s_mid = A("s_mid", [128, 64, 512], BF16)
        g1a = self._off
        self.s_ao = A("s_ao", [128, 16, 512], BF16)
        end_g1a = self._off
        self._off = g1a
        self.s_oacc = A("s_oacc", [128, 16, 512], BF16)
        self._off = max(self._off, end_g1a)
        self.s_w = A("s_w", [128, 4, 4096], BF16)
        self.s_gb = A("s_gb", [128, 8, 512], BF16)
        self.s_gr = A("s_gr", [8, 2, 512], F32)
        self.s_gr16 = A("s_gr16", [8, 2, 512], BF16)
        self.s_gf = A("s_gf", [8, 512], BF16)
        self.s_gtm = A("s_gtm", [128, 8, 8], F32)
        self.s_dbg = A("s_dbg", [128, 16, 8], BF16)
        self.s_dbg2 = A("s_dbg2", [128, 4, 512], BF16)
        self.s_gtm16 = A("s_gtm16", [128, 8, 8], BF16)
        end_g1 = self._off

        self._off = g1
        self.c_biasb = A("c_biasb", [128, BIASB_COLS], F32)
        self.s_ka = A("s_ka", [128, 2, 80, 128], BF16)
        self.s_va = A("s_va", [128, 2, 80, 128], BF16)
        self.s_vt = A("s_vt", [128, 2, 80, 128], BF16)
        self.s_q = A("s_q", [128, 4, 512], BF16)
        self.s_pt = A("s_pt", [128, 4, 512], BF16)
        self.s_zr = A("s_zr", [1, 4, 512], BF16)
        self.s_rbb = A("s_rbb", [128, 2, 512], BF16)
        self.s_aos = A("s_aos", [128, 4, 512], BF16)
        end_g2 = self._off

        self._off = max(end_g1, end_g2)
        assert self._off <= 207 * 1024, f"SBUF over budget: {self._off}"
        self.sbuf_total = self._off

    def _materialize_sbuf(self):
        nc = self.nc
        self.arena = self.ctx.enter_context(
            nc.sbuf_tensor("arena", [128, self._off], mybir.dt.uint8))
        base = nc.lookup_mloc(self.arena).addr
        for h, off in self._sb_virt:
            nc._tensor(h.name, list(h.shape), h.dtype, type="SB")
            mloc = nc.lookup_mloc(h)
            mloc.allocated = not nc.target_bir_lowering
            mloc.addr = base + off

    def _psum(self):
        nc = self.nc
        self.psA = self.ctx.enter_context(nc.psum_tensor("psA", [128, 4, 512], F32))
        self.psB = self.ctx.enter_context(nc.psum_tensor("psB", [128, 2, 512], F32))
        self.psZ = self.ctx.enter_context(nc.psum_tensor("psZ", [128, 1, 512], F32))
        self.psT = self.ctx.enter_context(nc.psum_tensor("psT", [128, 8, 128], BF16))
        self.rA = Rot(4)
        self.rB = Rot(2)
        self.rT = Rot(8)
        self._psZ_free = None
        self._psT_free = None
        self._vt_guard = [None, None]
        self._ao_free = None
        self._mid_free = None
        self._oacc_free = None
        self._gb_free = None
        self._gb_ready = None
        self._y_free = None
        self._xout_free = None
        self.wrot = Rot(4)

    def load_consts(self):
        nc, P = self.nc, self.P
        for sb, dr in ((self.c_biasa, "biasa"),
                       (self.c_bqkv, "bqkv"), (self.c_bfc, "bfc"),
                       (self.c_ebias, "ebias"), (self.c_identf, "ident")):
            P.inc(nc.sync.dma_start(out=sb[:], in_=self.d_in[dr]), "lx", 16)
        ev_f32 = P.ev("lx")
        for sb, dr in ((self.c_ident, "ident"), (self.c_ones, "onesin"),
                       (self.c_onehot, "onehot"), (self.c_masks, "masks")):
            P.inc(nc.gpsimd.dma_start(out=sb[:], in_=self.d_in[dr]), "lg", 16)
        self.ev_consts = [ev_f32, P.ev("lg")]
        P.wait("vector", self.ev_consts)
        i = nc.vector.tensor_scalar_mul(self.c_bqs[:, :], self.c_bqkv[:, 0:16], float(INV_SQRT_HD))
        self.ev_bqs = P.inc(i, "dv")

    def convert_weights(self, order):
        nc, P = self.nc, self.P
        for n in order:
            nb, bl = WSPEC[n]
            step = max(1, min(nb, (8 * 1024 * 1024) // (128 * bl * 4)))
            ev = None
            for b0 in range(0, nb, step):
                b1 = min(nb, b0 + step)
                i = nc.gpsimd.dma_start(out=self.wb[n][b0:b1], in_=self.d_in["wf_" + n][b0:b1])
                ev = P.inc(i, "cv_" + n, 16)
            self.ev_conv[n] = ev

    def wload(self, n, b, slot, prior_ev):
        nc, P = self.nc, self.P
        nb, bl = WSPEC[n]
        P.wait("sync", self.ev_conv[n])
        P.wait("sync", prior_ev)
        i = nc.sync.dma_start(out=self.s_w[:, slot, 0:bl], in_=self.wb[n][b])
        return P.inc(i, "lw%d" % slot, 16)

    def ln_stats_start(self):
        P = self.P
        sB, evB = self.rB.acquire()
        P.wait("tensor", evB)
        P.wait("tensor", self._psZ_free)
        return sB

    def ln_stat_chunk(self, c, sB, f32_ap, ev_ready):
        nc, P = self.nc, self.P
        P.wait("vector", ev_ready)
        i = nc.vector.tensor_copy(self.s_xb[:, c, :], f32_ap)
        ev_cast = P.inc(i, "dv")
        qslot = c % 4
        P.wait("scalar", ev_ready)
        P.wait("scalar", self._sq_free[qslot])
        i = nc.scalar.activation(self.s_t1[:, qslot, :], f32_ap, AF.Square)
        ev_sq = P.inc(i, "ac")
        P.wait("tensor", ev_cast)
        nc.tensor.matmul(self.psZ[0:1, 0, :], self.c_ones[:, 0:1], self.s_xb[:, c, :],
                         start=(c == 0), stop=(c == 15))
        P.wait("tensor", ev_sq)
        i = nc.tensor.matmul(self.psB[0:1, sB, :], self.c_ones[:, 0:1],
                             self.s_t1[:, qslot, :], start=(c == 0), stop=(c == 15))
        ev_mm = P.inc(i, "mm")
        self._sq_free[qslot] = ev_mm
        return [ev_cast, ev_sq]

    def ln_finish_stats(self, sB):
        nc, P = self.nc, self.P
        ev_mm = P.ev("mm")
        P.wait("vector", ev_mm)
        nc.vector.tensor_copy(self.s_st[:, 0, :], self.psZ[0:1, 0, :])
        i = nc.vector.tensor_copy(self.s_st[:, 1, :], self.psB[0:1, sB, :])
        ev_stats = P.inc(i, "dv")
        self.rB.release(sB, ev_stats)
        self._psZ_free = ev_stats
        return ev_stats

    def ln_apply(self, ev_stats, y_free_ev=None):
        nc, P = self.nc, self.P
        st, st16 = self.s_st, self.s_st16
        P.wait("vector", ev_stats)
        nc.vector.tensor_scalar_mul(st[:, 2, :], st[:, 0, :], 1.0 / 2048.0)
        nc.vector.tensor_scalar_mul(st[:, 3, :], st[:, 1, :], 1.0 / 2048.0)
        nc.vector.tensor_mul(st[:, 0, :], st[:, 2, :], st[:, 2, :])
        i = nc.vector.tensor_sub(st[:, 1, :], st[:, 3, :], st[:, 0, :])
        ev_var = P.inc(i, "dv")
        P.wait("scalar", ev_var)
        nc.scalar.activation(st[:, 0, :], st[:, 1, :], AF.Ln, bias=FP(1e-5))
        i = nc.scalar.activation(st16[:, 1, :], st[:, 0, :], AF.Exp, scale=FP(-0.5))
        ev_rstd = P.inc(i, "ac")
        P.wait("vector", ev_rstd)
        i = nc.vector.tensor_copy(st16[:, 0, :], st[:, 2, :])
        ev_m16 = P.inc(i, "dv")
        sA, evA = self.rA.acquire()
        sA2, evA2 = self.rA.acquire()
        P.wait("tensor", [ev_m16, ev_rstd, evA, evA2])
        nc.tensor.matmul(self.psA[:, sA, :], self.c_ones[0:1, :], st16[:, 0, :],
                         start=True, stop=True)
        i = nc.tensor.matmul(self.psA[:, sA2, :], self.c_ones[0:1, :], st16[:, 1, :],
                             start=True, stop=True)
        ev_bc = P.inc(i, "mm")
        P.wait("vector", ev_bc)
        nc.vector.tensor_copy(self.s_mb[:, :], self.psA[:, sA, :])
        i = nc.vector.tensor_copy(self.s_rb[:, :], self.psA[:, sA2, :])
        ev_bcs = P.inc(i, "dv")
        self.rA.release(sA, ev_bcs)
        self.rA.release(sA2, ev_bcs)
        P.wait("vector", y_free_ev)
        ev = None
        for c in range(16):
            sl = 4 + (c % 2)
            nc.vector.tensor_sub(self.s_t1[:, sl, :], self.s_xb[:, c, :], self.s_mb[:, :])
            i = nc.vector.tensor_mul(self.s_y[:, c, :], self.s_t1[:, sl, :], self.s_rb[:, :])
            if c == 15:
                ev = P.inc(i, "dv")
        return ev

    def ln_from_dram(self, src, ts, y_free_ev=None):
        nc, P = self.nc, self.P
        self._sq_free = [[P.ev("mm"), P.ev("dv")]] * 4
        sB = self.ln_stats_start()
        xrot = Rot(4)
        pend = {}
        for c in range(16):
            slot, evfree = xrot.acquire()
            P.wait("sync", evfree)
            i = nc.sync.dma_start(out=self.s_xc[:, slot, :], in_=src[:, c, ts])
            pend[c] = (slot, P.inc(i, "lx", 16))
            if c >= 3:
                cc = c - 3
                sl, ev = pend.pop(cc)
                rel = self.ln_stat_chunk(cc, sB, self.s_xc[:, sl, :], ev)
                xrot.release(sl, rel)
        for cc in range(13, 16):
            sl, ev = pend.pop(cc)
            rel = self.ln_stat_chunk(cc, sB, self.s_xc[:, sl, :], ev)
            xrot.release(sl, rel)
        ev_stats = self.ln_finish_stats(sB)
        self.P.barrier()
        ev = self.ln_apply(ev_stats, y_free_ev)
        self.P.barrier()
        return ev

    def proj(self, wname, y_ap_fn, mlist, evac_fn, extra_rhs_ev=None):
        nc, P = self.nc, self.P
        wrot = self.wrot
        pend = {}
        for idx in range(min(2, len(mlist))):
            slot, evfree = wrot.acquire()
            pend[idx] = (slot, self.wload(wname, mlist[idx], slot, evfree))
        nxt = min(2, len(mlist))
        for idx, m in enumerate(mlist):
            slot, ev_w = pend.pop(idx)
            sB, evB = self.rB.acquire()
            P.wait("tensor", [ev_w, evB, extra_rhs_ev])
            wv = self.s_w[:, slot, 0:2048].rearrange("p (k m) -> p k m", m=128)
            last = None
            for k in range(16):
                last = nc.tensor.matmul(self.psB[:, sB, :], wv[:, k, :], y_ap_fn(k),
                                        start=(k == 0), stop=(k == 15))
            ev_mm = P.inc(last, "mm")
            wrot.release(slot, ev_mm)
            if nxt < len(mlist):
                s2, evf2 = wrot.acquire()
                pend[nxt] = (s2, self.wload(wname, mlist[nxt], s2, evf2))
                nxt += 1
            ev_done = evac_fn(idx, m, self.psB[:, sB, :], ev_mm)
            self.rB.release(sB, ev_done)
            self.P.barrier()

    def phase0(self):
        nc, P = self.nc, self.P
        P.wait("tensor", self.ev_consts)
        evrot = Rot(6)
        y_free = None
        for t in range(NT):
            ts = slice(t * 512, (t + 1) * 512)
            ev_y = self.ln_from_dram(self.d_in["x"], ts, y_free)

            def evac(idx, m, ps, ev_mm, ts=ts):
                slot, evfree = evrot.acquire()
                P.wait("scalar", [ev_mm, evfree, self.ev_bqs])
                if m < 16:
                    i = nc.scalar.activation(self.s_ev[:, slot, :], ps, AF.Identity,
                                             bias=self.c_bqs[:, m:m + 1], scale=INV_SQRT_HD)
                else:
                    i = nc.scalar.activation(self.s_ev[:, slot, :], ps, AF.Identity,
                                             bias=self.c_bqkv[:, m:m + 1], scale=FP(1.0))
                ev_a = P.inc(i, "ac")
                if m < 16:
                    dst = self.qa_d[:, m, ts]
                elif m == 16:
                    dst = self.ag1_in[0:128, ts]
                else:
                    dst = self.ag1_in[128:256, ts]
                i = nc.scalar.dma_start(out=dst, in_=self.s_ev[:, slot, :])
                ev_st = P.inc(i, "ls", 16)
                evrot.release(slot, ev_st)
                return ev_a

            P.wait("tensor", ev_y)
            self.proj("wqkv", lambda k: self.s_y[:, k, :], list(range(18)), evac)
            y_free = P.ev("mm")

    def _span_loads(self, layer, h, buf, guard=None):
        nc, P = self.nc, self.P
        P.wait("sync", guard)
        if layer == "a":
            agout, agin, stride, krow, vrow = self.ag1_out, self.ag1_in, 256, 0, 128
        else:
            agout, agin, stride = self.ag2_out, self.ag2_in, 4096
            krow, vrow = h * 128, 2048 + h * 128
        for dst_t, row in ((self.s_ka, krow), (self.s_va, vrow)):
            for sgc in range(16):
                rho, sig = _SC_MAP[sgc]
                src = agout[rho * stride + row: rho * stride + row + 128,
                            sig * 512:(sig + 1) * 512]
                i = nc.sync.dma_start(
                    out=dst_t[:, buf, 4 * sgc:4 * sgc + 4, :],
                    in_=src.rearrange("p (c m) -> p c m", m=128))
                P.inc(i, "ka%d" % buf, 16)
            for u in range(4):
                src = agin[row:row + 128, u * 512:(u + 1) * 512]
                i = nc.sync.dma_start(
                    out=dst_t[:, buf, 64 + 4 * u:64 + 4 * u + 4, :],
                    in_=src.rearrange("p (c m) -> p c m", m=128))
                P.inc(i, "ka%d" % buf, 16)
        return P.ev("ka%d" % buf)

    def _emit_transpose_quad(self, buf, q):
        nc, P = self.nc, self.P
        P.wait("tensor", self._psT_free)
        P.wait("tensor", self._vt_guard[buf])
        P.wait("tensor", P.ev("ka%d" % buf))
        last = None
        with nc.allow_low_precision(reason="bf16 transpose staging, no accumulation"):
            for d in range(4):
                last = nc.tensor.transpose(self.psT[:, d, :], self.s_va[:, buf, 4 * q + d, :],
                                           self.c_ident[:, :])
        ev_t = P.inc(last, "mm")
        P.wait("vector", ev_t)
        i = nc.vector.tensor_copy(
            self.s_vt[:, buf, 4 * q:4 * q + 4, :],
            self.psT[:, 0:4, :])
        self._psT_free = P.inc(i, "dv")

    def flash_layer(self, layer):
        nc, P = self.nc, self.P
        P.wait("tensor", self.ev_consts)
        P.wait("sync", P.ev("ls"))
        if layer == "b":
            i = nc.sync.dma_start(out=self.c_biasb[:], in_=self.d_in["biasb"])
            P.inc(i, "lx", 16)
        self._bias_ev = P.ev("lx")
        fb = FB_A if layer == "a" else FB_B
        qsrc = self.qa_d if layer == "a" else self.qb_d
        aod = self.aoa_d if layer == "a" else self.aob_d
        btab = self.c_biasa if layer == "a" else self.c_biasb
        qrot = Rot(4)
        ptrot = Rot(4)
        zrot = Rot(4)
        rbrot = Rot(2)
        aorot = Rot(4)
        by_head = defaultdict(list)
        for b in fb:
            by_head[b["h"]].append(b)

        def load_q(b):
            slot, evfree = qrot.acquire()
            P.wait("sync", evfree)
            i = nc.sync.dma_start(
                out=self.s_q[:, slot, 0:b["W"]],
                in_=qsrc[:, b["h"], b["u"] * 512 + b["i"] * b["W"]:
                         b["u"] * 512 + (b["i"] + 1) * b["W"]])
            P.inc(i, "lq%d" % slot, 16)
            return slot

        if layer == "a":
            self._span_loads("a", 0, 0, guard=self._vt_guard[0])
            for q in range(20):
                self._emit_transpose_quad(0, q)
            head_buf = {h: 0 for h in range(NH)}
        else:
            self._span_loads("b", 0, 0, guard=self._vt_guard[0])
            for q in range(20):
                self._emit_transpose_quad(0, q)
            head_buf = {h: h % 2 for h in range(NH)}

        pend_tq = []
        for h in range(NH):
            buf = head_buf[h]
            pieces = by_head[h]
            if layer == "b" and h + 1 < NH:
                self._span_loads("b", h + 1, (h + 1) % 2,
                                 guard=self._vt_guard[(h + 1) % 2])
                pend_tq = [((h + 1) % 2, q) for q in range(20)]
            nq = max(1, (len(pend_tq) + len(pieces) - 1) // max(1, len(pieces)))
            qslots = {}
            for idx in range(min(2, len(pieces))):
                qslots[idx] = load_q(pieces[idx])
            for idx, b in enumerate(pieces):
                qslot = qslots.pop(idx)
                if idx + 2 < len(pieces):
                    qslots[idx + 2] = load_q(pieces[idx + 2])
                self._flash_piece(layer, b, buf, qslot, qrot, ptrot, zrot, rbrot,
                                  aorot, btab, aod)
                for _ in range(nq):
                    if pend_tq:
                        tbuf, tq = pend_tq.pop(0)
                        self._emit_transpose_quad(tbuf, tq)
            while pend_tq:
                tbuf, tq = pend_tq.pop(0)
                self._emit_transpose_quad(tbuf, tq)
            self._vt_guard[buf] = P.ev("mm")

    def _flash_piece(self, layer, b, buf, qslot, qrot, ptrot, zrot, rbrot, aorot,
                     btab, aod):
        nc, P = self.nc, self.P
        W = b["W"]
        n_all = b["npos"] + b["ndiag"]
        wdiv = W // 128
        sO, evO = self.rB.acquire()
        P.wait("tensor", [evO, self._psZ_free, P.ev("ka%d" % buf),
                          P.ev("lq%d" % qslot)])
        state = []
        for ci in range(n_all):
            if ci < b["npos"]:
                pos = ci
                bias_c = b["bias0"] + ci
                diag = None
            else:
                dd = ci - b["npos"]
                pos = 64 + 4 * b["u"] + b["i"] * wdiv + dd
                bias_c = b["bias0"] + b["npos"] + dd
                diag = dd
            sA, evA = self.rA.acquire()
            P.wait("tensor", evA)
            i = nc.tensor.matmul(self.psA[:, sA, 0:W], self.s_ka[:, buf, pos, :],
                                 self.s_q[:, qslot, 0:W], start=True, stop=True)
            ev_qk = P.inc(i, "mm")
            pt, evp = ptrot.acquire()
            P.wait("scalar", [ev_qk, evp, self._bias_ev])
            i = nc.scalar.activation(self.s_pt[:, pt, 0:W], self.psA[:, sA, 0:W],
                                     AF.Exp, bias=btab[:, bias_c:bias_c + 1],
                                     scale=FP(1.0))
            ev_exp = P.inc(i, "ac")
            self.rA.release(sA, ev_exp)
            ev_ready = ev_exp
            if diag is not None:
                mo = MASK_OFF[(W, diag)]
                P.wait("vector", ev_exp)
                i = nc.vector.tensor_mul(self.s_pt[:, pt, 0:W], self.s_pt[:, pt, 0:W],
                                         self.c_masks[:, mo:mo + W])
                ev_ready = P.inc(i, "dv")
            state.append((ci, pt, ev_ready))
            if len(state) > 2:
                self._flash_zav(state.pop(0), b, buf, sO, W, n_all, ptrot)
        while state:
            self._flash_zav(state.pop(0), b, buf, sO, W, n_all, ptrot)
        ev_mm = P.ev("mm")
        zs, evz = zrot.acquire()
        P.wait("vector", [ev_mm, evz])
        with nc.allow_low_precision(reason="softmax normalizer to bf16 for broadcast"):
            i = nc.vector.reciprocal(self.s_zr[0:1, zs, 0:W], self.psZ[0:1, 0, 0:W])
        ev_zr = P.inc(i, "dv")
        self._psZ_free = ev_zr
        sA, evA = self.rA.acquire()
        P.wait("tensor", [ev_zr, evA])
        i = nc.tensor.matmul(self.psA[:, sA, 0:W], self.c_ones[0:1, :],
                             self.s_zr[0:1, zs, 0:W], start=True, stop=True)
        ev_bc = P.inc(i, "mm")
        zrot.release(zs, ev_bc)
        rb, evrb = rbrot.acquire()
        P.wait("vector", [ev_bc, evrb])
        i = nc.vector.tensor_copy(self.s_rbb[:, rb, 0:W], self.psA[:, sA, 0:W])
        ev_rbc = P.inc(i, "dv")
        self.rA.release(sA, ev_rbc)
        ao, evao = aorot.acquire()
        P.wait("vector", evao)
        i = nc.vector.tensor_mul(self.s_aos[:, ao, 0:W], self.psB[:, sO, 0:W],
                                 self.s_rbb[:, rb, 0:W])
        ev_n = P.inc(i, "dv")
        rbrot.release(rb, ev_n)
        self.rB.release(sO, ev_n)
        qrot.release(qslot, P.ev("mm"))
        P.wait("scalar", ev_n)
        c0 = b["u"] * 512 + b["i"] * W
        i = nc.scalar.dma_start(out=aod[:, b["h"], c0:c0 + W],
                                in_=self.s_aos[:, ao, 0:W])
        ev_st = P.inc(i, "ls", 16)
        aorot.release(ao, ev_st)

    def _flash_zav(self, st, b, buf, sO, W, n_all, ptrot):
        nc, P = self.nc, self.P
        ci, pt, ev_ready = st
        if ci < b["npos"]:
            pos = ci
        else:
            pos = 64 + 4 * b["u"] + b["i"] * (W // 128) + (ci - b["npos"])
        P.wait("tensor", ev_ready)
        nc.tensor.matmul(self.psZ[0:1, 0, 0:W], self.c_ones[:, 0:1],
                         self.s_pt[:, pt, 0:W], start=(ci == 0), stop=(ci == n_all - 1))
        i = nc.tensor.matmul(self.psB[:, sO, 0:W], self.s_vt[:, buf, pos, :],
                             self.s_pt[:, pt, 0:W], start=(ci == 0),
                             stop=(ci == n_all - 1))
        ev_av = P.inc(i, "mm")
        ptrot.release(pt, ev_av)

    def _ln_stats_begin(self, pool):
        P = self.P
        self._sq_free = [[P.ev("mm"), P.ev("dv")]] * 4
        if pool == "B":
            sl, ev = self.rB.acquire()
            ap = self.psB[0:1, sl, :]
        else:
            sl, ev = self.rA.acquire()
            ap = self.psA[0:1, sl, :]
        P.wait("tensor", [ev, self._psZ_free])
        return (pool, sl, ap)

    def _ln_stats_finish(self, hnd):
        nc, P = self.nc, self.P
        pool, sl, ap = hnd
        ev_mm = P.ev("mm")
        P.wait("vector", ev_mm)
        nc.vector.tensor_copy(self.s_st[:, 0, :], self.psZ[0:1, 0, :])
        i = nc.vector.tensor_copy(self.s_st[:, 1, :], ap)
        ev_stats = P.inc(i, "dv")
        (self.rB if pool == "B" else self.rA).release(sl, ev_stats)
        self._psZ_free = ev_stats
        return ev_stats

    def _ln_stat_chunk2(self, c, hnd, f32_ap, ev_ready):
        nc, P = self.nc, self.P
        pool, sl, ap = hnd
        P.wait("vector", ev_ready)
        i = nc.vector.tensor_copy(self.s_xb[:, c, :], f32_ap)
        ev_cast = P.inc(i, "dv")
        qslot = c % 4
        P.wait("scalar", [ev_ready, self._sq_free[qslot]])
        i = nc.scalar.activation(self.s_t1[:, qslot, :], f32_ap, AF.Square)
        ev_sq = P.inc(i, "ac")
        P.wait("tensor", ev_cast)
        nc.tensor.matmul(self.psZ[0:1, 0, :], self.c_ones[:, 0:1], self.s_xb[:, c, :],
                         start=(c == 0), stop=(c == 15))
        P.wait("tensor", ev_sq)
        i = nc.tensor.matmul(ap, self.c_ones[:, 0:1],
                             self.s_t1[:, qslot, :], start=(c == 0), stop=(c == 15))
        ev_mm = P.inc(i, "mm")
        self._sq_free[qslot] = ev_mm
        return [ev_cast, ev_sq]

    def phase_mid(self, layer):
        nc, P = self.nc, self.P
        for t in range(NT):
            ts = slice(t * 512, (t + 1) * 512)
            self._mid_tile(layer, t, ts)

    def _mid_tile(self, layer, t, ts):
        nc, P = self.nc, self.P
        ao_d = self.aoa_d if layer == 1 else self.aob_d
        res_src = self.d_in["x"] if layer == 1 else self.x2_d
        wname = "wfc" if layer == 1 else "wo"
        P.wait("sync", self._ao_free)
        P.wait("sync", P.ev("ls"))
        i = nc.sync.dma_start(out=self.s_ao[:, :, :], in_=ao_d[:, :, ts])
        P.inc(i, "lx", 16)
        ev_ao = P.ev("lx")
        hnd = self._ln_stats_begin("A")
        xin_rot = Rot(2)

        def evac(idx, m, ps, ev_mm, ts=ts, layer=layer):
            xs, evx = xin_rot.acquire()
            P.wait("sync", evx)
            P.wait("sync", P.ev("ls"))
            i = nc.sync.dma_start(out=self.s_xc[:, xs, :], in_=res_src[:, m, ts])
            P.inc(i, "lx", 16)
            P.wait("vector", [ev_mm, self._xout_free])
            P.wait("vector", P.ev("lx"))
            if layer == 1:
                i = nc.vector.scalar_tensor_tensor(
                    self.s_xc[:, 3, :], ps, self.c_bfc[:, m:m + 1],
                    self.s_xc[:, xs, :], ALU.add, ALU.add)
            else:
                i = nc.vector.scalar_tensor_tensor(
                    self.s_xc[:, 3, :], ps, 1.0,
                    self.s_xc[:, xs, :], ALU.mult, ALU.add)
            ev_x1 = P.inc(i, "dv")
            xin_rot.release(xs, ev_x1)
            rel = self._ln_stat_chunk2(m, hnd, self.s_xc[:, 3, :], ev_x1)
            P.wait("scalar", ev_x1)
            i = nc.scalar.dma_start(out=self.xr_d[:, m, ts], in_=self.s_xc[:, 3, :])
            ev_sp = P.inc(i, "ls", 16)
            self._xout_free = [rel[0], rel[1], ev_sp]
            return ev_x1

        self.proj(wname, lambda k: self.s_ao[:, k, :], list(range(16)), evac,
                  extra_rhs_ev=ev_ao)
        ev_stats = self._ln_stats_finish(hnd)
        ev_y = self.ln_apply(ev_stats, self._y_free)
        self._ffn_tile(t, ts, ev_y, layer)
        if layer == 1:
            ev_stats2 = self._ln_stats_finish(self._hnd2)
            ev_y2 = self.ln_apply(ev_stats2, [P.ev("mm"), P.ev("dv")])
            for wn, kind in (("wq", "q"), ("wk", "k"), ("wv", "v")):
                evr = Rot(3)

                def evac3(idx, m, ps, ev_mm, wn=wn, kind=kind, ts=ts, evr=evr):
                    slot, evfree = evr.acquire()
                    P.wait("scalar", [ev_mm, evfree])
                    sc = INV_SQRT_HD if kind == "q" else FP(1.0)
                    i = nc.scalar.activation(self.s_ev[:, slot, :], ps, AF.Identity,
                                             scale=sc)
                    ev_a = P.inc(i, "ac")
                    P.wait("scalar", ev_a)
                    if kind == "q":
                        dst = self.qb_d[:, m, ts]
                    elif kind == "k":
                        dst = self.ag2_in[m * 128:(m + 1) * 128, ts]
                    else:
                        dst = self.ag2_in[2048 + m * 128:2048 + (m + 1) * 128, ts]
                    i = nc.scalar.dma_start(out=dst, in_=self.s_ev[:, slot, :])
                    ev_st = P.inc(i, "ls", 16)
                    evr.release(slot, ev_st)
                    return ev_a

                P.wait("tensor", ev_y2)
                self.proj(wn, lambda k: self.s_y[:, k, :], list(range(16)), evac3)
            self._y_free = P.ev("mm")
        else:
            self._y_free = P.ev("mm")

    def _ffn_tile(self, t, ts, ev_y, layer):
        nc, P = self.nc, self.P
        P.wait("tensor", ev_y)
        wrot = self.wrot
        t1rot = Rot(4)
        t1rot.free_ev = [[P.ev("mm"), P.ev("dv")]] * 4

        mid_guard = self._mid_free
        pend = {}
        wload4 = self.wload

        for bi in range(2):
            for wi, n in enumerate(("sw1", "sw2")):
                slot, evf = wrot.acquire()
                pend[(bi, wi)] = (slot, wload4(n, bi, slot, evf))
        nxt = 2
        for bi in range(32):
            s1, ev1 = pend.pop((bi, 0))
            s2, ev2 = pend.pop((bi, 1))
            wv1 = self.s_w[:, s1, :].rearrange("p (k m) -> p k m", m=256)
            wv2 = self.s_w[:, s2, :].rearrange("p (k m) -> p k m", m=256)
            for c2 in range(2):
                hc = bi * 2 + c2
                sg, evg = self.rA.acquire()
                su, evu = self.rA.acquire()
                P.wait("tensor", [ev1, ev2, evg, evu])
                for k in range(16):
                    nc.tensor.matmul(self.psA[:, sg, :], wv1[:, k, 128 * c2:128 * c2 + 128],
                                     self.s_y[:, k, :], start=(k == 0), stop=(k == 15))
                last = None
                for k in range(16):
                    last = nc.tensor.matmul(self.psA[:, su, :], wv2[:, k, 128 * c2:128 * c2 + 128],
                                            self.s_y[:, k, :], start=(k == 0), stop=(k == 15))
                ev_u = P.inc(last, "mm")
                tslot, evt = t1rot.acquire()
                P.wait("scalar", [ev_u, evt])
                i = nc.scalar.activation(self.s_t1[:, tslot, :], self.psA[:, sg, :], AF.Silu)
                ev_si = P.inc(i, "ac")
                P.wait("vector", [ev_si, mid_guard])
                if hc == 0:
                    nc.vector.tensor_copy(self.s_dbg2[:, 0, :], self.psA[:, sg, :])
                    nc.vector.tensor_copy(self.s_dbg2[:, 1, :], self.psA[:, su, :])
                    nc.vector.tensor_copy(self.s_dbg2[:, 2, :], self.s_t1[:, tslot, :])
                    nc.vector.tensor_copy(self.s_dbg2[:, 3, :], self.s_y[:, 0, :])
                self.rA.release(sg, ev_si)
                i = nc.vector.tensor_mul(self.s_mid[:, hc, :], self.s_t1[:, tslot, :],
                                         self.psA[:, su, :])
                ev_m = P.inc(i, "dv")
                self.rA.release(su, ev_m)
                t1rot.release(tslot, ev_m)
            ev_wdone = P.ev("mm")
            wrot.release(s1, ev_wdone)
            wrot.release(s2, ev_wdone)
            if nxt < 32:
                for wi, n in enumerate(("sw1", "sw2")):
                    slot, evf = wrot.acquire()
                    pend[(nxt, wi)] = (slot, wload4(n, nxt, slot, evf))
                nxt += 1

        self._router_gating(ev_y)

        ev_mid_a = P.ev("dv")
        pend = {}
        for ii in range(2):
            slot, evf = wrot.acquire()
            pend[ii] = (slot, wload4("sw3", ii, slot, evf))
        nxt = 2
        for m in range(16):
            outs = []
            for half in range(2):
                ii = m * 2 + half
                slot, ev_w = pend.pop(ii)
                wv = self.s_w[:, slot, :].rearrange("p (k m) -> p k m", m=128)
                if half == 0:
                    sB, evB = self.rB.acquire()
                    P.wait("tensor", [evB, ev_mid_a])
                P.wait("tensor", ev_w)
                last = None
                for kc in range(32):
                    last = nc.tensor.matmul(self.psB[:, sB, :], wv[:, kc, :],
                                            self.s_mid[:, half * 32 + kc, :],
                                            start=(half == 0 and kc == 0),
                                            stop=(half == 1 and kc == 31))
                ev_mm = P.inc(last, "mm")
                wrot.release(slot, ev_mm)
                if nxt < 32:
                    slot2, evf2 = wrot.acquire()
                    pend[nxt] = (slot2, wload4("sw3", nxt, slot2, evf2))
                    nxt += 1
            P.wait("vector", [ev_mm, self._oacc_free])
            i = nc.vector.tensor_copy(self.s_oacc[:, m, :], self.psB[:, sB, :])
            ev_oc = P.inc(i, "dv")
            self.rB.release(sB, ev_oc)

        ev_mid_rd = P.ev("mm")
        pend = {}
        order = []
        for e in range(8):
            for bp in range(4):
                order.append((e, bp))
        for ii in range(2):
            e, bp = order[ii]
            sl1, evf1 = wrot.acquire()
            sl2, evf2 = wrot.acquire()
            pend[ii] = ((sl1, wload4("w12", e * 8 + bp, sl1, evf1)),
                        (sl2, wload4("w12", e * 8 + 4 + bp, sl2, evf2)))
        nxt = 2
        for ii, (e, bp) in enumerate(order):
            (sl1, ev1), (sl2, ev2) = pend.pop(ii)
            wv1 = self.s_w[:, sl1, :].rearrange("p (k m) -> p k m", m=256)
            wv2 = self.s_w[:, sl2, :].rearrange("p (k m) -> p k m", m=256)
            for c2 in range(2):
                hc = 2 * bp + c2
                s1a, ev1a = self.rA.acquire()
                s2a, ev2a = self.rA.acquire()
                P.wait("tensor", [ev1, ev2, ev1a, ev2a])
                for k in range(16):
                    nc.tensor.matmul(self.psA[:, s1a, :], wv1[:, k, 128 * c2:128 * c2 + 128],
                                     self.s_y[:, k, :], start=(k == 0), stop=(k == 15))
                last = None
                for k in range(16):
                    last = nc.tensor.matmul(self.psA[:, s2a, :], wv2[:, k, 128 * c2:128 * c2 + 128],
                                            self.s_y[:, k, :], start=(k == 0), stop=(k == 15))
                ev_u = P.inc(last, "mm")
                tslot, evt = t1rot.acquire()
                P.wait("scalar", [ev_u, evt])
                i = nc.scalar.activation(self.s_t1[:, tslot, :], self.psA[:, s1a, :], AF.Silu)
                ev_si = P.inc(i, "ac")
                self.rA.release(s1a, ev_si)
                t2slot, evt2 = t1rot.acquire()
                P.wait("vector", [ev_si, evt2, ev_mid_rd, self._gb_ready])
                i = nc.vector.tensor_mul(self.s_t1[:, t2slot, :], self.s_t1[:, tslot, :],
                                         self.psA[:, s2a, :])
                i = nc.vector.tensor_mul(self.s_mid[:, e * 8 + hc, :],
                                         self.s_t1[:, t2slot, :], self.s_gb[:, e, :])
                ev_m = P.inc(i, "dv")
                self.rA.release(s2a, ev_m)
                t1rot.release(tslot, ev_m)
                t1rot.release(t2slot, ev_m)
            ev_wdone = P.ev("mm")
            wrot.release(sl1, ev_wdone)
            wrot.release(sl2, ev_wdone)
            if nxt < 32:
                e2, bp2 = order[nxt]
                sl1, evf1 = wrot.acquire()
                sl2, evf2 = wrot.acquire()
                pend[nxt] = ((sl1, wload4("w12", e2 * 8 + bp2, sl1, evf1)),
                             (sl2, wload4("w12", e2 * 8 + 4 + bp2, sl2, evf2)))
                nxt += 1

        if layer == 1:
            self._hnd2 = self._ln_stats_begin("A")
        ev_mid_d = P.ev("dv")
        pend = {}
        for ii in range(2):
            slot, evf = wrot.acquire()
            pend[ii] = (slot, wload4("w3", ii, slot, evf))
        nxt = 2
        xin_rot = Rot(2)
        for m in range(16):
            for half in range(2):
                ii = m * 2 + half
                slot, ev_w = pend.pop(ii)
                wv = self.s_w[:, slot, :].rearrange("p (k m) -> p k m", m=128)
                if half == 0:
                    sB, evB = self.rB.acquire()
                    P.wait("tensor", [evB, ev_mid_d])
                P.wait("tensor", ev_w)
                last = None
                for idx2 in range(32):
                    e4, kc = idx2 // 8, idx2 % 8
                    e = half * 4 + e4
                    last = nc.tensor.matmul(self.psB[:, sB, :], wv[:, idx2, :],
                                            self.s_mid[:, e * 8 + kc, :],
                                            start=(half == 0 and idx2 == 0),
                                            stop=(half == 1 and idx2 == 31))
                ev_mm = P.inc(last, "mm")
                wrot.release(slot, ev_mm)
                if nxt < 32:
                    slot2, evf2 = wrot.acquire()
                    pend[nxt] = (slot2, wload4("w3", nxt, slot2, evf2))
                    nxt += 1
            xs, evx = xin_rot.acquire()
            P.wait("sync", evx)
            P.wait("sync", P.ev("ls"))
            i = nc.sync.dma_start(out=self.s_xc[:, xs, :], in_=self.xr_d[:, m, ts])
            P.inc(i, "lx", 16)
            P.wait("vector", [ev_mm, P.ev("lx"), self._xout_free])
            i = nc.vector.scalar_tensor_tensor(
                self.s_xc[:, 2, :], self.psB[:, sB, :], 1.0,
                self.s_oacc[:, m, :], ALU.mult, ALU.add)
            i = nc.vector.tensor_add(self.s_xc[:, 3, :], self.s_xc[:, 2, :],
                                     self.s_xc[:, xs, :])
            ev_x2 = P.inc(i, "dv")
            self.rB.release(sB, ev_x2)
            xin_rot.release(xs, ev_x2)
            relx = [ev_x2]
            if layer == 1:
                relx = self._ln_stat_chunk2(m, self._hnd2, self.s_xc[:, 3, :], ev_x2)
            P.wait("scalar", ev_x2)
            dst = self.x2_d[:, m, ts] if layer == 1 else self.out[:, m, ts]
            i = nc.scalar.dma_start(out=dst, in_=self.s_xc[:, 3, :])
            ev_sp = P.inc(i, "ls", 16)
            self._xout_free = relx + [ev_sp]
        self._oacc_free = [P.ev("mm"), P.ev("dv")]
        self._mid_free = P.ev("mm")
        self._ao_free = [P.ev("mm"), P.ev("dv")]

    def _router_gating(self, ev_y):
        nc, P = self.nc, self.P
        P.wait("tensor", [self._psZ_free, self.ev_router, ev_y])
        last = None
        for k in range(16):
            last = nc.tensor.matmul(self.psZ[0:8, 0, :], self.c_router[:, k * 8:(k + 1) * 8],
                                    self.s_y[:, k, :], start=(k == 0), stop=(k == 15))
        ev_mm = P.inc(last, "mm")
        P.wait("scalar", ev_mm)
        i = nc.scalar.activation(self.s_gr[:, 1, :], self.psZ[0:8, 0, :], AF.Tanh,
                                 scale=FP(0.5))
        ev_t = P.inc(i, "ac")
        P.wait("vector", ev_t)
        i = nc.vector.tensor_scalar(self.s_gr[:, 0, :], self.s_gr[:, 1, :], 0.5, 0.5,
                                    ALU.mult, ALU.add)
        ev_sc = P.inc(i, "dv")
        self._psZ_free = ev_sc
        i = nc.vector.tensor_scalar_add(self.s_gr[:, 1, :], self.s_gr[:, 0, :],
                                        self.c_ebias[:, 0:1])
        nc.vector.tensor_copy(self.s_gr16[:, 0, :], self.s_gr[:, 0, :])
        i = nc.vector.tensor_copy(self.s_gr16[:, 1, :], self.s_gr[:, 1, :])
        ev_16 = P.inc(i, "dv")
        P.wait("tensor", [ev_16, self._psT_free])
        with nc.allow_low_precision(reason="router transpose staging"):
            for d in range(4):
                nc.tensor.transpose(self.psT[:, d, 0:8], self.s_gr16[0:8, 1, 128 * d:128 * (d + 1)],
                                    self.c_ident[0:8, 0:8])
            last = None
            for d in range(4):
                last = nc.tensor.transpose(self.psT[:, 4 + d, 0:8],
                                           self.s_gr16[0:8, 0, 128 * d:128 * (d + 1)],
                                           self.c_ident[0:8, 0:8])
        ev_tr = P.inc(last, "mm")
        P.wait("vector", ev_tr)
        i = nc.vector.tensor_copy(self.s_gtm16[:, 0:8, :], self.psT[:, 0:8, 0:8])
        ev_tm = P.inc(i, "dv")
        self._psT_free = ev_tm
        nc.vector.tensor_copy(self.s_dbg[:, 0:8, :], self.s_gtm16[:, 0:8, :])
        g = self.s_gtm

        def _sync(i):
            ev = P.inc(i, "dv")
            P.wait("vector", ev)
            return ev

        for d in range(4):
            sel = self.s_gtm16[:, d, :]
            sco = self.s_gtm16[:, 4 + d, :]
            _sync(nc.vector.reduce_max(g[:, 0, 0:1], sel, AX.X))
            _sync(nc.vector.tensor_scalar(g[:, 1, :], sel, g[:, 0, 0:1], None, ALU.is_ge))
            _sync(nc.vector.scalar_tensor_tensor(g[:, 2, :], g[:, 1, :], -1e30, sel,
                                                 ALU.mult, ALU.add))
            _sync(nc.vector.reduce_max(g[:, 3, 0:1], g[:, 2, :], AX.X))
            _sync(nc.vector.tensor_scalar(g[:, 4, :], sel, g[:, 3, 0:1], None, ALU.is_ge))
            _sync(nc.vector.tensor_mul(g[:, 5, :], g[:, 4, :], sco))
            _sync(nc.vector.reduce_sum(g[:, 6, 0:1], g[:, 5, :], AX.X))
            _sync(nc.vector.tensor_scalar_add(g[:, 6, 0:1], g[:, 6, 0:1], 1e-9))
            _sync(nc.vector.reciprocal(g[:, 7, 0:1], g[:, 6, 0:1]))
            ev_g = _sync(nc.vector.tensor_scalar(self.s_gtm16[:, d, :], g[:, 5, :],
                                                 g[:, 7, 0:1], None, ALU.mult))
            if d == 0:
                _sync(nc.vector.tensor_copy(self.s_dbg[:, 8:16, :], g[:, :, :]))
        nc.vector.tensor_copy(self.s_dbg[:, 8:12, :], self.s_gtm16[:, 0:4, :])
        P.wait("tensor", [ev_g, self._psT_free])
        with nc.allow_low_precision(reason="router transpose staging"):
            last = None
            for d in range(4):
                last = nc.tensor.transpose(self.psT[0:8, d, :],
                                           self.s_gtm16[:, d, :],
                                           self.c_ident[:, :])
        ev_tr2 = P.inc(last, "mm")
        P.wait("vector", ev_tr2)
        last = None
        for d in range(4):
            last = nc.vector.tensor_copy(self.s_gf[0:8, 128 * d:128 * (d + 1)],
                                         self.psT[0:8, d, 0:128])
        ev_gf = P.inc(last, "dv")
        self._psT_free = ev_gf
        for e in range(8):
            sA, evA = self.rA.acquire()
            P.wait("tensor", [ev_gf, evA])
            i = nc.tensor.matmul(self.psA[:, sA, :], self.c_onehot[0:8, e * 128:(e + 1) * 128],
                                 self.s_gf[0:8, :], start=True, stop=True)
            ev_b = P.inc(i, "mm")
            P.wait("vector", [ev_b, self._gb_free])
            i = nc.vector.tensor_copy(self.s_gb[:, e, :], self.psA[:, sA, :])
            ev_c = P.inc(i, "dv")
            self.rA.release(sA, ev_c)
        self._gb_ready = P.ev("dv")
        self._gb_free = None

    def allgather(self, which):
        nc, P = self.nc, self.P
        groups = [[0, 1, 2, 3], [4, 5, 6, 7]]
        ain, aout = (self.ag1_in, self.ag1_out) if which == 1 else (self.ag2_in, self.ag2_out)
        P.wait("gpsimd", P.ev("ls"))
        i = nc.gpsimd.collective_compute(
            "AllGather", ALU.bypass, replica_groups=groups,
            ins=[ain.opt()], outs=[aout.opt()])
        return P.inc(i, "ag", 1)

    def tap(self, name, src, shape, dt):
        self.taps.append((name, src, shape, dt))

    def finish(self):
        nc, P = self.nc, self.P
        for sname in list(P.sems):
            if sname != "z":
                P.wait("gpsimd", P.ev(sname))
        for name, src, shape, dt in self.taps:
            o = nc.dram_tensor("tap_" + name, shape, dt, kind="ExternalOutput").ap()
            i = nc.gpsimd.dma_start(out=o, in_=src)
            P.inc(i, "lg", 16)
        P.wait("gpsimd", P.ev("lg"))
        for sh in P.sems.values():
            nc.gpsimd.sem_clear(sh)
        self.ctx.close()
        return self.nc

    def build(self):
        self._materialize_sbuf()
        self.load_consts()
        self.ev_conv = {}
        self.convert_weights(["wqkv", "router"])
        nc, P = self.nc, self.P
        P.wait("sync", self.ev_conv["router"])
        i = nc.sync.dma_start(out=self.c_router[:], in_=self.wb["router"][0])
        P.inc(i, "lx", 16)
        self.ev_router = P.ev("lx")
        self.phase0()
        ev_ag1 = self.allgather(1)
        self.convert_weights(["wfc", "sw1", "sw2", "sw3", "w12", "w3",
                              "wq", "wk", "wv", "wo"])
        if self.upto == "p0":
            self.tap("qa", self.qa_d, [128, 16, 2048], BF16)
            self.tap("ag1", self.ag1_out, [1024, 2048], BF16)
            return self.finish()
        P = self.P
        for e in ("sync", "tensor", "scalar", "vector"):
            P.wait(e, ev_ag1)
        self.flash_layer("a")
        if self.upto == "p1a":
            self.tap("aoa", self.aoa_d, [128, 16, 2048], BF16)
            return self.finish()
        self.phase_mid(1)
        if self.upto == "p1b":
            self.tap("x2", self.x2_d, [128, 16, 2048], F32)
            self.tap("qb", self.qb_d, [128, 16, 2048], BF16)
            self.tap("ag2in", self.ag2_in, [4096, 2048], BF16)
            self.tap("xr", self.xr_d, [128, 16, 2048], F32)
            self.tap("mid", self.s_mid[:, :, :], [128, 64, 512], BF16)
            self.tap("oacc", self.s_oacc[:, :, :], [128, 16, 512], BF16)
            self.tap("gb", self.s_gb[:, :, :], [128, 8, 512], BF16)
            self.tap("ylast", self.s_y[:, :, :], [128, 16, 512], BF16)
            self.tap("gf", self.s_gf[:, :], [8, 512], BF16)
            self.tap("gtm16", self.s_gtm16[:, :, :], [128, 8, 8], BF16)
            self.tap("gr", self.s_gr[:, :, :], [8, 2, 512], F32)
            self.tap("gr16", self.s_gr16[:, :, :], [8, 2, 512], BF16)
            self.tap("dbg", self.s_dbg[:, :, :], [128, 16, 8], BF16)
            self.tap("dbg2", self.s_dbg2[:, :, :], [128, 4, 512], BF16)
            self.tap("wsw1b0", self.wb["sw1"][0:1], [1, 128, 4096], BF16)
            self.tap("wsw3b0", self.wb["sw3"][0:1], [1, 128, 4096], BF16)
            return self.finish()
        ev_ag2 = self.allgather(2)
        for e in ("sync", "tensor", "scalar", "vector"):
            P.wait(e, ev_ag2)
        self.flash_layer("b")
        if self.upto == "p2a":
            self.tap("aob", self.aob_d, [128, 16, 2048], BF16)
            return self.finish()
        self.phase_mid(2)
        return self.finish()


_CACHE = {}

def _get_nc(upto):
    if upto not in _CACHE:
        _CACHE[upto] = Builder(upto=upto).build()
    return _CACHE[upto]


def kernel(**inputs):
    upto = os.environ.get("ATHENA_UPTO", "full")
    nc = _get_nc(upto)
    in_maps = make_in_maps(inputs)
    res = run_bass_kernel_spmd(nc, in_maps, core_ids=list(range(8)))
    return reassemble(res.results)


# version 03
# speedup vs baseline: 1.1912x; 1.0704x over previous
import os
from collections import defaultdict
from contextlib import ExitStack

import numpy as np
import ml_dtypes

import concourse.bass as bass
import concourse.mybir as mybir
from concourse.bass_utils import run_bass_kernel_spmd

F32 = mybir.dt.float32
BF16 = mybir.dt.bfloat16
AF = mybir.ActivationFunctionType
ALU = mybir.AluOpType
AX = mybir.AxisListType
BF = ml_dtypes.bfloat16

NCH = 16
NT = 4
NH = 16
KCE = [16, 32, 48, 64]
SLOPES = [0.5 ** ((i + 1) * 0.5) for i in range(NH)]
WH = [128, 256, 256] + [512] * 13
NEG = -1.0e9
INV_SQRT_HD = np.float32(1.0 / np.sqrt(128.0))
FP = np.float32

def sc_of(j):
    return [j, 7 - j, 8 + j, 15 - j]

_SC_MAP = {}
for _j in range(4):
    for _u, _s in enumerate(sc_of(_j)):
        _SC_MAP[_s] = (_j, _u)

def flash_blocks(layer):
    col = 0
    out = []
    acol = {}
    for h in range(NH):
        ws = 512 if layer == "a" else WH[h]
        for u in range(4):
            for i in range(512 // ws):
                npos, ndiag = KCE[u], ws // 128
                if layer == "a":
                    if (u, i) not in acol:
                        acol[(u, i)] = col
                        col += npos + ndiag
                    b0 = acol[(u, i)]
                else:
                    b0 = col
                    col += npos + ndiag
                out.append(dict(h=h, u=u, i=i, W=ws, npos=npos, ndiag=ndiag, bias0=b0))
    return out, col

FB_A, BIASA_COLS = flash_blocks("a")
FB_B, BIASB_COLS = flash_blocks("b")

MASK_OFF = {}
_c = 0
for _wd in ((512, 0), (512, 1), (512, 2), (512, 3), (256, 0), (256, 1), (128, 0)):
    MASK_OFF[_wd] = _c
    _c += _wd[0]
MASK_COLS = _c

WSPEC = {
    "wqkv": (18, 2048), "wfc": (16, 2048), "wq": (16, 2048), "wk": (16, 2048),
    "wv": (16, 2048), "wo": (16, 2048), "sw1": (32, 4096), "sw2": (32, 4096),
    "sw3": (32, 4096), "w12": (64, 4096), "w3": (32, 4096), "router": (1, 128),
}


def _mk_masks():
    m = np.zeros((128, MASK_COLS), dtype=BF)
    for (w, d), off in MASK_OFF.items():
        k = np.arange(128)[:, None]
        q = np.arange(w)[None, :]
        m[:, off:off + w] = (128 * d + k <= q).astype(BF)
    return m

def _mk_bias(layer, j):
    fb, ncols = (FB_A, BIASA_COLS) if layer == "a" else (FB_B, BIASB_COLS)
    tab = np.zeros((128, ncols), dtype=np.float32)
    p = np.arange(128, dtype=np.float32)
    for b in fb:
        sc = sc_of(j)[b["u"]]
        q0 = sc * 512 + b["i"] * b["W"]
        slope = 0.0 if layer == "a" else SLOPES[b["h"]]
        R = q0 + b["W"] / 2.0
        for kc in range(b["npos"]):
            c = b["bias0"] + kc
            if (kc + 1) * 128 <= q0:
                tab[:, c] = slope * (128.0 * kc + p - R)
            else:
                tab[:, c] = NEG
        for dd in range(b["ndiag"]):
            c = b["bias0"] + b["npos"] + dd
            kc = q0 // 128 + dd
            tab[:, c] = slope * (128.0 * kc + p - R)
    return tab

def _wblocks(Wt, kcs, mblk):
    K, M = Wt.shape
    nb = M // mblk
    a = Wt.reshape(kcs, 128, nb, mblk).transpose(2, 1, 0, 3)
    return np.ascontiguousarray(a.reshape(nb, 128, kcs * mblk), dtype=np.float32)

def _host_weight_arrays(inp):
    w = {}
    w["wqkv"] = _wblocks(np.asarray(inp["L0_attn_a_w_qkv"], np.float32), 16, 128)
    w["wfc"] = _wblocks(np.asarray(inp["L0_attn_a_w_fc"], np.float32), 16, 128)
    for n, k in (("wq", "L0_attn_b_Wq"), ("wk", "L0_attn_b_Wk"),
                 ("wv", "L0_attn_b_Wv"), ("wo", "L0_attn_b_Wo")):
        w[n] = _wblocks(np.ascontiguousarray(np.asarray(inp[k], np.float32).T), 16, 128)
    w["sw1"] = _wblocks(np.ascontiguousarray(np.asarray(inp["L0_ffn_sw1"], np.float32).T), 16, 256)
    w["sw2"] = _wblocks(np.ascontiguousarray(np.asarray(inp["L0_ffn_sw2"], np.float32).T), 16, 256)
    s3 = np.ascontiguousarray(np.asarray(inp["L0_ffn_sw3"], np.float32).T)
    a = s3.reshape(64, 128, 16, 128).transpose(2, 0, 1, 3)
    a = a.reshape(16, 2, 32, 128, 128).transpose(0, 1, 3, 2, 4)
    w["sw3"] = np.ascontiguousarray(a.reshape(32, 128, 32 * 128), dtype=np.float32)
    w12t = np.asarray(inp["L0_ffn_w12"], np.float32).transpose(0, 2, 1)
    a = w12t.reshape(8, 16, 128, 8, 256).transpose(0, 3, 2, 1, 4)
    w["w12"] = np.ascontiguousarray(a.reshape(64, 128, 16 * 256), dtype=np.float32)
    w3t = np.asarray(inp["L0_ffn_w3"], np.float32).transpose(0, 2, 1)
    a = w3t.reshape(8, 8, 128, 16, 128).transpose(3, 0, 1, 2, 4)
    a = a.reshape(16, 2, 4, 8, 128, 128).transpose(0, 1, 4, 2, 3, 5)
    w["w3"] = np.ascontiguousarray(a.reshape(32, 128, 4 * 8 * 128), dtype=np.float32)
    rt = np.ascontiguousarray(np.asarray(inp["L0_ffn_router_w"], np.float32).T)
    a = rt.reshape(16, 128, 1, 8).transpose(2, 1, 0, 3)
    w["router"] = np.ascontiguousarray(a.reshape(1, 128, 16 * 8), dtype=np.float32)
    return w

def _host_x(x0, core):
    g, j = core // 4, core % 4
    toks = np.concatenate([s * 512 + np.arange(512) for s in sc_of(j)])
    xs = np.asarray(x0, np.float32)[g][toks]
    arr = xs.T.reshape(16, 128, 2048).transpose(1, 0, 2)
    return np.ascontiguousarray(arr, dtype=np.float32)

def make_in_maps(inputs):
    w = _host_weight_arrays(inputs)
    masks = _mk_masks().astype(np.float32)
    ident = np.eye(128, dtype=np.float32)
    ones = np.ones((128, 128), dtype=np.float32)
    onehot = np.zeros((8, 1024), dtype=np.float32)
    for e in range(8):
        onehot[e, e * 128:(e + 1) * 128] = 1
    bqkv = np.ascontiguousarray(np.asarray(inputs["L0_attn_a_b_qkv"], np.float32).reshape(18, 128).T)
    bfc = np.ascontiguousarray(np.asarray(inputs["L0_attn_a_b_fc"], np.float32).reshape(16, 128).T)
    ebias = np.ascontiguousarray(np.asarray(inputs["L0_ffn_expert_bias"], np.float32).reshape(8, 1))
    in_maps = []
    for core in range(8):
        j = core % 4
        m = {
            "x_in": _host_x(inputs["L0_attn_a_x"], core),
            "bqkv": bqkv, "bfc": bfc, "ebias": ebias,
            "ident": ident, "onesin": ones, "onehot": onehot, "masks": masks,
            "biasa": _mk_bias("a", j), "biasb": _mk_bias("b", j),
        }
        for n in WSPEC:
            m["wf_" + n] = w[n]
        in_maps.append(m)
    return in_maps

def reassemble(results):
    out = np.empty((2, 8192, 2048), dtype=np.float32)
    for core in range(8):
        g, j = core // 4, core % 4
        toks = np.concatenate([s * 512 + np.arange(512) for s in sc_of(j)])
        shard = results[core]["out"]
        xs = shard.transpose(1, 0, 2).reshape(2048, 2048)
        out[g, toks, :] = xs.T
    return out


class Prog:
    def __init__(self, nc, ctx):
        self.nc = nc
        self.sems = {}
        self.cnt = defaultdict(int)
        self.lastw = defaultdict(dict)
        names = ["z", "lw0", "lw1", "lw2", "lw3", "lx", "lq0", "lq1", "lq2", "lq3",
                 "ka0", "ka1", "ls", "lg", "mm", "ac", "dv", "ag"]
        names += ["cv_" + n for n in WSPEC]
        for s in names:
            self.sems[s] = ctx.enter_context(nc.semaphore(s))
        self.dma_sems = {n for n in names if n not in ("mm", "ac", "dv", "ag")}
        self._fence_a = nc.dram_tensor("fence_a", [1, 16], F32, kind="Internal").ap()
        self._fence_b = nc.dram_tensor("fence_b", [1, 16], F32, kind="Internal").ap()
        mynums = sorted(h.num for h in self.sems.values())
        myrange = range(mynums[0], mynums[-1] + 1)
        nc.gpsimd.dma_reset(myrange)
        for s in self.sems.values():
            nc.gpsimd.sem_clear(s)
        for _ in range(4):
            nc.gpsimd.nop()
        nc.gpsimd.dma_start(out=self._fence_b, in_=self._fence_a).then_inc(self.sems["z"], 16)
        self.cnt["z"] = 16
        for eng in ("sync", "tensor", "scalar", "vector"):
            e = getattr(nc, eng)
            for _ in range(24):
                e.nop()
            e.wait_ge(self.sems["z"], 16)
            self.lastw[eng]["z"] = 16
        nc.gpsimd.wait_ge(self.sems["z"], 16)
        self.lastw["gpsimd"]["z"] = 16

    def inc(self, inst, sem, n=1):
        inst.then_inc(self.sems[sem], n)
        self.cnt[sem] += n
        return (sem, self.cnt[sem])

    def ev(self, sem):
        return (sem, self.cnt[sem])

    def barrier(self):
        pass

    def wait(self, engname, ev):
        if ev is None:
            return
        if isinstance(ev, list):
            for e in ev:
                self.wait(engname, e)
            return
        sem, v = ev
        if sem in self.dma_sems:
            v = self.cnt[sem]
        if v <= 0 or self.lastw[engname].get(sem, 0) >= v:
            return
        getattr(self.nc, engname).wait_ge(self.sems[sem], v)
        self.lastw[engname][sem] = v


class Rot:
    def __init__(self, n):
        self.n = n
        self.free_ev = [None] * n
        self.i = 0

    def acquire(self):
        idx = self.i
        self.i = (self.i + 1) % self.n
        return idx, self.free_ev[idx]

    def release(self, idx, ev):
        self.free_ev[idx] = ev


def _ceil32(x):
    return (x + 31) // 32 * 32


SERIAL = int(os.environ.get("ATHENA_SERIAL", "0"))

class Builder:
    def __init__(self, upto="full"):
        self.upto = upto
        self.taps = []
        nc = bass.Bass("TRN2", target_bir_lowering=False, debug=False)
        self.nc = nc
        self.ctx = ExitStack()
        self.P = Prog(nc, self.ctx)
        self._declare_dram()
        self._plan_sbuf()
        self._psum()

    def _declare_dram(self):
        nc = self.nc
        self.d_in = {}
        self.d_in["x"] = nc.dram_tensor("x_in", [128, 16, 2048], F32, kind="ExternalInput").ap()
        for n, (nb, bl) in WSPEC.items():
            self.d_in["wf_" + n] = nc.dram_tensor("wf_" + n, [nb, 128, bl], F32, kind="ExternalInput").ap()
        for nm, shp, dt in (("bqkv", [128, 18], F32), ("bfc", [128, 16], F32),
                            ("ebias", [8, 1], F32), ("ident", [128, 128], F32),
                            ("onesin", [128, 128], F32), ("onehot", [8, 1024], F32),
                            ("masks", [128, MASK_COLS], F32),
                            ("biasa", [128, BIASA_COLS], F32),
                            ("biasb", [128, BIASB_COLS], F32)):
            self.d_in[nm] = nc.dram_tensor(nm, shp, dt, kind="ExternalInput").ap()
        self.out = nc.dram_tensor("out", [128, 16, 2048], F32, kind="ExternalOutput").ap()

        self.wb = {}
        for n, (nb, bl) in WSPEC.items():
            self.wb[n] = nc.dram_tensor("wb_" + n, [nb, 128, bl], BF16, kind="Internal").ap()
        self.qa_d = nc.dram_tensor("qa_d", [128, 16, 2048], BF16, kind="Internal").ap()
        self.qb_d = nc.dram_tensor("qb_d", [128, 16, 2048], BF16, kind="Internal").ap()
        self.aoa_d = nc.dram_tensor("aoa_d", [128, 16, 2048], BF16, kind="Internal").ap()
        self.aob_d = nc.dram_tensor("aob_d", [128, 16, 2048], BF16, kind="Internal").ap()
        self.xr_d = nc.dram_tensor("xr_d", [128, 16, 2048], F32, kind="Internal").ap()
        self.x2_d = nc.dram_tensor("x2_d", [128, 16, 2048], F32, kind="Internal").ap()
        self.ag1_in = nc.dram_tensor("ag1_in", [256, 2048], BF16, kind="Internal").ap()
        self.ag1_out = nc.dram_tensor("ag1_out", [1024, 2048], BF16, kind="Internal").ap()
        self.ag2_in = nc.dram_tensor("ag2_in", [4096, 2048], BF16, kind="Internal").ap()
        self.agk_out = nc.dram_tensor("agk_out", [4, 2048, 2048], BF16, kind="Internal").ap()
        self.agv_out = nc.dram_tensor("agv_out", [4, 2048, 2048], BF16, kind="Internal").ap()

    def _plan_sbuf(self):
        nc = self.nc
        self._sb_virt = []
        self._off = 0

        def A(name, shape, dtype):
            per = 1
            for s in shape[1:]:
                per *= s
            per *= mybir.dt.size(dtype)
            h = nc.virtual_sbuf_tensor(name, shape, dtype)
            self._sb_virt.append((h, self._off))
            self._off = _ceil32(self._off + per)
            return h

        self.c_ident = A("c_ident", [128, 128], BF16)
        self.c_identf = A("c_identf", [128, 128], F32)
        self.c_ones = A("c_ones", [128, 128], BF16)
        self.c_onehot = A("c_onehot", [8, 1024], BF16)
        self.c_masks = A("c_masks", [128, MASK_COLS], BF16)
        self.c_biasa = A("c_biasa", [128, BIASA_COLS], F32)
        self.c_bqkv = A("c_bqkv", [128, 18], F32)
        self.c_bqs = A("c_bqs", [128, 16], F32)
        self.c_bfc = A("c_bfc", [128, 16], F32)
        self.c_ebias = A("c_ebias", [8, 1], F32)
        self.c_router = A("c_router", [128, 128], BF16)
        self.s_xc = A("s_xc", [128, 4, 512], F32)
        self.s_st = A("s_st", [1, 4, 512], F32)
        self.s_st16 = A("s_st16", [1, 2, 512], BF16)
        self.s_ev = A("s_ev", [128, 6, 512], BF16)
        self.s_t1 = A("s_t1", [128, 6, 512], BF16)
        self.s_mb = A("s_mb", [128, 512], BF16)
        self.s_rb = A("s_rb", [128, 512], BF16)

        g1 = self._off
        self.s_y = A("s_y", [128, 16, 512], BF16)
        self.s_xb = A("s_xb", [128, 16, 512], BF16)
        self.s_mid = A("s_mid", [128, 64, 512], BF16)
        g1a = self._off
        self.s_ao = A("s_ao", [128, 16, 512], BF16)
        end_g1a = self._off
        self._off = g1a
        self.s_oacc = A("s_oacc", [128, 16, 512], BF16)
        self._off = max(self._off, end_g1a)
        self.s_w = A("s_w", [128, 4, 4096], BF16)
        self.s_gb = A("s_gb", [128, 8, 512], BF16)
        self.s_gr = A("s_gr", [8, 2, 512], F32)
        self.s_gr16 = A("s_gr16", [8, 2, 512], BF16)
        self.s_gf = A("s_gf", [8, 512], BF16)
        self.s_gtm = A("s_gtm", [128, 8, 8], F32)
        self.s_dbg = A("s_dbg", [128, 16, 8], BF16)
        self.s_dbg2 = A("s_dbg2", [128, 4, 512], BF16)
        self.s_gtm16 = A("s_gtm16", [128, 8, 8], BF16)
        end_g1 = self._off

        self._off = g1
        self.c_biasb = A("c_biasb", [128, BIASB_COLS], F32)
        self.s_ka = A("s_ka", [128, 2, 80, 128], BF16)
        self.s_va = A("s_va", [128, 2, 80, 128], BF16)
        self.s_vt = A("s_vt", [128, 2, 80, 128], BF16)
        self.s_q = A("s_q", [128, 4, 512], BF16)
        self.s_pt = A("s_pt", [128, 4, 512], BF16)
        self.s_zr = A("s_zr", [1, 4, 512], BF16)
        self.s_rbb = A("s_rbb", [128, 2, 512], BF16)
        self.s_aos = A("s_aos", [128, 4, 512], BF16)
        end_g2 = self._off

        self._off = max(end_g1, end_g2)
        assert self._off <= 207 * 1024, f"SBUF over budget: {self._off}"
        self.sbuf_total = self._off

    def _materialize_sbuf(self):
        nc = self.nc
        self.arena = self.ctx.enter_context(
            nc.sbuf_tensor("arena", [128, self._off], mybir.dt.uint8))
        base = nc.lookup_mloc(self.arena).addr
        for h, off in self._sb_virt:
            nc._tensor(h.name, list(h.shape), h.dtype, type="SB")
            mloc = nc.lookup_mloc(h)
            mloc.allocated = not nc.target_bir_lowering
            mloc.addr = base + off

    def _psum(self):
        nc = self.nc
        self.psA = self.ctx.enter_context(nc.psum_tensor("psA", [128, 4, 512], F32))
        self.psB = self.ctx.enter_context(nc.psum_tensor("psB", [128, 2, 512], F32))
        self.psZ = self.ctx.enter_context(nc.psum_tensor("psZ", [128, 1, 512], F32))
        self.psT = self.ctx.enter_context(nc.psum_tensor("psT", [128, 8, 128], BF16))
        self.rA = Rot(4)
        self.rB = Rot(2)
        self.rT = Rot(8)
        self._psZ_free = None
        self._psT_free = None
        self._vt_guard = [None, None]
        self._ao_free = None
        self._mid_free = None
        self._oacc_free = None
        self._gb_free = None
        self._gb_ready = None
        self._y_free = None
        self._xout_free = None
        self.wrot = Rot(4)

    def load_consts(self):
        nc, P = self.nc, self.P
        for sb, dr in ((self.c_biasa, "biasa"),
                       (self.c_bqkv, "bqkv"), (self.c_bfc, "bfc"),
                       (self.c_ebias, "ebias"), (self.c_identf, "ident")):
            P.inc(nc.sync.dma_start(out=sb[:], in_=self.d_in[dr]), "lx", 16)
        ev_f32 = P.ev("lx")
        for sb, dr in ((self.c_ident, "ident"), (self.c_ones, "onesin"),
                       (self.c_onehot, "onehot"), (self.c_masks, "masks")):
            P.inc(nc.gpsimd.dma_start(out=sb[:], in_=self.d_in[dr]), "lg", 16)
        self.ev_consts = [ev_f32, P.ev("lg")]
        P.wait("vector", self.ev_consts)
        i = nc.vector.tensor_scalar_mul(self.c_bqs[:, :], self.c_bqkv[:, 0:16], float(INV_SQRT_HD))
        self.ev_bqs = P.inc(i, "dv")

    def convert_weights(self, order, defer=False):
        nc, P = self.nc, self.P
        for n in order:
            nb, bl = WSPEC[n]
            step = max(1, min(nb, (8 * 1024 * 1024) // (128 * bl * 4)))
            ev = None
            for b0 in range(0, nb, step):
                b1 = min(nb, b0 + step)
                if defer:
                    self._pend_conv.append((n, b0, b1))
                    P.cnt["cv_" + n] += 16
                else:
                    i = nc.gpsimd.dma_start(out=self.wb[n][b0:b1],
                                            in_=self.d_in["wf_" + n][b0:b1])
                    ev = P.inc(i, "cv_" + n, 16)
            self.ev_conv[n] = ("cv_" + n, P.cnt["cv_" + n])

    def _emit_conv_chunks(self, k):
        nc, P = self.nc, self.P
        for _ in range(k):
            if not self._pend_conv:
                return
            n, b0, b1 = self._pend_conv.pop(0)
            P.lastw["gpsimd"].pop("mm", None)
            P.wait("gpsimd", ("mm", P.cnt["mm"]))
            i = nc.gpsimd.dma_start(out=self.wb[n][b0:b1], in_=self.d_in["wf_" + n][b0:b1])
            i.then_inc(P.sems["cv_" + n], 16)

    def wload(self, n, b, slot, prior_ev):
        nc, P = self.nc, self.P
        nb, bl = WSPEC[n]
        P.wait("sync", self.ev_conv[n])
        P.wait("sync", prior_ev)
        i = nc.sync.dma_start(out=self.s_w[:, slot, 0:bl], in_=self.wb[n][b])
        return P.inc(i, "lw%d" % slot, 16)

    def ln_stats_start(self):
        P = self.P
        sB, evB = self.rB.acquire()
        P.wait("tensor", evB)
        P.wait("tensor", self._psZ_free)
        return sB

    def ln_stat_chunk(self, c, sB, f32_ap, ev_ready):
        nc, P = self.nc, self.P
        P.wait("vector", ev_ready)
        i = nc.vector.tensor_copy(self.s_xb[:, c, :], f32_ap)
        ev_cast = P.inc(i, "dv")
        qslot = c % 4
        P.wait("scalar", ev_ready)
        P.wait("scalar", self._sq_free[qslot])
        i = nc.scalar.activation(self.s_t1[:, qslot, :], f32_ap, AF.Square)
        ev_sq = P.inc(i, "ac")
        P.wait("tensor", ev_cast)
        nc.tensor.matmul(self.psZ[0:1, 0, :], self.c_ones[:, 0:1], self.s_xb[:, c, :],
                         start=(c == 0), stop=(c == 15))
        P.wait("tensor", ev_sq)
        i = nc.tensor.matmul(self.psB[0:1, sB, :], self.c_ones[:, 0:1],
                             self.s_t1[:, qslot, :], start=(c == 0), stop=(c == 15))
        ev_mm = P.inc(i, "mm")
        self._sq_free[qslot] = ev_mm
        return [ev_cast, ev_sq]

    def ln_finish_stats(self, sB):
        nc, P = self.nc, self.P
        ev_mm = P.ev("mm")
        P.wait("vector", ev_mm)
        nc.vector.tensor_copy(self.s_st[:, 0, :], self.psZ[0:1, 0, :])
        i = nc.vector.tensor_copy(self.s_st[:, 1, :], self.psB[0:1, sB, :])
        ev_stats = P.inc(i, "dv")
        self.rB.release(sB, ev_stats)
        self._psZ_free = ev_stats
        return ev_stats

    def ln_apply(self, ev_stats, y_free_ev=None):
        nc, P = self.nc, self.P
        st, st16 = self.s_st, self.s_st16
        P.wait("vector", ev_stats)
        nc.vector.tensor_scalar_mul(st[:, 2, :], st[:, 0, :], 1.0 / 2048.0)
        nc.vector.tensor_scalar_mul(st[:, 3, :], st[:, 1, :], 1.0 / 2048.0)
        nc.vector.tensor_mul(st[:, 0, :], st[:, 2, :], st[:, 2, :])
        i = nc.vector.tensor_sub(st[:, 1, :], st[:, 3, :], st[:, 0, :])
        ev_var = P.inc(i, "dv")
        P.wait("scalar", ev_var)
        nc.scalar.activation(st[:, 0, :], st[:, 1, :], AF.Ln, bias=FP(1e-5))
        i = nc.scalar.activation(st16[:, 1, :], st[:, 0, :], AF.Exp, scale=FP(-0.5))
        ev_rstd = P.inc(i, "ac")
        P.wait("vector", ev_rstd)
        i = nc.vector.tensor_copy(st16[:, 0, :], st[:, 2, :])
        ev_m16 = P.inc(i, "dv")
        sA, evA = self.rA.acquire()
        sA2, evA2 = self.rA.acquire()
        P.wait("tensor", [ev_m16, ev_rstd, evA, evA2])
        nc.tensor.matmul(self.psA[:, sA, :], self.c_ones[0:1, :], st16[:, 0, :],
                         start=True, stop=True)
        i = nc.tensor.matmul(self.psA[:, sA2, :], self.c_ones[0:1, :], st16[:, 1, :],
                             start=True, stop=True)
        ev_bc = P.inc(i, "mm")
        P.wait("vector", ev_bc)
        nc.vector.tensor_copy(self.s_mb[:, :], self.psA[:, sA, :])
        i = nc.vector.tensor_copy(self.s_rb[:, :], self.psA[:, sA2, :])
        ev_bcs = P.inc(i, "dv")
        self.rA.release(sA, ev_bcs)
        self.rA.release(sA2, ev_bcs)
        P.wait("vector", y_free_ev)
        ev = None
        for c in range(16):
            sl = 4 + (c % 2)
            nc.vector.tensor_sub(self.s_t1[:, sl, :], self.s_xb[:, c, :], self.s_mb[:, :])
            i = nc.vector.tensor_mul(self.s_y[:, c, :], self.s_t1[:, sl, :], self.s_rb[:, :])
            if c == 15:
                ev = P.inc(i, "dv")
        return ev

    def ln_from_dram(self, src, ts, y_free_ev=None):
        nc, P = self.nc, self.P
        self._sq_free = [[P.ev("mm"), P.ev("dv")]] * 4
        sB = self.ln_stats_start()
        xrot = Rot(4)
        pend = {}
        for c in range(16):
            slot, evfree = xrot.acquire()
            P.wait("sync", evfree)
            i = nc.sync.dma_start(out=self.s_xc[:, slot, :], in_=src[:, c, ts])
            pend[c] = (slot, P.inc(i, "lx", 16))
            if c >= 3:
                cc = c - 3
                sl, ev = pend.pop(cc)
                rel = self.ln_stat_chunk(cc, sB, self.s_xc[:, sl, :], ev)
                xrot.release(sl, rel)
        for cc in range(13, 16):
            sl, ev = pend.pop(cc)
            rel = self.ln_stat_chunk(cc, sB, self.s_xc[:, sl, :], ev)
            xrot.release(sl, rel)
        ev_stats = self.ln_finish_stats(sB)
        self.P.barrier()
        ev = self.ln_apply(ev_stats, y_free_ev)
        self.P.barrier()
        return ev

    def proj(self, wname, y_ap_fn, mlist, evac_fn, extra_rhs_ev=None):
        nc, P = self.nc, self.P
        wrot = self.wrot
        pend = {}
        for idx in range(min(2, len(mlist))):
            slot, evfree = wrot.acquire()
            pend[idx] = (slot, self.wload(wname, mlist[idx], slot, evfree))
        nxt = min(2, len(mlist))
        for idx, m in enumerate(mlist):
            slot, ev_w = pend.pop(idx)
            sB, evB = self.rB.acquire()
            P.wait("tensor", [ev_w, evB, extra_rhs_ev])
            wv = self.s_w[:, slot, 0:2048].rearrange("p (k m) -> p k m", m=128)
            last = None
            for k in range(16):
                last = nc.tensor.matmul(self.psB[:, sB, :], wv[:, k, :], y_ap_fn(k),
                                        start=(k == 0), stop=(k == 15))
            ev_mm = P.inc(last, "mm")
            wrot.release(slot, ev_mm)
            if nxt < len(mlist):
                s2, evf2 = wrot.acquire()
                pend[nxt] = (s2, self.wload(wname, mlist[nxt], s2, evf2))
                nxt += 1
            ev_done = evac_fn(idx, m, self.psB[:, sB, :], ev_mm)
            self.rB.release(sB, ev_done)
            self.P.barrier()

    def phase0(self):
        nc, P = self.nc, self.P
        P.wait("tensor", self.ev_consts)
        evrot = Rot(6)
        y_free = None
        for t in range(NT):
            ts = slice(t * 512, (t + 1) * 512)
            ev_y = self.ln_from_dram(self.d_in["x"], ts, y_free)

            def evac(idx, m, ps, ev_mm, ts=ts):
                slot, evfree = evrot.acquire()
                P.wait("scalar", [ev_mm, evfree, self.ev_bqs])
                if m < 16:
                    i = nc.scalar.activation(self.s_ev[:, slot, :], ps, AF.Identity,
                                             bias=self.c_bqs[:, m:m + 1], scale=INV_SQRT_HD)
                else:
                    i = nc.scalar.activation(self.s_ev[:, slot, :], ps, AF.Identity,
                                             bias=self.c_bqkv[:, m:m + 1], scale=FP(1.0))
                ev_a = P.inc(i, "ac")
                if m < 16:
                    dst = self.qa_d[:, m, ts]
                elif m == 16:
                    dst = self.ag1_in[0:128, ts]
                else:
                    dst = self.ag1_in[128:256, ts]
                i = nc.scalar.dma_start(out=dst, in_=self.s_ev[:, slot, :])
                ev_st = P.inc(i, "ls", 16)
                evrot.release(slot, ev_st)
                return ev_a

            P.wait("tensor", ev_y)
            self.proj("wqkv", lambda k: self.s_y[:, k, :], list(range(18)), evac)
            y_free = P.ev("mm")

    def _span_loads(self, layer, h, buf, guard=None):
        nc, P = self.nc, self.P
        P.wait("sync", guard)
        if layer == "a":
            P.wait("sync", ("ag", 1))
            plan = ((self.s_ka, self.ag1_out, 256, 0, self.ag1_in, 0),
                    (self.s_va, self.ag1_out, 256, 128, self.ag1_in, 128))
        else:
            gi, hr = h // 4, h % 4
            P.wait("sync", self.ev_ag2g[gi])
            plan = ((self.s_ka, self.agk_out[gi], 512, hr * 128, self.ag2_in, h * 128),
                    (self.s_va, self.agv_out[gi], 512, hr * 128, self.ag2_in, 2048 + h * 128))
        for dst_t, agout, stride, row, agin, irow in plan:
            for sgc in range(16):
                rho, sig = _SC_MAP[sgc]
                src = agout[rho * stride + row: rho * stride + row + 128,
                            sig * 512:(sig + 1) * 512]
                i = nc.sync.dma_start(
                    out=dst_t[:, buf, 4 * sgc:4 * sgc + 4, :],
                    in_=src.rearrange("p (c m) -> p c m", m=128))
                P.inc(i, "ka%d" % buf, 16)
            for u in range(4):
                src = agin[irow:irow + 128, u * 512:(u + 1) * 512]
                i = nc.sync.dma_start(
                    out=dst_t[:, buf, 64 + 4 * u:64 + 4 * u + 4, :],
                    in_=src.rearrange("p (c m) -> p c m", m=128))
                P.inc(i, "ka%d" % buf, 16)
        return P.ev("ka%d" % buf)

    def _emit_transpose_quad(self, buf, q):
        nc, P = self.nc, self.P
        P.wait("tensor", self._psT_free)
        P.wait("tensor", self._vt_guard[buf])
        P.wait("tensor", P.ev("ka%d" % buf))
        last = None
        with nc.allow_low_precision(reason="bf16 transpose staging, no accumulation"):
            for d in range(4):
                last = nc.tensor.transpose(self.psT[:, d, :], self.s_va[:, buf, 4 * q + d, :],
                                           self.c_ident[:, :])
        ev_t = P.inc(last, "mm")
        P.wait("vector", ev_t)
        i = nc.vector.tensor_copy(
            self.s_vt[:, buf, 4 * q:4 * q + 4, :],
            self.psT[:, 0:4, :])
        self._psT_free = P.inc(i, "dv")

    def flash_layer(self, layer):
        nc, P = self.nc, self.P
        P.wait("tensor", self.ev_consts)
        P.wait("sync", P.ev("ls"))
        if layer == "b":
            i = nc.sync.dma_start(out=self.c_biasb[:], in_=self.d_in["biasb"])
            P.inc(i, "lx", 16)
        self._bias_ev = P.ev("lx")
        fb = FB_A if layer == "a" else FB_B
        qsrc = self.qa_d if layer == "a" else self.qb_d
        aod = self.aoa_d if layer == "a" else self.aob_d
        btab = self.c_biasa if layer == "a" else self.c_biasb
        qrot = Rot(4)
        ptrot = Rot(4)
        zrot = Rot(4)
        rbrot = Rot(2)
        aorot = Rot(4)
        by_head = defaultdict(list)
        for b in fb:
            by_head[b["h"]].append(b)

        def load_q(b):
            slot, evfree = qrot.acquire()
            P.wait("sync", evfree)
            i = nc.sync.dma_start(
                out=self.s_q[:, slot, 0:b["W"]],
                in_=qsrc[:, b["h"], b["u"] * 512 + b["i"] * b["W"]:
                         b["u"] * 512 + (b["i"] + 1) * b["W"]])
            P.inc(i, "lq%d" % slot, 16)
            return slot

        if layer == "a":
            self._span_loads("a", 0, 0, guard=self._vt_guard[0])
            for q in range(20):
                self._emit_transpose_quad(0, q)
            head_buf = {h: 0 for h in range(NH)}
        else:
            self._span_loads("b", 0, 0, guard=self._vt_guard[0])
            for q in range(20):
                self._emit_transpose_quad(0, q)
            head_buf = {h: h % 2 for h in range(NH)}

        pend_tq = []
        for h in range(NH):
            buf = head_buf[h]
            pieces = by_head[h]
            if layer == "b" and h + 1 < NH:
                self._span_loads("b", h + 1, (h + 1) % 2,
                                 guard=self._vt_guard[(h + 1) % 2])
                pend_tq = [((h + 1) % 2, q) for q in range(20)]
            nq = max(1, (len(pend_tq) + len(pieces) - 1) // max(1, len(pieces)))
            qslots = {}
            for idx in range(min(2, len(pieces))):
                qslots[idx] = load_q(pieces[idx])
            for idx, b in enumerate(pieces):
                qslot = qslots.pop(idx)
                if idx + 2 < len(pieces):
                    qslots[idx + 2] = load_q(pieces[idx + 2])
                self._flash_piece(layer, b, buf, qslot, qrot, ptrot, zrot, rbrot,
                                  aorot, btab, aod)
                if layer == "a" and getattr(self, "_pend_conv", None):
                    self._emit_conv_chunks(2)
                for _ in range(nq):
                    if pend_tq:
                        tbuf, tq = pend_tq.pop(0)
                        self._emit_transpose_quad(tbuf, tq)
            while pend_tq:
                tbuf, tq = pend_tq.pop(0)
                self._emit_transpose_quad(tbuf, tq)
            self._vt_guard[buf] = P.ev("mm")

    def _flash_piece(self, layer, b, buf, qslot, qrot, ptrot, zrot, rbrot, aorot,
                     btab, aod):
        nc, P = self.nc, self.P
        W = b["W"]
        n_all = b["npos"] + b["ndiag"]
        wdiv = W // 128
        sO, evO = self.rB.acquire()
        P.wait("tensor", [evO, self._psZ_free, P.ev("ka%d" % buf),
                          P.ev("lq%d" % qslot)])
        state = []
        for ci in range(n_all):
            if ci < b["npos"]:
                pos = ci
                bias_c = b["bias0"] + ci
                diag = None
            else:
                dd = ci - b["npos"]
                pos = 64 + 4 * b["u"] + b["i"] * wdiv + dd
                bias_c = b["bias0"] + b["npos"] + dd
                diag = dd
            sA, evA = self.rA.acquire()
            P.wait("tensor", evA)
            i = nc.tensor.matmul(self.psA[:, sA, 0:W], self.s_ka[:, buf, pos, :],
                                 self.s_q[:, qslot, 0:W], start=True, stop=True)
            ev_qk = P.inc(i, "mm")
            pt, evp = ptrot.acquire()
            P.wait("scalar", [ev_qk, evp, self._bias_ev])
            i = nc.scalar.activation(self.s_pt[:, pt, 0:W], self.psA[:, sA, 0:W],
                                     AF.Exp, bias=btab[:, bias_c:bias_c + 1],
                                     scale=FP(1.0))
            ev_exp = P.inc(i, "ac")
            self.rA.release(sA, ev_exp)
            ev_ready = ev_exp
            if diag is not None:
                mo = MASK_OFF[(W, diag)]
                P.wait("vector", ev_exp)
                i = nc.vector.tensor_mul(self.s_pt[:, pt, 0:W], self.s_pt[:, pt, 0:W],
                                         self.c_masks[:, mo:mo + W])
                ev_ready = P.inc(i, "dv")
            state.append((ci, pt, ev_ready))
            if len(state) > 2:
                self._flash_zav(state.pop(0), b, buf, sO, W, n_all, ptrot)
        while state:
            self._flash_zav(state.pop(0), b, buf, sO, W, n_all, ptrot)
        ev_mm = P.ev("mm")
        zs, evz = zrot.acquire()
        P.wait("vector", [ev_mm, evz])
        with nc.allow_low_precision(reason="softmax normalizer to bf16 for broadcast"):
            i = nc.vector.reciprocal(self.s_zr[0:1, zs, 0:W], self.psZ[0:1, 0, 0:W])
        ev_zr = P.inc(i, "dv")
        self._psZ_free = ev_zr
        sA, evA = self.rA.acquire()
        P.wait("tensor", [ev_zr, evA])
        i = nc.tensor.matmul(self.psA[:, sA, 0:W], self.c_ones[0:1, :],
                             self.s_zr[0:1, zs, 0:W], start=True, stop=True)
        ev_bc = P.inc(i, "mm")
        zrot.release(zs, ev_bc)
        rb, evrb = rbrot.acquire()
        P.wait("vector", [ev_bc, evrb])
        i = nc.vector.tensor_copy(self.s_rbb[:, rb, 0:W], self.psA[:, sA, 0:W])
        ev_rbc = P.inc(i, "dv")
        self.rA.release(sA, ev_rbc)
        ao, evao = aorot.acquire()
        P.wait("vector", evao)
        i = nc.vector.tensor_mul(self.s_aos[:, ao, 0:W], self.psB[:, sO, 0:W],
                                 self.s_rbb[:, rb, 0:W])
        ev_n = P.inc(i, "dv")
        rbrot.release(rb, ev_n)
        self.rB.release(sO, ev_n)
        qrot.release(qslot, P.ev("mm"))
        P.wait("scalar", ev_n)
        c0 = b["u"] * 512 + b["i"] * W
        i = nc.scalar.dma_start(out=aod[:, b["h"], c0:c0 + W],
                                in_=self.s_aos[:, ao, 0:W])
        ev_st = P.inc(i, "ls", 16)
        aorot.release(ao, ev_st)

    def _flash_zav(self, st, b, buf, sO, W, n_all, ptrot):
        nc, P = self.nc, self.P
        ci, pt, ev_ready = st
        if ci < b["npos"]:
            pos = ci
        else:
            pos = 64 + 4 * b["u"] + b["i"] * (W // 128) + (ci - b["npos"])
        P.wait("tensor", ev_ready)
        nc.tensor.matmul(self.psZ[0:1, 0, 0:W], self.c_ones[:, 0:1],
                         self.s_pt[:, pt, 0:W], start=(ci == 0), stop=(ci == n_all - 1))
        i = nc.tensor.matmul(self.psB[:, sO, 0:W], self.s_vt[:, buf, pos, :],
                             self.s_pt[:, pt, 0:W], start=(ci == 0),
                             stop=(ci == n_all - 1))
        ev_av = P.inc(i, "mm")
        ptrot.release(pt, ev_av)

    def _ln_stats_begin(self, pool):
        P = self.P
        self._sq_free = [[P.ev("mm"), P.ev("dv")]] * 4
        if pool == "B":
            sl, ev = self.rB.acquire()
            ap = self.psB[0:1, sl, :]
        else:
            sl, ev = self.rA.acquire()
            ap = self.psA[0:1, sl, :]
        P.wait("tensor", [ev, self._psZ_free])
        return (pool, sl, ap)

    def _ln_stats_finish(self, hnd):
        nc, P = self.nc, self.P
        pool, sl, ap = hnd
        ev_mm = P.ev("mm")
        P.wait("vector", ev_mm)
        nc.vector.tensor_copy(self.s_st[:, 0, :], self.psZ[0:1, 0, :])
        i = nc.vector.tensor_copy(self.s_st[:, 1, :], ap)
        ev_stats = P.inc(i, "dv")
        (self.rB if pool == "B" else self.rA).release(sl, ev_stats)
        self._psZ_free = ev_stats
        return ev_stats

    def _ln_stat_chunk2(self, c, hnd, f32_ap, ev_ready):
        nc, P = self.nc, self.P
        pool, sl, ap = hnd
        P.wait("vector", ev_ready)
        i = nc.vector.tensor_copy(self.s_xb[:, c, :], f32_ap)
        ev_cast = P.inc(i, "dv")
        qslot = c % 4
        P.wait("scalar", [ev_ready, self._sq_free[qslot]])
        i = nc.scalar.activation(self.s_t1[:, qslot, :], f32_ap, AF.Square)
        ev_sq = P.inc(i, "ac")
        P.wait("tensor", ev_cast)
        nc.tensor.matmul(self.psZ[0:1, 0, :], self.c_ones[:, 0:1], self.s_xb[:, c, :],
                         start=(c == 0), stop=(c == 15))
        P.wait("tensor", ev_sq)
        i = nc.tensor.matmul(ap, self.c_ones[:, 0:1],
                             self.s_t1[:, qslot, :], start=(c == 0), stop=(c == 15))
        ev_mm = P.inc(i, "mm")
        self._sq_free[qslot] = ev_mm
        return [ev_cast, ev_sq]

    def phase_mid(self, layer):
        nc, P = self.nc, self.P
        for t in range(NT):
            ts = slice(t * 512, (t + 1) * 512)
            self._mid_tile(layer, t, ts)

    def _mid_tile(self, layer, t, ts):
        nc, P = self.nc, self.P
        ao_d = self.aoa_d if layer == 1 else self.aob_d
        res_src = self.d_in["x"] if layer == 1 else self.x2_d
        wname = "wfc" if layer == 1 else "wo"
        P.wait("sync", self._ao_free)
        P.wait("sync", P.ev("ls"))
        i = nc.sync.dma_start(out=self.s_ao[:, :, :], in_=ao_d[:, :, ts])
        P.inc(i, "lx", 16)
        ev_ao = P.ev("lx")
        hnd = self._ln_stats_begin("A")
        xin_rot = Rot(2)

        def evac(idx, m, ps, ev_mm, ts=ts, layer=layer):
            xs, evx = xin_rot.acquire()
            P.wait("sync", evx)
            P.wait("sync", P.ev("ls"))
            i = nc.sync.dma_start(out=self.s_xc[:, xs, :], in_=res_src[:, m, ts])
            P.inc(i, "lx", 16)
            P.wait("vector", [ev_mm, self._xout_free])
            P.wait("vector", P.ev("lx"))
            if layer == 1:
                i = nc.vector.scalar_tensor_tensor(
                    self.s_xc[:, 3, :], ps, self.c_bfc[:, m:m + 1],
                    self.s_xc[:, xs, :], ALU.add, ALU.add)
            else:
                i = nc.vector.scalar_tensor_tensor(
                    self.s_xc[:, 3, :], ps, 1.0,
                    self.s_xc[:, xs, :], ALU.mult, ALU.add)
            ev_x1 = P.inc(i, "dv")
            xin_rot.release(xs, ev_x1)
            rel = self._ln_stat_chunk2(m, hnd, self.s_xc[:, 3, :], ev_x1)
            P.wait("scalar", ev_x1)
            i = nc.scalar.dma_start(out=self.xr_d[:, m, ts], in_=self.s_xc[:, 3, :])
            ev_sp = P.inc(i, "ls", 16)
            self._xout_free = [rel[0], rel[1], ev_sp]
            return ev_x1

        self.proj(wname, lambda k: self.s_ao[:, k, :], list(range(16)), evac,
                  extra_rhs_ev=ev_ao)
        ev_stats = self._ln_stats_finish(hnd)
        ev_y = self.ln_apply(ev_stats, self._y_free)
        self._ffn_tile(t, ts, ev_y, layer)
        if layer == 1:
            ev_stats2 = self._ln_stats_finish(self._hnd2)
            ev_y2 = self.ln_apply(ev_stats2, [P.ev("mm"), P.ev("dv")])
            for wn, kind in (("wq", "q"), ("wk", "k"), ("wv", "v")):
                evr = Rot(3)

                def evac3(idx, m, ps, ev_mm, wn=wn, kind=kind, ts=ts, evr=evr):
                    slot, evfree = evr.acquire()
                    P.wait("scalar", [ev_mm, evfree])
                    sc = INV_SQRT_HD if kind == "q" else FP(1.0)
                    i = nc.scalar.activation(self.s_ev[:, slot, :], ps, AF.Identity,
                                             scale=sc)
                    ev_a = P.inc(i, "ac")
                    P.wait("scalar", ev_a)
                    if kind == "q":
                        dst = self.qb_d[:, m, ts]
                    elif kind == "k":
                        dst = self.ag2_in[m * 128:(m + 1) * 128, ts]
                    else:
                        dst = self.ag2_in[2048 + m * 128:2048 + (m + 1) * 128, ts]
                    i = nc.scalar.dma_start(out=dst, in_=self.s_ev[:, slot, :])
                    ev_st = P.inc(i, "ls", 16)
                    evr.release(slot, ev_st)
                    return ev_a

                P.wait("tensor", ev_y2)
                self.proj(wn, lambda k: self.s_y[:, k, :], list(range(16)), evac3)
            self._y_free = P.ev("mm")
        else:
            self._y_free = P.ev("mm")

    def _ffn_tile(self, t, ts, ev_y, layer):
        nc, P = self.nc, self.P
        P.wait("tensor", ev_y)
        wrot = self.wrot
        t1rot = Rot(4)
        t1rot.free_ev = [[P.ev("mm"), P.ev("dv")]] * 4

        mid_guard = self._mid_free
        pend = {}
        wload4 = self.wload

        for bi in range(2):
            for wi, n in enumerate(("sw1", "sw2")):
                slot, evf = wrot.acquire()
                pend[(bi, wi)] = (slot, wload4(n, bi, slot, evf))
        nxt = 2
        for bi in range(32):
            s1, ev1 = pend.pop((bi, 0))
            s2, ev2 = pend.pop((bi, 1))
            wv1 = self.s_w[:, s1, :].rearrange("p (k m) -> p k m", m=256)
            wv2 = self.s_w[:, s2, :].rearrange("p (k m) -> p k m", m=256)
            for c2 in range(2):
                hc = bi * 2 + c2
                sg, evg = self.rA.acquire()
                su, evu = self.rA.acquire()
                P.wait("tensor", [ev1, ev2, evg, evu])
                for k in range(16):
                    nc.tensor.matmul(self.psA[:, sg, :], wv1[:, k, 128 * c2:128 * c2 + 128],
                                     self.s_y[:, k, :], start=(k == 0), stop=(k == 15))
                last = None
                for k in range(16):
                    last = nc.tensor.matmul(self.psA[:, su, :], wv2[:, k, 128 * c2:128 * c2 + 128],
                                            self.s_y[:, k, :], start=(k == 0), stop=(k == 15))
                ev_u = P.inc(last, "mm")
                tslot, evt = t1rot.acquire()
                P.wait("scalar", [ev_u, evt])
                i = nc.scalar.activation(self.s_t1[:, tslot, :], self.psA[:, sg, :], AF.Silu)
                ev_si = P.inc(i, "ac")
                P.wait("vector", [ev_si, mid_guard])
                if hc == 0:
                    nc.vector.tensor_copy(self.s_dbg2[:, 0, :], self.psA[:, sg, :])
                    nc.vector.tensor_copy(self.s_dbg2[:, 1, :], self.psA[:, su, :])
                    nc.vector.tensor_copy(self.s_dbg2[:, 2, :], self.s_t1[:, tslot, :])
                    nc.vector.tensor_copy(self.s_dbg2[:, 3, :], self.s_y[:, 0, :])
                self.rA.release(sg, ev_si)
                i = nc.vector.tensor_mul(self.s_mid[:, hc, :], self.s_t1[:, tslot, :],
                                         self.psA[:, su, :])
                ev_m = P.inc(i, "dv")
                self.rA.release(su, ev_m)
                t1rot.release(tslot, ev_m)
            ev_wdone = P.ev("mm")
            wrot.release(s1, ev_wdone)
            wrot.release(s2, ev_wdone)
            if nxt < 32:
                for wi, n in enumerate(("sw1", "sw2")):
                    slot, evf = wrot.acquire()
                    pend[(nxt, wi)] = (slot, wload4(n, nxt, slot, evf))
                nxt += 1

        self._router_gating(ev_y)

        ev_mid_a = P.ev("dv")
        pend = {}
        for ii in range(2):
            slot, evf = wrot.acquire()
            pend[ii] = (slot, wload4("sw3", ii, slot, evf))
        nxt = 2
        for m in range(16):
            outs = []
            for half in range(2):
                ii = m * 2 + half
                slot, ev_w = pend.pop(ii)
                wv = self.s_w[:, slot, :].rearrange("p (k m) -> p k m", m=128)
                if half == 0:
                    sB, evB = self.rB.acquire()
                    P.wait("tensor", [evB, ev_mid_a])
                P.wait("tensor", ev_w)
                last = None
                for kc in range(32):
                    last = nc.tensor.matmul(self.psB[:, sB, :], wv[:, kc, :],
                                            self.s_mid[:, half * 32 + kc, :],
                                            start=(half == 0 and kc == 0),
                                            stop=(half == 1 and kc == 31))
                ev_mm = P.inc(last, "mm")
                wrot.release(slot, ev_mm)
                if nxt < 32:
                    slot2, evf2 = wrot.acquire()
                    pend[nxt] = (slot2, wload4("sw3", nxt, slot2, evf2))
                    nxt += 1
            P.wait("vector", [ev_mm, self._oacc_free])
            i = nc.vector.tensor_copy(self.s_oacc[:, m, :], self.psB[:, sB, :])
            ev_oc = P.inc(i, "dv")
            self.rB.release(sB, ev_oc)

        ev_mid_rd = P.ev("mm")
        pend = {}
        order = []
        for e in range(8):
            for bp in range(4):
                order.append((e, bp))
        for ii in range(2):
            e, bp = order[ii]
            sl1, evf1 = wrot.acquire()
            sl2, evf2 = wrot.acquire()
            pend[ii] = ((sl1, wload4("w12", e * 8 + bp, sl1, evf1)),
                        (sl2, wload4("w12", e * 8 + 4 + bp, sl2, evf2)))
        nxt = 2
        for ii, (e, bp) in enumerate(order):
            (sl1, ev1), (sl2, ev2) = pend.pop(ii)
            wv1 = self.s_w[:, sl1, :].rearrange("p (k m) -> p k m", m=256)
            wv2 = self.s_w[:, sl2, :].rearrange("p (k m) -> p k m", m=256)
            for c2 in range(2):
                hc = 2 * bp + c2
                s1a, ev1a = self.rA.acquire()
                s2a, ev2a = self.rA.acquire()
                P.wait("tensor", [ev1, ev2, ev1a, ev2a])
                for k in range(16):
                    nc.tensor.matmul(self.psA[:, s1a, :], wv1[:, k, 128 * c2:128 * c2 + 128],
                                     self.s_y[:, k, :], start=(k == 0), stop=(k == 15))
                last = None
                for k in range(16):
                    last = nc.tensor.matmul(self.psA[:, s2a, :], wv2[:, k, 128 * c2:128 * c2 + 128],
                                            self.s_y[:, k, :], start=(k == 0), stop=(k == 15))
                ev_u = P.inc(last, "mm")
                tslot, evt = t1rot.acquire()
                P.wait("scalar", [ev_u, evt])
                i = nc.scalar.activation(self.s_t1[:, tslot, :], self.psA[:, s1a, :], AF.Silu)
                ev_si = P.inc(i, "ac")
                self.rA.release(s1a, ev_si)
                t2slot, evt2 = t1rot.acquire()
                P.wait("vector", [ev_si, evt2, ev_mid_rd, self._gb_ready])
                i = nc.vector.tensor_mul(self.s_t1[:, t2slot, :], self.s_t1[:, tslot, :],
                                         self.psA[:, s2a, :])
                i = nc.vector.tensor_mul(self.s_mid[:, e * 8 + hc, :],
                                         self.s_t1[:, t2slot, :], self.s_gb[:, e, :])
                ev_m = P.inc(i, "dv")
                self.rA.release(s2a, ev_m)
                t1rot.release(tslot, ev_m)
                t1rot.release(t2slot, ev_m)
            ev_wdone = P.ev("mm")
            wrot.release(sl1, ev_wdone)
            wrot.release(sl2, ev_wdone)
            if nxt < 32:
                e2, bp2 = order[nxt]
                sl1, evf1 = wrot.acquire()
                sl2, evf2 = wrot.acquire()
                pend[nxt] = ((sl1, wload4("w12", e2 * 8 + bp2, sl1, evf1)),
                             (sl2, wload4("w12", e2 * 8 + 4 + bp2, sl2, evf2)))
                nxt += 1

        if layer == 1:
            self._hnd2 = self._ln_stats_begin("A")
        ev_mid_d = P.ev("dv")
        pend = {}
        for ii in range(2):
            slot, evf = wrot.acquire()
            pend[ii] = (slot, wload4("w3", ii, slot, evf))
        nxt = 2
        xin_rot = Rot(2)
        for m in range(16):
            for half in range(2):
                ii = m * 2 + half
                slot, ev_w = pend.pop(ii)
                wv = self.s_w[:, slot, :].rearrange("p (k m) -> p k m", m=128)
                if half == 0:
                    sB, evB = self.rB.acquire()
                    P.wait("tensor", [evB, ev_mid_d])
                P.wait("tensor", ev_w)
                last = None
                for idx2 in range(32):
                    e4, kc = idx2 // 8, idx2 % 8
                    e = half * 4 + e4
                    last = nc.tensor.matmul(self.psB[:, sB, :], wv[:, idx2, :],
                                            self.s_mid[:, e * 8 + kc, :],
                                            start=(half == 0 and idx2 == 0),
                                            stop=(half == 1 and idx2 == 31))
                ev_mm = P.inc(last, "mm")
                wrot.release(slot, ev_mm)
                if nxt < 32:
                    slot2, evf2 = wrot.acquire()
                    pend[nxt] = (slot2, wload4("w3", nxt, slot2, evf2))
                    nxt += 1
            xs, evx = xin_rot.acquire()
            P.wait("sync", evx)
            P.wait("sync", P.ev("ls"))
            i = nc.sync.dma_start(out=self.s_xc[:, xs, :], in_=self.xr_d[:, m, ts])
            P.inc(i, "lx", 16)
            P.wait("vector", [ev_mm, P.ev("lx"), self._xout_free])
            i = nc.vector.scalar_tensor_tensor(
                self.s_xc[:, 2, :], self.psB[:, sB, :], 1.0,
                self.s_oacc[:, m, :], ALU.mult, ALU.add)
            i = nc.vector.tensor_add(self.s_xc[:, 3, :], self.s_xc[:, 2, :],
                                     self.s_xc[:, xs, :])
            ev_x2 = P.inc(i, "dv")
            self.rB.release(sB, ev_x2)
            xin_rot.release(xs, ev_x2)
            relx = [ev_x2]
            if layer == 1:
                relx = self._ln_stat_chunk2(m, self._hnd2, self.s_xc[:, 3, :], ev_x2)
            P.wait("scalar", ev_x2)
            dst = self.x2_d[:, m, ts] if layer == 1 else self.out[:, m, ts]
            i = nc.scalar.dma_start(out=dst, in_=self.s_xc[:, 3, :])
            ev_sp = P.inc(i, "ls", 16)
            self._xout_free = relx + [ev_sp]
        self._oacc_free = [P.ev("mm"), P.ev("dv")]
        self._mid_free = P.ev("mm")
        self._ao_free = [P.ev("mm"), P.ev("dv")]

    def _router_gating(self, ev_y):
        nc, P = self.nc, self.P
        P.wait("tensor", [self._psZ_free, self.ev_router, ev_y])
        last = None
        for k in range(16):
            last = nc.tensor.matmul(self.psZ[0:8, 0, :], self.c_router[:, k * 8:(k + 1) * 8],
                                    self.s_y[:, k, :], start=(k == 0), stop=(k == 15))
        ev_mm = P.inc(last, "mm")
        P.wait("scalar", ev_mm)
        i = nc.scalar.activation(self.s_gr[:, 1, :], self.psZ[0:8, 0, :], AF.Tanh,
                                 scale=FP(0.5))
        ev_t = P.inc(i, "ac")
        P.wait("vector", ev_t)
        i = nc.vector.tensor_scalar(self.s_gr[:, 0, :], self.s_gr[:, 1, :], 0.5, 0.5,
                                    ALU.mult, ALU.add)
        ev_sc = P.inc(i, "dv")
        self._psZ_free = ev_sc
        i = nc.vector.tensor_scalar_add(self.s_gr[:, 1, :], self.s_gr[:, 0, :],
                                        self.c_ebias[:, 0:1])
        nc.vector.tensor_copy(self.s_gr16[:, 0, :], self.s_gr[:, 0, :])
        i = nc.vector.tensor_copy(self.s_gr16[:, 1, :], self.s_gr[:, 1, :])
        ev_16 = P.inc(i, "dv")
        P.wait("tensor", [ev_16, self._psT_free])
        with nc.allow_low_precision(reason="router transpose staging"):
            for d in range(4):
                nc.tensor.transpose(self.psT[:, d, 0:8], self.s_gr16[0:8, 1, 128 * d:128 * (d + 1)],
                                    self.c_ident[0:8, 0:8])
            last = None
            for d in range(4):
                last = nc.tensor.transpose(self.psT[:, 4 + d, 0:8],
                                           self.s_gr16[0:8, 0, 128 * d:128 * (d + 1)],
                                           self.c_ident[0:8, 0:8])
        ev_tr = P.inc(last, "mm")
        P.wait("vector", ev_tr)
        i = nc.vector.tensor_copy(self.s_gtm16[:, 0:8, :], self.psT[:, 0:8, 0:8])
        ev_tm = P.inc(i, "dv")
        self._psT_free = ev_tm
        nc.vector.tensor_copy(self.s_dbg[:, 0:8, :], self.s_gtm16[:, 0:8, :])
        g = self.s_gtm

        def _sync(i):
            ev = P.inc(i, "dv")
            P.wait("vector", ev)
            return ev

        for d in range(4):
            sel = self.s_gtm16[:, d, :]
            sco = self.s_gtm16[:, 4 + d, :]
            _sync(nc.vector.reduce_max(g[:, 0, 0:1], sel, AX.X))
            _sync(nc.vector.tensor_scalar(g[:, 1, :], sel, g[:, 0, 0:1], None, ALU.is_ge))
            _sync(nc.vector.scalar_tensor_tensor(g[:, 2, :], g[:, 1, :], -1e30, sel,
                                                 ALU.mult, ALU.add))
            _sync(nc.vector.reduce_max(g[:, 3, 0:1], g[:, 2, :], AX.X))
            _sync(nc.vector.tensor_scalar(g[:, 4, :], sel, g[:, 3, 0:1], None, ALU.is_ge))
            _sync(nc.vector.tensor_mul(g[:, 5, :], g[:, 4, :], sco))
            _sync(nc.vector.reduce_sum(g[:, 6, 0:1], g[:, 5, :], AX.X))
            _sync(nc.vector.tensor_scalar_add(g[:, 6, 0:1], g[:, 6, 0:1], 1e-9))
            _sync(nc.vector.reciprocal(g[:, 7, 0:1], g[:, 6, 0:1]))
            ev_g = _sync(nc.vector.tensor_scalar(self.s_gtm16[:, d, :], g[:, 5, :],
                                                 g[:, 7, 0:1], None, ALU.mult))
            if d == 0:
                _sync(nc.vector.tensor_copy(self.s_dbg[:, 8:16, :], g[:, :, :]))
        nc.vector.tensor_copy(self.s_dbg[:, 8:12, :], self.s_gtm16[:, 0:4, :])
        P.wait("tensor", [ev_g, self._psT_free])
        with nc.allow_low_precision(reason="router transpose staging"):
            last = None
            for d in range(4):
                last = nc.tensor.transpose(self.psT[0:8, d, :],
                                           self.s_gtm16[:, d, :],
                                           self.c_ident[:, :])
        ev_tr2 = P.inc(last, "mm")
        P.wait("vector", ev_tr2)
        last = None
        for d in range(4):
            last = nc.vector.tensor_copy(self.s_gf[0:8, 128 * d:128 * (d + 1)],
                                         self.psT[0:8, d, 0:128])
        ev_gf = P.inc(last, "dv")
        self._psT_free = ev_gf
        for e in range(8):
            sA, evA = self.rA.acquire()
            P.wait("tensor", [ev_gf, evA])
            i = nc.tensor.matmul(self.psA[:, sA, :], self.c_onehot[0:8, e * 128:(e + 1) * 128],
                                 self.s_gf[0:8, :], start=True, stop=True)
            ev_b = P.inc(i, "mm")
            P.wait("vector", [ev_b, self._gb_free])
            i = nc.vector.tensor_copy(self.s_gb[:, e, :], self.psA[:, sA, :])
            ev_c = P.inc(i, "dv")
            self.rA.release(sA, ev_c)
        self._gb_ready = P.ev("dv")
        self._gb_free = None

    def allgather(self, which):
        nc, P = self.nc, self.P
        groups = [[0, 1, 2, 3], [4, 5, 6, 7]]
        P.wait("gpsimd", P.ev("ls"))
        if which == 1:
            i = nc.gpsimd.collective_compute(
                "AllGather", ALU.bypass, replica_groups=groups,
                ins=[self.ag1_in.opt()], outs=[self.ag1_out.opt()])
            return P.inc(i, "ag", 1)
        self.ev_ag2g = []
        for gi in range(4):
            rs = slice(512 * gi, 512 * (gi + 1))
            i = nc.gpsimd.collective_compute(
                "AllGather", ALU.bypass, replica_groups=groups,
                ins=[self.ag2_in[rs, :].opt()], outs=[self.agk_out[gi].opt()])
            P.inc(i, "ag", 1)
            rs2 = slice(2048 + 512 * gi, 2048 + 512 * (gi + 1))
            i = nc.gpsimd.collective_compute(
                "AllGather", ALU.bypass, replica_groups=groups,
                ins=[self.ag2_in[rs2, :].opt()], outs=[self.agv_out[gi].opt()])
            ev = P.inc(i, "ag", 1)
            self.ev_ag2g.append(ev)
        return self.ev_ag2g[-1]

    def tap(self, name, src, shape, dt):
        self.taps.append((name, src, shape, dt))

    def finish(self):
        nc, P = self.nc, self.P
        if getattr(self, "_pend_conv", None):
            self._emit_conv_chunks(10000)
        for sname in list(P.sems):
            if sname != "z":
                P.wait("gpsimd", P.ev(sname))
        for name, src, shape, dt in self.taps:
            o = nc.dram_tensor("tap_" + name, shape, dt, kind="ExternalOutput").ap()
            i = nc.gpsimd.dma_start(out=o, in_=src)
            P.inc(i, "lg", 16)
        P.wait("gpsimd", P.ev("lg"))
        for sh in P.sems.values():
            nc.gpsimd.sem_clear(sh)
        self.ctx.close()
        return self.nc

    def build(self):
        self._materialize_sbuf()
        self.load_consts()
        self.ev_conv = {}
        self.convert_weights(["wqkv", "router"])
        nc, P = self.nc, self.P
        P.wait("sync", self.ev_conv["router"])
        i = nc.sync.dma_start(out=self.c_router[:], in_=self.wb["router"][0])
        P.inc(i, "lx", 16)
        self.ev_router = P.ev("lx")
        self.phase0()
        ev_ag1 = self.allgather(1)
        self._pend_conv = []
        self.convert_weights(["wfc", "sw1", "sw2", "sw3", "w12", "w3",
                              "wq", "wk", "wv", "wo"], defer=True)
        if self.upto == "p0":
            self.tap("qa", self.qa_d, [128, 16, 2048], BF16)
            self.tap("ag1", self.ag1_out, [1024, 2048], BF16)
            return self.finish()
        P = self.P
        for e in ("sync", "tensor", "scalar", "vector"):
            P.wait(e, ev_ag1)
        self.flash_layer("a")
        self._emit_conv_chunks(10000)
        if self.upto == "p1a":
            self.tap("aoa", self.aoa_d, [128, 16, 2048], BF16)
            return self.finish()
        self.phase_mid(1)
        if self.upto == "p1b":
            self.tap("x2", self.x2_d, [128, 16, 2048], F32)
            self.tap("qb", self.qb_d, [128, 16, 2048], BF16)
            self.tap("ag2in", self.ag2_in, [4096, 2048], BF16)
            self.tap("xr", self.xr_d, [128, 16, 2048], F32)
            self.tap("mid", self.s_mid[:, :, :], [128, 64, 512], BF16)
            self.tap("oacc", self.s_oacc[:, :, :], [128, 16, 512], BF16)
            self.tap("gb", self.s_gb[:, :, :], [128, 8, 512], BF16)
            self.tap("ylast", self.s_y[:, :, :], [128, 16, 512], BF16)
            self.tap("gf", self.s_gf[:, :], [8, 512], BF16)
            self.tap("gtm16", self.s_gtm16[:, :, :], [128, 8, 8], BF16)
            self.tap("gr", self.s_gr[:, :, :], [8, 2, 512], F32)
            self.tap("gr16", self.s_gr16[:, :, :], [8, 2, 512], BF16)
            self.tap("dbg", self.s_dbg[:, :, :], [128, 16, 8], BF16)
            self.tap("dbg2", self.s_dbg2[:, :, :], [128, 4, 512], BF16)
            self.tap("wsw1b0", self.wb["sw1"][0:1], [1, 128, 4096], BF16)
            self.tap("wsw3b0", self.wb["sw3"][0:1], [1, 128, 4096], BF16)
            return self.finish()
        self.allgather(2)
        self.flash_layer("b")
        if self.upto == "p2a":
            self.tap("aob", self.aob_d, [128, 16, 2048], BF16)
            return self.finish()
        self.phase_mid(2)
        return self.finish()


_CACHE = {}

def _get_nc(upto):
    if upto not in _CACHE:
        _CACHE[upto] = Builder(upto=upto).build()
    return _CACHE[upto]


def kernel(**inputs):
    upto = os.environ.get("ATHENA_UPTO", "full")
    nc = _get_nc(upto)
    in_maps = make_in_maps(inputs)
    res = run_bass_kernel_spmd(nc, in_maps, core_ids=list(range(8)))
    return reassemble(res.results)


# version 04
# speedup vs baseline: 1.2011x; 1.0083x over previous
import os
from collections import defaultdict
from contextlib import ExitStack

import numpy as np
import ml_dtypes

import concourse.bass as bass
import concourse.mybir as mybir
from concourse.bass_utils import run_bass_kernel_spmd

F32 = mybir.dt.float32
BF16 = mybir.dt.bfloat16
AF = mybir.ActivationFunctionType
ALU = mybir.AluOpType
AX = mybir.AxisListType
BF = ml_dtypes.bfloat16

NCH = 16
NT = 4
NH = 16
KCE = [16, 32, 48, 64]
SLOPES = [0.5 ** ((i + 1) * 0.5) for i in range(NH)]
WH = [128, 256, 256] + [512] * 13
NEG = -1.0e9
INV_SQRT_HD = np.float32(1.0 / np.sqrt(128.0))
FP = np.float32

def sc_of(j):
    return [j, 7 - j, 8 + j, 15 - j]

_SC_MAP = {}
for _j in range(4):
    for _u, _s in enumerate(sc_of(_j)):
        _SC_MAP[_s] = (_j, _u)

def flash_blocks(layer):
    col = 0
    out = []
    acol = {}
    for h in range(NH):
        ws = 512 if layer == "a" else WH[h]
        for u in range(4):
            for i in range(512 // ws):
                npos, ndiag = KCE[u], ws // 128
                if layer == "a":
                    if (u, i) not in acol:
                        acol[(u, i)] = col
                        col += npos + ndiag
                    b0 = acol[(u, i)]
                else:
                    b0 = col
                    col += npos + ndiag
                out.append(dict(h=h, u=u, i=i, W=ws, npos=npos, ndiag=ndiag, bias0=b0))
    return out, col

FB_A, BIASA_COLS = flash_blocks("a")
FB_B, BIASB_COLS = flash_blocks("b")

MASK_OFF = {}
_c = 0
for _wd in ((512, 0), (512, 1), (512, 2), (512, 3), (256, 0), (256, 1), (128, 0)):
    MASK_OFF[_wd] = _c
    _c += _wd[0]
MASK_COLS = _c

WSPEC = {
    "wqkv": (18, 2048), "wfc": (16, 2048), "wq": (16, 2048), "wk": (16, 2048),
    "wv": (16, 2048), "wo": (16, 2048), "sw1": (32, 4096), "sw2": (32, 4096),
    "sw3": (32, 4096), "w12": (64, 4096), "w3": (32, 4096), "router": (1, 128),
}


def _mk_masks():
    m = np.zeros((128, MASK_COLS), dtype=BF)
    for (w, d), off in MASK_OFF.items():
        k = np.arange(128)[:, None]
        q = np.arange(w)[None, :]
        m[:, off:off + w] = (128 * d + k <= q).astype(BF)
    return m

def _mk_bias(layer, j):
    fb, ncols = (FB_A, BIASA_COLS) if layer == "a" else (FB_B, BIASB_COLS)
    tab = np.zeros((128, ncols), dtype=np.float32)
    p = np.arange(128, dtype=np.float32)
    for b in fb:
        sc = sc_of(j)[b["u"]]
        q0 = sc * 512 + b["i"] * b["W"]
        slope = 0.0 if layer == "a" else SLOPES[b["h"]]
        R = q0 + b["W"] / 2.0
        for kc in range(b["npos"]):
            c = b["bias0"] + kc
            if (kc + 1) * 128 <= q0:
                tab[:, c] = slope * (128.0 * kc + p - R)
            else:
                tab[:, c] = NEG
        for dd in range(b["ndiag"]):
            c = b["bias0"] + b["npos"] + dd
            kc = q0 // 128 + dd
            tab[:, c] = slope * (128.0 * kc + p - R)
    return tab

def _wblocks(Wt, kcs, mblk):
    K, M = Wt.shape
    nb = M // mblk
    a = Wt.reshape(kcs, 128, nb, mblk).transpose(2, 1, 0, 3)
    return np.ascontiguousarray(a.reshape(nb, 128, kcs * mblk), dtype=np.float32)

def _host_weight_arrays(inp):
    w = {}
    w["wqkv"] = _wblocks(np.asarray(inp["L0_attn_a_w_qkv"], np.float32), 16, 128)
    w["wfc"] = _wblocks(np.asarray(inp["L0_attn_a_w_fc"], np.float32), 16, 128)
    for n, k in (("wq", "L0_attn_b_Wq"), ("wk", "L0_attn_b_Wk"),
                 ("wv", "L0_attn_b_Wv"), ("wo", "L0_attn_b_Wo")):
        w[n] = _wblocks(np.ascontiguousarray(np.asarray(inp[k], np.float32).T), 16, 128)
    w["sw1"] = _wblocks(np.ascontiguousarray(np.asarray(inp["L0_ffn_sw1"], np.float32).T), 16, 256)
    w["sw2"] = _wblocks(np.ascontiguousarray(np.asarray(inp["L0_ffn_sw2"], np.float32).T), 16, 256)
    s3 = np.ascontiguousarray(np.asarray(inp["L0_ffn_sw3"], np.float32).T)
    a = s3.reshape(64, 128, 16, 128).transpose(2, 0, 1, 3)
    a = a.reshape(16, 2, 32, 128, 128).transpose(0, 1, 3, 2, 4)
    w["sw3"] = np.ascontiguousarray(a.reshape(32, 128, 32 * 128), dtype=np.float32)
    w12t = np.asarray(inp["L0_ffn_w12"], np.float32).transpose(0, 2, 1)
    a = w12t.reshape(8, 16, 128, 8, 256).transpose(0, 3, 2, 1, 4)
    w["w12"] = np.ascontiguousarray(a.reshape(64, 128, 16 * 256), dtype=np.float32)
    w3t = np.asarray(inp["L0_ffn_w3"], np.float32).transpose(0, 2, 1)
    a = w3t.reshape(8, 8, 128, 16, 128).transpose(3, 0, 1, 2, 4)
    a = a.reshape(16, 2, 4, 8, 128, 128).transpose(0, 1, 4, 2, 3, 5)
    w["w3"] = np.ascontiguousarray(a.reshape(32, 128, 4 * 8 * 128), dtype=np.float32)
    rt = np.ascontiguousarray(np.asarray(inp["L0_ffn_router_w"], np.float32).T)
    a = rt.reshape(16, 128, 1, 8).transpose(2, 1, 0, 3)
    w["router"] = np.ascontiguousarray(a.reshape(1, 128, 16 * 8), dtype=np.float32)
    return w

def _host_x(x0, core):
    g, j = core // 4, core % 4
    toks = np.concatenate([s * 512 + np.arange(512) for s in sc_of(j)])
    xs = np.asarray(x0, np.float32)[g][toks]
    arr = xs.T.reshape(16, 128, 2048).transpose(1, 0, 2)
    return np.ascontiguousarray(arr, dtype=np.float32)

def make_in_maps(inputs):
    w = _host_weight_arrays(inputs)
    masks = _mk_masks().astype(np.float32)
    ident = np.eye(128, dtype=np.float32)
    ones = np.ones((128, 128), dtype=np.float32)
    onehot = np.zeros((8, 1024), dtype=np.float32)
    for e in range(8):
        onehot[e, e * 128:(e + 1) * 128] = 1
    bqkv = np.ascontiguousarray(np.asarray(inputs["L0_attn_a_b_qkv"], np.float32).reshape(18, 128).T)
    bfc = np.ascontiguousarray(np.asarray(inputs["L0_attn_a_b_fc"], np.float32).reshape(16, 128).T)
    ebias = np.ascontiguousarray(np.asarray(inputs["L0_ffn_expert_bias"], np.float32).reshape(8, 1))
    in_maps = []
    for core in range(8):
        j = core % 4
        m = {
            "x_in": _host_x(inputs["L0_attn_a_x"], core),
            "bqkv": bqkv, "bfc": bfc, "ebias": ebias,
            "ident": ident, "onesin": ones, "onehot": onehot, "masks": masks,
            "biasa": _mk_bias("a", j), "biasb": _mk_bias("b", j),
        }
        for n in WSPEC:
            m["wf_" + n] = w[n]
        in_maps.append(m)
    return in_maps

def reassemble(results):
    out = np.empty((2, 8192, 2048), dtype=np.float32)
    for core in range(8):
        g, j = core // 4, core % 4
        toks = np.concatenate([s * 512 + np.arange(512) for s in sc_of(j)])
        shard = results[core]["out"]
        xs = shard.transpose(1, 0, 2).reshape(2048, 2048)
        out[g, toks, :] = xs.T
    return out


class Prog:
    def __init__(self, nc, ctx):
        self.nc = nc
        self.sems = {}
        self.cnt = defaultdict(int)
        self.lastw = defaultdict(dict)
        names = ["z", "lw0", "lw1", "lw2", "lw3", "lx", "lq0", "lq1", "lq2", "lq3",
                 "ka0", "ka1", "ls", "lg", "mm", "ac", "dv", "ag"]
        names += ["cv_" + n for n in WSPEC]
        for s in names:
            self.sems[s] = ctx.enter_context(nc.semaphore(s))
        self.dma_sems = {n for n in names if n not in ("mm", "ac", "dv", "ag")}
        self._fence_a = nc.dram_tensor("fence_a", [1, 16], F32, kind="Internal").ap()
        self._fence_b = nc.dram_tensor("fence_b", [1, 16], F32, kind="Internal").ap()
        mynums = sorted(h.num for h in self.sems.values())
        myrange = range(mynums[0], mynums[-1] + 1)
        nc.gpsimd.dma_reset(myrange)
        for s in self.sems.values():
            nc.gpsimd.sem_clear(s)
        for _ in range(4):
            nc.gpsimd.nop()
        nc.gpsimd.dma_start(out=self._fence_b, in_=self._fence_a).then_inc(self.sems["z"], 16)
        self.cnt["z"] = 16
        for eng in ("sync", "tensor", "scalar", "vector"):
            e = getattr(nc, eng)
            for _ in range(24):
                e.nop()
            e.wait_ge(self.sems["z"], 16)
            self.lastw[eng]["z"] = 16
        nc.gpsimd.wait_ge(self.sems["z"], 16)
        self.lastw["gpsimd"]["z"] = 16

    def inc(self, inst, sem, n=1):
        inst.then_inc(self.sems[sem], n)
        self.cnt[sem] += n
        return (sem, self.cnt[sem])

    def ev(self, sem):
        return (sem, self.cnt[sem])

    def barrier(self):
        pass

    def wait(self, engname, ev):
        if ev is None:
            return
        if isinstance(ev, list):
            for e in ev:
                self.wait(engname, e)
            return
        sem, v = ev
        if sem in self.dma_sems:
            v = self.cnt[sem]
        if v <= 0 or self.lastw[engname].get(sem, 0) >= v:
            return
        getattr(self.nc, engname).wait_ge(self.sems[sem], v)
        self.lastw[engname][sem] = v


class Rot:
    def __init__(self, n):
        self.n = n
        self.free_ev = [None] * n
        self.i = 0

    def acquire(self):
        idx = self.i
        self.i = (self.i + 1) % self.n
        return idx, self.free_ev[idx]

    def release(self, idx, ev):
        self.free_ev[idx] = ev


def _ceil32(x):
    return (x + 31) // 32 * 32


SERIAL = int(os.environ.get("ATHENA_SERIAL", "0"))

class Builder:
    def __init__(self, upto="full"):
        self.upto = upto
        self.taps = []
        nc = bass.Bass("TRN2", target_bir_lowering=False, debug=False)
        self.nc = nc
        self.ctx = ExitStack()
        self.P = Prog(nc, self.ctx)
        self._declare_dram()
        self._plan_sbuf()
        self._psum()

    def _declare_dram(self):
        nc = self.nc
        self.d_in = {}
        self.d_in["x"] = nc.dram_tensor("x_in", [128, 16, 2048], F32, kind="ExternalInput").ap()
        for n, (nb, bl) in WSPEC.items():
            self.d_in["wf_" + n] = nc.dram_tensor("wf_" + n, [nb, 128, bl], F32, kind="ExternalInput").ap()
        for nm, shp, dt in (("bqkv", [128, 18], F32), ("bfc", [128, 16], F32),
                            ("ebias", [8, 1], F32), ("ident", [128, 128], F32),
                            ("onesin", [128, 128], F32), ("onehot", [8, 1024], F32),
                            ("masks", [128, MASK_COLS], F32),
                            ("biasa", [128, BIASA_COLS], F32),
                            ("biasb", [128, BIASB_COLS], F32)):
            self.d_in[nm] = nc.dram_tensor(nm, shp, dt, kind="ExternalInput").ap()
        self.out = nc.dram_tensor("out", [128, 16, 2048], F32, kind="ExternalOutput").ap()

        self.wb = {}
        for n, (nb, bl) in WSPEC.items():
            self.wb[n] = nc.dram_tensor("wb_" + n, [nb, 128, bl], BF16, kind="Internal").ap()
        self.qa_d = nc.dram_tensor("qa_d", [128, 16, 2048], BF16, kind="Internal").ap()
        self.qb_d = nc.dram_tensor("qb_d", [128, 16, 2048], BF16, kind="Internal").ap()
        self.aoa_d = nc.dram_tensor("aoa_d", [128, 16, 2048], BF16, kind="Internal").ap()
        self.aob_d = nc.dram_tensor("aob_d", [128, 16, 2048], BF16, kind="Internal").ap()
        self.xr_d = nc.dram_tensor("xr_d", [128, 16, 2048], F32, kind="Internal").ap()
        self.x2_d = nc.dram_tensor("x2_d", [128, 16, 2048], F32, kind="Internal").ap()
        self.ag1_in = nc.dram_tensor("ag1_in", [256, 2048], BF16, kind="Internal").ap()
        self.ag1_out = nc.dram_tensor("ag1_out", [1024, 2048], BF16, kind="Internal").ap()
        self.ag2_in = nc.dram_tensor("ag2_in", [4096, 2048], BF16, kind="Internal").ap()
        self.agk_out = nc.dram_tensor("agk_out", [4, 2048, 2048], BF16, kind="Internal").ap()
        self.agv_out = nc.dram_tensor("agv_out", [4, 2048, 2048], BF16, kind="Internal").ap()

    def _plan_sbuf(self):
        nc = self.nc
        self._sb_virt = []
        self._off = 0

        def A(name, shape, dtype):
            per = 1
            for s in shape[1:]:
                per *= s
            per *= mybir.dt.size(dtype)
            h = nc.virtual_sbuf_tensor(name, shape, dtype)
            self._sb_virt.append((h, self._off))
            self._off = _ceil32(self._off + per)
            return h

        self.c_ident = A("c_ident", [128, 128], BF16)
        self.c_identf = A("c_identf", [128, 128], F32)
        self.c_ones = A("c_ones", [128, 128], BF16)
        self.c_onehot = A("c_onehot", [8, 1024], BF16)
        self.c_masks = A("c_masks", [128, MASK_COLS], BF16)
        self.c_biasa = A("c_biasa", [128, BIASA_COLS], F32)
        self.c_bqkv = A("c_bqkv", [128, 18], F32)
        self.c_bqs = A("c_bqs", [128, 16], F32)
        self.c_bfc = A("c_bfc", [128, 16], F32)
        self.c_ebias = A("c_ebias", [8, 1], F32)
        self.c_router = A("c_router", [128, 128], BF16)
        self.s_xc = A("s_xc", [128, 4, 512], F32)
        self.s_st = A("s_st", [1, 4, 512], F32)
        self.s_st16 = A("s_st16", [1, 2, 512], BF16)
        self.s_ev = A("s_ev", [128, 6, 512], BF16)
        self.s_t1 = A("s_t1", [128, 6, 512], BF16)
        self.s_mb = A("s_mb", [128, 512], BF16)
        self.s_rb = A("s_rb", [128, 512], BF16)

        g1 = self._off
        self.s_y = A("s_y", [128, 16, 512], BF16)
        self.s_xb = A("s_xb", [128, 16, 512], BF16)
        self.s_mid = A("s_mid", [128, 64, 512], BF16)
        g1a = self._off
        self.s_ao = A("s_ao", [128, 16, 512], BF16)
        end_g1a = self._off
        self._off = g1a
        self.s_oacc = A("s_oacc", [128, 16, 512], BF16)
        self._off = max(self._off, end_g1a)
        self.s_w = A("s_w", [128, 4, 4096], BF16)
        self.s_gb = A("s_gb", [128, 8, 512], BF16)
        self.s_gr = A("s_gr", [8, 2, 512], F32)
        self.s_gr16 = A("s_gr16", [8, 2, 512], BF16)
        self.s_gf = A("s_gf", [8, 512], BF16)
        self.s_gtm = A("s_gtm", [128, 8, 8], F32)
        self.s_dbg = A("s_dbg", [128, 16, 8], BF16)
        self.s_dbg2 = A("s_dbg2", [128, 4, 512], BF16)
        self.s_gtm16 = A("s_gtm16", [128, 8, 8], BF16)
        end_g1 = self._off

        self._off = g1
        self.c_biasb = A("c_biasb", [128, BIASB_COLS], F32)
        self.s_ka = A("s_ka", [128, 2, 80, 128], BF16)
        self.s_va = A("s_va", [128, 2, 80, 128], BF16)
        self.s_vt = A("s_vt", [128, 2, 80, 128], BF16)
        self.s_q = A("s_q", [128, 4, 512], BF16)
        self.s_pt = A("s_pt", [128, 4, 512], BF16)
        self.s_zr = A("s_zr", [1, 4, 512], BF16)
        self.s_rbb = A("s_rbb", [128, 2, 512], BF16)
        self.s_aos = A("s_aos", [128, 4, 512], BF16)
        end_g2 = self._off

        self._off = max(end_g1, end_g2)
        assert self._off <= 207 * 1024, f"SBUF over budget: {self._off}"
        self.sbuf_total = self._off

    def _materialize_sbuf(self):
        nc = self.nc
        self.arena = self.ctx.enter_context(
            nc.sbuf_tensor("arena", [128, self._off], mybir.dt.uint8))
        base = nc.lookup_mloc(self.arena).addr
        for h, off in self._sb_virt:
            nc._tensor(h.name, list(h.shape), h.dtype, type="SB")
            mloc = nc.lookup_mloc(h)
            mloc.allocated = not nc.target_bir_lowering
            mloc.addr = base + off

    def _psum(self):
        nc = self.nc
        self.psA = self.ctx.enter_context(nc.psum_tensor("psA", [128, 4, 512], F32))
        self.psB = self.ctx.enter_context(nc.psum_tensor("psB", [128, 2, 512], F32))
        self.psZ = self.ctx.enter_context(nc.psum_tensor("psZ", [128, 1, 512], F32))
        self.psT = self.ctx.enter_context(nc.psum_tensor("psT", [128, 8, 128], BF16))
        self.rA = Rot(4)
        self.rB = Rot(2)
        self.rT = Rot(8)
        self._psZ_free = None
        self._psT_free = None
        self._vt_guard = [None, None]
        self._ao_free = None
        self._mid_free = None
        self._oacc_free = None
        self._gb_free = None
        self._gb_ready = None
        self._y_free = None
        self._xout_free = None
        self.wrot = Rot(4)

    def load_consts(self):
        nc, P = self.nc, self.P
        for sb, dr in ((self.c_biasa, "biasa"),
                       (self.c_bqkv, "bqkv"), (self.c_bfc, "bfc"),
                       (self.c_ebias, "ebias"), (self.c_identf, "ident")):
            P.inc(nc.sync.dma_start(out=sb[:], in_=self.d_in[dr]), "lx", 16)
        ev_f32 = P.ev("lx")
        for sb, dr in ((self.c_ident, "ident"), (self.c_ones, "onesin"),
                       (self.c_onehot, "onehot"), (self.c_masks, "masks")):
            P.inc(nc.gpsimd.dma_start(out=sb[:], in_=self.d_in[dr]), "lg", 16)
        self.ev_consts = [ev_f32, P.ev("lg")]
        P.wait("vector", self.ev_consts)
        i = nc.vector.tensor_scalar_mul(self.c_bqs[:, :], self.c_bqkv[:, 0:16], float(INV_SQRT_HD))
        self.ev_bqs = P.inc(i, "dv")

    def convert_weights(self, order, defer=False):
        nc, P = self.nc, self.P
        for n in order:
            nb, bl = WSPEC[n]
            step = max(1, min(nb, (8 * 1024 * 1024) // (128 * bl * 4)))
            ev = None
            for b0 in range(0, nb, step):
                b1 = min(nb, b0 + step)
                if defer:
                    self._pend_conv.append((n, b0, b1))
                    P.cnt["cv_" + n] += 16
                else:
                    i = nc.gpsimd.dma_start(out=self.wb[n][b0:b1],
                                            in_=self.d_in["wf_" + n][b0:b1])
                    ev = P.inc(i, "cv_" + n, 16)
            self.ev_conv[n] = ("cv_" + n, P.cnt["cv_" + n])

    def _emit_conv_chunks(self, k):
        nc, P = self.nc, self.P
        for _ in range(k):
            if not self._pend_conv:
                return
            n, b0, b1 = self._pend_conv.pop(0)
            P.lastw["gpsimd"].pop("mm", None)
            P.wait("gpsimd", ("mm", P.cnt["mm"]))
            i = nc.gpsimd.dma_start(out=self.wb[n][b0:b1], in_=self.d_in["wf_" + n][b0:b1])
            i.then_inc(P.sems["cv_" + n], 16)

    def wload(self, n, b, slot, prior_ev):
        nc, P = self.nc, self.P
        nb, bl = WSPEC[n]
        P.wait("sync", self.ev_conv[n])
        P.wait("sync", prior_ev)
        i = nc.sync.dma_start(out=self.s_w[:, slot, 0:bl], in_=self.wb[n][b])
        return P.inc(i, "lw%d" % slot, 16)

    def ln_stats_start(self):
        P = self.P
        sB, evB = self.rB.acquire()
        P.wait("tensor", evB)
        P.wait("tensor", self._psZ_free)
        return sB

    def ln_stat_chunk(self, c, sB, f32_ap, ev_ready):
        nc, P = self.nc, self.P
        P.wait("vector", ev_ready)
        i = nc.vector.tensor_copy(self.s_xb[:, c, :], f32_ap)
        ev_cast = P.inc(i, "dv")
        qslot = c % 4
        P.wait("scalar", ev_ready)
        P.wait("scalar", self._sq_free[qslot])
        i = nc.scalar.activation(self.s_t1[:, qslot, :], f32_ap, AF.Square)
        ev_sq = P.inc(i, "ac")
        P.wait("tensor", ev_cast)
        nc.tensor.matmul(self.psZ[0:1, 0, :], self.c_ones[:, 0:1], self.s_xb[:, c, :],
                         start=(c == 0), stop=(c == 15))
        P.wait("tensor", ev_sq)
        i = nc.tensor.matmul(self.psB[0:1, sB, :], self.c_ones[:, 0:1],
                             self.s_t1[:, qslot, :], start=(c == 0), stop=(c == 15))
        ev_mm = P.inc(i, "mm")
        self._sq_free[qslot] = ev_mm
        return [ev_cast, ev_sq]

    def ln_finish_stats(self, sB):
        nc, P = self.nc, self.P
        ev_mm = P.ev("mm")
        P.wait("vector", ev_mm)
        nc.vector.tensor_copy(self.s_st[:, 0, :], self.psZ[0:1, 0, :])
        i = nc.vector.tensor_copy(self.s_st[:, 1, :], self.psB[0:1, sB, :])
        ev_stats = P.inc(i, "dv")
        self.rB.release(sB, ev_stats)
        self._psZ_free = ev_stats
        return ev_stats

    def ln_apply(self, ev_stats, y_free_ev=None):
        nc, P = self.nc, self.P
        st, st16 = self.s_st, self.s_st16
        P.wait("vector", ev_stats)
        nc.vector.tensor_scalar_mul(st[:, 2, :], st[:, 0, :], 1.0 / 2048.0)
        nc.vector.tensor_scalar_mul(st[:, 3, :], st[:, 1, :], 1.0 / 2048.0)
        nc.vector.tensor_mul(st[:, 0, :], st[:, 2, :], st[:, 2, :])
        i = nc.vector.tensor_sub(st[:, 1, :], st[:, 3, :], st[:, 0, :])
        ev_var = P.inc(i, "dv")
        P.wait("scalar", ev_var)
        nc.scalar.activation(st[:, 0, :], st[:, 1, :], AF.Ln, bias=FP(1e-5))
        i = nc.scalar.activation(st16[:, 1, :], st[:, 0, :], AF.Exp, scale=FP(-0.5))
        ev_rstd = P.inc(i, "ac")
        P.wait("vector", ev_rstd)
        i = nc.vector.tensor_copy(st16[:, 0, :], st[:, 2, :])
        ev_m16 = P.inc(i, "dv")
        sA, evA = self.rA.acquire()
        sA2, evA2 = self.rA.acquire()
        P.wait("tensor", [ev_m16, ev_rstd, evA, evA2])
        nc.tensor.matmul(self.psA[:, sA, :], self.c_ones[0:1, :], st16[:, 0, :],
                         start=True, stop=True)
        i = nc.tensor.matmul(self.psA[:, sA2, :], self.c_ones[0:1, :], st16[:, 1, :],
                             start=True, stop=True)
        ev_bc = P.inc(i, "mm")
        P.wait("vector", ev_bc)
        nc.vector.tensor_copy(self.s_mb[:, :], self.psA[:, sA, :])
        i = nc.vector.tensor_copy(self.s_rb[:, :], self.psA[:, sA2, :])
        ev_bcs = P.inc(i, "dv")
        self.rA.release(sA, ev_bcs)
        self.rA.release(sA2, ev_bcs)
        P.wait("vector", y_free_ev)
        ev = None
        for c in range(16):
            sl = 4 + (c % 2)
            nc.vector.tensor_sub(self.s_t1[:, sl, :], self.s_xb[:, c, :], self.s_mb[:, :])
            i = nc.vector.tensor_mul(self.s_y[:, c, :], self.s_t1[:, sl, :], self.s_rb[:, :])
            if c == 15:
                ev = P.inc(i, "dv")
        return ev

    def ln_from_dram(self, src, ts, y_free_ev=None):
        nc, P = self.nc, self.P
        self._sq_free = [[P.ev("mm"), P.ev("dv")]] * 4
        sB = self.ln_stats_start()
        xrot = Rot(4)
        pend = {}
        for c in range(16):
            slot, evfree = xrot.acquire()
            P.wait("sync", evfree)
            i = nc.sync.dma_start(out=self.s_xc[:, slot, :], in_=src[:, c, ts])
            pend[c] = (slot, P.inc(i, "lx", 16))
            if c >= 3:
                cc = c - 3
                sl, ev = pend.pop(cc)
                rel = self.ln_stat_chunk(cc, sB, self.s_xc[:, sl, :], ev)
                xrot.release(sl, rel)
        for cc in range(13, 16):
            sl, ev = pend.pop(cc)
            rel = self.ln_stat_chunk(cc, sB, self.s_xc[:, sl, :], ev)
            xrot.release(sl, rel)
        ev_stats = self.ln_finish_stats(sB)
        self.P.barrier()
        ev = self.ln_apply(ev_stats, y_free_ev)
        self.P.barrier()
        return ev

    def proj(self, wname, y_ap_fn, mlist, evac_fn, extra_rhs_ev=None):
        nc, P = self.nc, self.P
        wrot = self.wrot
        pend = {}
        for idx in range(min(2, len(mlist))):
            slot, evfree = wrot.acquire()
            pend[idx] = (slot, self.wload(wname, mlist[idx], slot, evfree))
        nxt = min(2, len(mlist))
        for idx, m in enumerate(mlist):
            slot, ev_w = pend.pop(idx)
            sB, evB = self.rB.acquire()
            P.wait("tensor", [ev_w, evB, extra_rhs_ev])
            wv = self.s_w[:, slot, 0:2048].rearrange("p (k m) -> p k m", m=128)
            last = None
            for k in range(16):
                last = nc.tensor.matmul(self.psB[:, sB, :], wv[:, k, :], y_ap_fn(k),
                                        start=(k == 0), stop=(k == 15))
            ev_mm = P.inc(last, "mm")
            wrot.release(slot, ev_mm)
            if nxt < len(mlist):
                s2, evf2 = wrot.acquire()
                pend[nxt] = (s2, self.wload(wname, mlist[nxt], s2, evf2))
                nxt += 1
            ev_done = evac_fn(idx, m, self.psB[:, sB, :], ev_mm)
            self.rB.release(sB, ev_done)
            self.P.barrier()

    def phase0(self):
        nc, P = self.nc, self.P
        P.wait("tensor", self.ev_consts)
        evrot = Rot(6)
        y_free = None
        for t in range(NT):
            ts = slice(t * 512, (t + 1) * 512)
            ev_y = self.ln_from_dram(self.d_in["x"], ts, y_free)

            def evac(idx, m, ps, ev_mm, ts=ts):
                slot, evfree = evrot.acquire()
                P.wait("scalar", [ev_mm, evfree, self.ev_bqs])
                if m < 16:
                    i = nc.scalar.activation(self.s_ev[:, slot, :], ps, AF.Identity,
                                             bias=self.c_bqs[:, m:m + 1], scale=INV_SQRT_HD)
                else:
                    i = nc.scalar.activation(self.s_ev[:, slot, :], ps, AF.Identity,
                                             bias=self.c_bqkv[:, m:m + 1], scale=FP(1.0))
                ev_a = P.inc(i, "ac")
                if m < 16:
                    dst = self.qa_d[:, m, ts]
                elif m == 16:
                    dst = self.ag1_in[0:128, ts]
                else:
                    dst = self.ag1_in[128:256, ts]
                i = nc.scalar.dma_start(out=dst, in_=self.s_ev[:, slot, :])
                ev_st = P.inc(i, "ls", 16)
                evrot.release(slot, ev_st)
                return ev_a

            P.wait("tensor", ev_y)
            self.proj("wqkv", lambda k: self.s_y[:, k, :], list(range(18)), evac)
            y_free = P.ev("mm")

    def _span_loads(self, layer, h, buf, guard=None):
        nc, P = self.nc, self.P
        P.wait("sync", guard)
        if layer == "a":
            P.wait("sync", ("ag", 1))
            plan = ((self.s_ka, self.ag1_out, 256, 0, self.ag1_in, 0),
                    (self.s_va, self.ag1_out, 256, 128, self.ag1_in, 128))
        else:
            gi, hr = h // 4, h % 4
            P.wait("sync", self.ev_ag2g[gi])
            plan = ((self.s_ka, self.agk_out[gi], 512, hr * 128, self.ag2_in, h * 128),
                    (self.s_va, self.agv_out[gi], 512, hr * 128, self.ag2_in, 2048 + h * 128))
        for dst_t, agout, stride, row, agin, irow in plan:
            for sgc in range(16):
                rho, sig = _SC_MAP[sgc]
                src = agout[rho * stride + row: rho * stride + row + 128,
                            sig * 512:(sig + 1) * 512]
                i = nc.sync.dma_start(
                    out=dst_t[:, buf, 4 * sgc:4 * sgc + 4, :],
                    in_=src.rearrange("p (c m) -> p c m", m=128))
                P.inc(i, "ka%d" % buf, 16)
            for u in range(4):
                src = agin[irow:irow + 128, u * 512:(u + 1) * 512]
                i = nc.sync.dma_start(
                    out=dst_t[:, buf, 64 + 4 * u:64 + 4 * u + 4, :],
                    in_=src.rearrange("p (c m) -> p c m", m=128))
                P.inc(i, "ka%d" % buf, 16)
        return P.ev("ka%d" % buf)

    def _emit_transpose_quad(self, buf, q):
        nc, P = self.nc, self.P
        P.wait("tensor", self._psT_free)
        P.wait("tensor", self._vt_guard[buf])
        P.wait("tensor", P.ev("ka%d" % buf))
        last = None
        with nc.allow_low_precision(reason="bf16 transpose staging, no accumulation"):
            for d in range(4):
                last = nc.tensor.transpose(self.psT[:, d, :], self.s_va[:, buf, 4 * q + d, :],
                                           self.c_ident[:, :])
        ev_t = P.inc(last, "mm")
        P.wait("vector", ev_t)
        i = nc.vector.tensor_copy(
            self.s_vt[:, buf, 4 * q:4 * q + 4, :],
            self.psT[:, 0:4, :])
        self._psT_free = P.inc(i, "dv")

    def flash_layer(self, layer):
        nc, P = self.nc, self.P
        P.wait("tensor", self.ev_consts)
        P.wait("sync", P.ev("ls"))
        if layer == "b":
            i = nc.sync.dma_start(out=self.c_biasb[:], in_=self.d_in["biasb"])
            P.inc(i, "lx", 16)
        self._bias_ev = P.ev("lx")
        fb = FB_A if layer == "a" else FB_B
        qsrc = self.qa_d if layer == "a" else self.qb_d
        aod = self.aoa_d if layer == "a" else self.aob_d
        btab = self.c_biasa if layer == "a" else self.c_biasb
        qrot = Rot(4)
        ptrot = Rot(4)
        zrot = Rot(4)
        rbrot = Rot(2)
        aorot = Rot(4)
        by_head = defaultdict(list)
        for b in fb:
            by_head[b["h"]].append(b)

        def load_q(b):
            slot, evfree = qrot.acquire()
            P.wait("sync", evfree)
            i = nc.sync.dma_start(
                out=self.s_q[:, slot, 0:b["W"]],
                in_=qsrc[:, b["h"], b["u"] * 512 + b["i"] * b["W"]:
                         b["u"] * 512 + (b["i"] + 1) * b["W"]])
            P.inc(i, "lq%d" % slot, 16)
            return slot

        if layer == "a":
            self._span_loads("a", 0, 0, guard=self._vt_guard[0])
            for q in range(20):
                self._emit_transpose_quad(0, q)
            head_buf = {h: 0 for h in range(NH)}
        else:
            self._span_loads("b", 0, 0, guard=self._vt_guard[0])
            for q in range(20):
                self._emit_transpose_quad(0, q)
            head_buf = {h: h % 2 for h in range(NH)}

        pend_tq = []
        for h in range(NH):
            buf = head_buf[h]
            pieces = by_head[h]
            qslots = {}
            for idx in range(min(2, len(pieces))):
                qslots[idx] = load_q(pieces[idx])
            if layer == "b" and h + 1 < NH:
                self._span_loads("b", h + 1, (h + 1) % 2,
                                 guard=self._vt_guard[(h + 1) % 2])
                pend_tq = [((h + 1) % 2, q) for q in range(20)]
            nq = max(1, (len(pend_tq) + len(pieces) - 1) // max(1, len(pieces)))
            for idx, b in enumerate(pieces):
                qslot = qslots.pop(idx)
                if idx + 2 < len(pieces):
                    qslots[idx + 2] = load_q(pieces[idx + 2])
                self._flash_piece(layer, b, buf, qslot, qrot, ptrot, zrot, rbrot,
                                  aorot, btab, aod)
                if layer == "a" and getattr(self, "_pend_conv", None):
                    self._emit_conv_chunks(2)
                for _ in range(nq):
                    if pend_tq:
                        tbuf, tq = pend_tq.pop(0)
                        self._emit_transpose_quad(tbuf, tq)
            while pend_tq:
                tbuf, tq = pend_tq.pop(0)
                self._emit_transpose_quad(tbuf, tq)
            self._vt_guard[buf] = P.ev("mm")

    def _flash_piece(self, layer, b, buf, qslot, qrot, ptrot, zrot, rbrot, aorot,
                     btab, aod):
        nc, P = self.nc, self.P
        W = b["W"]
        n_all = b["npos"] + b["ndiag"]
        wdiv = W // 128
        sO, evO = self.rB.acquire()
        P.wait("tensor", [evO, self._psZ_free, P.ev("ka%d" % buf),
                          P.ev("lq%d" % qslot)])

        def chunk_info(ci):
            if ci < b["npos"]:
                return ci, b["bias0"] + ci, None
            dd = ci - b["npos"]
            return (64 + 4 * b["u"] + b["i"] * wdiv + dd,
                    b["bias0"] + b["npos"] + dd, dd)

        pairs = [list(range(p0, min(p0 + 2, n_all))) for p0 in range(0, n_all, 2)]
        state = []
        for cis in pairs:
            sas = []
            last = None
            for ci in cis:
                pos, bias_c, diag = chunk_info(ci)
                sA, evA = self.rA.acquire()
                P.wait("tensor", evA)
                last = nc.tensor.matmul(self.psA[:, sA, 0:W], self.s_ka[:, buf, pos, :],
                                        self.s_q[:, qslot, 0:W], start=True, stop=True)
                sas.append(sA)
            ev_qk = P.inc(last, "mm")
            pts = []
            last = None
            for k2, ci in enumerate(cis):
                pos, bias_c, diag = chunk_info(ci)
                pt, evp = ptrot.acquire()
                P.wait("scalar", [ev_qk, evp, self._bias_ev])
                last = nc.scalar.activation(self.s_pt[:, pt, 0:W], self.psA[:, sas[k2], 0:W],
                                            AF.Exp, bias=btab[:, bias_c:bias_c + 1],
                                            scale=FP(1.0))
                pts.append(pt)
            ev_exp = P.inc(last, "ac")
            for sA in sas:
                self.rA.release(sA, ev_exp)
            ev_ready = ev_exp
            for k2, ci in enumerate(cis):
                pos, bias_c, diag = chunk_info(ci)
                if diag is not None:
                    mo = MASK_OFF[(W, diag)]
                    P.wait("vector", ev_exp)
                    i = nc.vector.tensor_mul(self.s_pt[:, pts[k2], 0:W],
                                             self.s_pt[:, pts[k2], 0:W],
                                             self.c_masks[:, mo:mo + W])
                    ev_ready = P.inc(i, "dv")
            state.append((cis, pts, ev_ready))
            if len(state) > 1:
                self._flash_zav_pair(state.pop(0), b, buf, sO, W, n_all, ptrot)
        while state:
            self._flash_zav_pair(state.pop(0), b, buf, sO, W, n_all, ptrot)
        ev_mm = P.ev("mm")
        zs, evz = zrot.acquire()
        P.wait("vector", [ev_mm, evz])
        with nc.allow_low_precision(reason="softmax normalizer to bf16 for broadcast"):
            i = nc.vector.reciprocal(self.s_zr[0:1, zs, 0:W], self.psZ[0:1, 0, 0:W])
        ev_zr = P.inc(i, "dv")
        self._psZ_free = ev_zr
        sA, evA = self.rA.acquire()
        P.wait("tensor", [ev_zr, evA])
        i = nc.tensor.matmul(self.psA[:, sA, 0:W], self.c_ones[0:1, :],
                             self.s_zr[0:1, zs, 0:W], start=True, stop=True)
        ev_bc = P.inc(i, "mm")
        zrot.release(zs, ev_bc)
        rb, evrb = rbrot.acquire()
        P.wait("vector", [ev_bc, evrb])
        i = nc.vector.tensor_copy(self.s_rbb[:, rb, 0:W], self.psA[:, sA, 0:W])
        ev_rbc = P.inc(i, "dv")
        self.rA.release(sA, ev_rbc)
        ao, evao = aorot.acquire()
        P.wait("vector", evao)
        i = nc.vector.tensor_mul(self.s_aos[:, ao, 0:W], self.psB[:, sO, 0:W],
                                 self.s_rbb[:, rb, 0:W])
        ev_n = P.inc(i, "dv")
        rbrot.release(rb, ev_n)
        self.rB.release(sO, ev_n)
        qrot.release(qslot, P.ev("mm"))
        P.wait("scalar", ev_n)
        c0 = b["u"] * 512 + b["i"] * W
        i = nc.scalar.dma_start(out=aod[:, b["h"], c0:c0 + W],
                                in_=self.s_aos[:, ao, 0:W])
        ev_st = P.inc(i, "ls", 16)
        aorot.release(ao, ev_st)

    def _flash_zav_pair(self, st, b, buf, sO, W, n_all, ptrot):
        nc, P = self.nc, self.P
        cis, pts, ev_ready = st
        P.wait("tensor", ev_ready)
        i = None
        for k2, ci in enumerate(cis):
            if ci < b["npos"]:
                pos = ci
            else:
                pos = 64 + 4 * b["u"] + b["i"] * (W // 128) + (ci - b["npos"])
            nc.tensor.matmul(self.psZ[0:1, 0, 0:W], self.c_ones[:, 0:1],
                             self.s_pt[:, pts[k2], 0:W], start=(ci == 0),
                             stop=(ci == n_all - 1))
            i = nc.tensor.matmul(self.psB[:, sO, 0:W], self.s_vt[:, buf, pos, :],
                                 self.s_pt[:, pts[k2], 0:W], start=(ci == 0),
                                 stop=(ci == n_all - 1))
        ev_av = P.inc(i, "mm")
        for pt in pts:
            ptrot.release(pt, ev_av)

    def _ln_stats_begin(self, pool):
        P = self.P
        self._sq_free = [[P.ev("mm"), P.ev("dv")]] * 4
        if pool == "B":
            sl, ev = self.rB.acquire()
            ap = self.psB[0:1, sl, :]
        else:
            sl, ev = self.rA.acquire()
            ap = self.psA[0:1, sl, :]
        P.wait("tensor", [ev, self._psZ_free])
        return (pool, sl, ap)

    def _ln_stats_finish(self, hnd):
        nc, P = self.nc, self.P
        pool, sl, ap = hnd
        ev_mm = P.ev("mm")
        P.wait("vector", ev_mm)
        nc.vector.tensor_copy(self.s_st[:, 0, :], self.psZ[0:1, 0, :])
        i = nc.vector.tensor_copy(self.s_st[:, 1, :], ap)
        ev_stats = P.inc(i, "dv")
        (self.rB if pool == "B" else self.rA).release(sl, ev_stats)
        self._psZ_free = ev_stats
        return ev_stats

    def _ln_stat_chunk2(self, c, hnd, f32_ap, ev_ready):
        nc, P = self.nc, self.P
        pool, sl, ap = hnd
        P.wait("vector", ev_ready)
        i = nc.vector.tensor_copy(self.s_xb[:, c, :], f32_ap)
        ev_cast = P.inc(i, "dv")
        qslot = c % 4
        P.wait("scalar", [ev_ready, self._sq_free[qslot]])
        i = nc.scalar.activation(self.s_t1[:, qslot, :], f32_ap, AF.Square)
        ev_sq = P.inc(i, "ac")
        P.wait("tensor", ev_cast)
        nc.tensor.matmul(self.psZ[0:1, 0, :], self.c_ones[:, 0:1], self.s_xb[:, c, :],
                         start=(c == 0), stop=(c == 15))
        P.wait("tensor", ev_sq)
        i = nc.tensor.matmul(ap, self.c_ones[:, 0:1],
                             self.s_t1[:, qslot, :], start=(c == 0), stop=(c == 15))
        ev_mm = P.inc(i, "mm")
        self._sq_free[qslot] = ev_mm
        return [ev_cast, ev_sq]

    def phase_mid(self, layer):
        nc, P = self.nc, self.P
        for t in range(NT):
            ts = slice(t * 512, (t + 1) * 512)
            self._mid_tile(layer, t, ts)

    def _mid_tile(self, layer, t, ts):
        nc, P = self.nc, self.P
        ao_d = self.aoa_d if layer == 1 else self.aob_d
        res_src = self.d_in["x"] if layer == 1 else self.x2_d
        wname = "wfc" if layer == 1 else "wo"
        P.wait("sync", self._ao_free)
        P.wait("sync", P.ev("ls"))
        i = nc.sync.dma_start(out=self.s_ao[:, :, :], in_=ao_d[:, :, ts])
        P.inc(i, "lx", 16)
        ev_ao = P.ev("lx")
        hnd = self._ln_stats_begin("A")
        xin_rot = Rot(2)

        def evac(idx, m, ps, ev_mm, ts=ts, layer=layer):
            xs, evx = xin_rot.acquire()
            P.wait("sync", evx)
            P.wait("sync", P.ev("ls"))
            i = nc.sync.dma_start(out=self.s_xc[:, xs, :], in_=res_src[:, m, ts])
            P.inc(i, "lx", 16)
            P.wait("vector", [ev_mm, self._xout_free])
            P.wait("vector", P.ev("lx"))
            if layer == 1:
                i = nc.vector.scalar_tensor_tensor(
                    self.s_xc[:, 3, :], ps, self.c_bfc[:, m:m + 1],
                    self.s_xc[:, xs, :], ALU.add, ALU.add)
            else:
                i = nc.vector.scalar_tensor_tensor(
                    self.s_xc[:, 3, :], ps, 1.0,
                    self.s_xc[:, xs, :], ALU.mult, ALU.add)
            ev_x1 = P.inc(i, "dv")
            xin_rot.release(xs, ev_x1)
            rel = self._ln_stat_chunk2(m, hnd, self.s_xc[:, 3, :], ev_x1)
            P.wait("scalar", ev_x1)
            i = nc.scalar.dma_start(out=self.xr_d[:, m, ts], in_=self.s_xc[:, 3, :])
            ev_sp = P.inc(i, "ls", 16)
            self._xout_free = [rel[0], rel[1], ev_sp]
            return ev_x1

        self.proj(wname, lambda k: self.s_ao[:, k, :], list(range(16)), evac,
                  extra_rhs_ev=ev_ao)
        ev_stats = self._ln_stats_finish(hnd)
        ev_y = self.ln_apply(ev_stats, self._y_free)
        self._ffn_tile(t, ts, ev_y, layer)
        if layer == 1:
            ev_stats2 = self._ln_stats_finish(self._hnd2)
            ev_y2 = self.ln_apply(ev_stats2, [P.ev("mm"), P.ev("dv")])
            for wn, kind in (("wq", "q"), ("wk", "k"), ("wv", "v")):
                evr = Rot(3)

                def evac3(idx, m, ps, ev_mm, wn=wn, kind=kind, ts=ts, evr=evr):
                    slot, evfree = evr.acquire()
                    P.wait("scalar", [ev_mm, evfree])
                    sc = INV_SQRT_HD if kind == "q" else FP(1.0)
                    i = nc.scalar.activation(self.s_ev[:, slot, :], ps, AF.Identity,
                                             scale=sc)
                    ev_a = P.inc(i, "ac")
                    P.wait("scalar", ev_a)
                    if kind == "q":
                        dst = self.qb_d[:, m, ts]
                    elif kind == "k":
                        dst = self.ag2_in[m * 128:(m + 1) * 128, ts]
                    else:
                        dst = self.ag2_in[2048 + m * 128:2048 + (m + 1) * 128, ts]
                    i = nc.scalar.dma_start(out=dst, in_=self.s_ev[:, slot, :])
                    ev_st = P.inc(i, "ls", 16)
                    evr.release(slot, ev_st)
                    return ev_a

                P.wait("tensor", ev_y2)
                self.proj(wn, lambda k: self.s_y[:, k, :], list(range(16)), evac3)
            self._y_free = P.ev("mm")
        else:
            self._y_free = P.ev("mm")

    def _ffn_tile(self, t, ts, ev_y, layer):
        nc, P = self.nc, self.P
        P.wait("tensor", ev_y)
        wrot = self.wrot
        t1rot = Rot(4)
        t1rot.free_ev = [[P.ev("mm"), P.ev("dv")]] * 4

        mid_guard = self._mid_free
        pend = {}
        wload4 = self.wload

        for bi in range(2):
            for wi, n in enumerate(("sw1", "sw2")):
                slot, evf = wrot.acquire()
                pend[(bi, wi)] = (slot, wload4(n, bi, slot, evf))
        nxt = 2
        for bi in range(32):
            s1, ev1 = pend.pop((bi, 0))
            s2, ev2 = pend.pop((bi, 1))
            wv1 = self.s_w[:, s1, :].rearrange("p (k m) -> p k m", m=256)
            wv2 = self.s_w[:, s2, :].rearrange("p (k m) -> p k m", m=256)
            for c2 in range(2):
                hc = bi * 2 + c2
                sg, evg = self.rA.acquire()
                su, evu = self.rA.acquire()
                P.wait("tensor", [ev1, ev2, evg, evu])
                for k in range(16):
                    nc.tensor.matmul(self.psA[:, sg, :], wv1[:, k, 128 * c2:128 * c2 + 128],
                                     self.s_y[:, k, :], start=(k == 0), stop=(k == 15))
                last = None
                for k in range(16):
                    last = nc.tensor.matmul(self.psA[:, su, :], wv2[:, k, 128 * c2:128 * c2 + 128],
                                            self.s_y[:, k, :], start=(k == 0), stop=(k == 15))
                ev_u = P.inc(last, "mm")
                tslot, evt = t1rot.acquire()
                P.wait("scalar", [ev_u, evt])
                i = nc.scalar.activation(self.s_t1[:, tslot, :], self.psA[:, sg, :], AF.Silu)
                ev_si = P.inc(i, "ac")
                P.wait("vector", [ev_si, mid_guard])
                if hc == 0:
                    nc.vector.tensor_copy(self.s_dbg2[:, 0, :], self.psA[:, sg, :])
                    nc.vector.tensor_copy(self.s_dbg2[:, 1, :], self.psA[:, su, :])
                    nc.vector.tensor_copy(self.s_dbg2[:, 2, :], self.s_t1[:, tslot, :])
                    nc.vector.tensor_copy(self.s_dbg2[:, 3, :], self.s_y[:, 0, :])
                self.rA.release(sg, ev_si)
                i = nc.vector.tensor_mul(self.s_mid[:, hc, :], self.s_t1[:, tslot, :],
                                         self.psA[:, su, :])
                ev_m = P.inc(i, "dv")
                self.rA.release(su, ev_m)
                t1rot.release(tslot, ev_m)
            ev_wdone = P.ev("mm")
            wrot.release(s1, ev_wdone)
            wrot.release(s2, ev_wdone)
            if nxt < 32:
                for wi, n in enumerate(("sw1", "sw2")):
                    slot, evf = wrot.acquire()
                    pend[(nxt, wi)] = (slot, wload4(n, nxt, slot, evf))
                nxt += 1

        self._router_gating(ev_y)

        ev_mid_a = P.ev("dv")
        pend = {}
        for ii in range(2):
            slot, evf = wrot.acquire()
            pend[ii] = (slot, wload4("sw3", ii, slot, evf))
        nxt = 2
        for m in range(16):
            outs = []
            for half in range(2):
                ii = m * 2 + half
                slot, ev_w = pend.pop(ii)
                wv = self.s_w[:, slot, :].rearrange("p (k m) -> p k m", m=128)
                if half == 0:
                    sB, evB = self.rB.acquire()
                    P.wait("tensor", [evB, ev_mid_a])
                P.wait("tensor", ev_w)
                last = None
                for kc in range(32):
                    last = nc.tensor.matmul(self.psB[:, sB, :], wv[:, kc, :],
                                            self.s_mid[:, half * 32 + kc, :],
                                            start=(half == 0 and kc == 0),
                                            stop=(half == 1 and kc == 31))
                ev_mm = P.inc(last, "mm")
                wrot.release(slot, ev_mm)
                if nxt < 32:
                    slot2, evf2 = wrot.acquire()
                    pend[nxt] = (slot2, wload4("sw3", nxt, slot2, evf2))
                    nxt += 1
            P.wait("vector", [ev_mm, self._oacc_free])
            i = nc.vector.tensor_copy(self.s_oacc[:, m, :], self.psB[:, sB, :])
            ev_oc = P.inc(i, "dv")
            self.rB.release(sB, ev_oc)

        ev_mid_rd = P.ev("mm")
        pend = {}
        order = []
        for e in range(8):
            for bp in range(4):
                order.append((e, bp))
        for ii in range(2):
            e, bp = order[ii]
            sl1, evf1 = wrot.acquire()
            sl2, evf2 = wrot.acquire()
            pend[ii] = ((sl1, wload4("w12", e * 8 + bp, sl1, evf1)),
                        (sl2, wload4("w12", e * 8 + 4 + bp, sl2, evf2)))
        nxt = 2
        for ii, (e, bp) in enumerate(order):
            (sl1, ev1), (sl2, ev2) = pend.pop(ii)
            wv1 = self.s_w[:, sl1, :].rearrange("p (k m) -> p k m", m=256)
            wv2 = self.s_w[:, sl2, :].rearrange("p (k m) -> p k m", m=256)
            for c2 in range(2):
                hc = 2 * bp + c2
                s1a, ev1a = self.rA.acquire()
                s2a, ev2a = self.rA.acquire()
                P.wait("tensor", [ev1, ev2, ev1a, ev2a])
                for k in range(16):
                    nc.tensor.matmul(self.psA[:, s1a, :], wv1[:, k, 128 * c2:128 * c2 + 128],
                                     self.s_y[:, k, :], start=(k == 0), stop=(k == 15))
                last = None
                for k in range(16):
                    last = nc.tensor.matmul(self.psA[:, s2a, :], wv2[:, k, 128 * c2:128 * c2 + 128],
                                            self.s_y[:, k, :], start=(k == 0), stop=(k == 15))
                ev_u = P.inc(last, "mm")
                tslot, evt = t1rot.acquire()
                P.wait("scalar", [ev_u, evt])
                i = nc.scalar.activation(self.s_t1[:, tslot, :], self.psA[:, s1a, :], AF.Silu)
                ev_si = P.inc(i, "ac")
                self.rA.release(s1a, ev_si)
                t2slot, evt2 = t1rot.acquire()
                P.wait("vector", [ev_si, evt2, ev_mid_rd, self._gb_ready])
                i = nc.vector.tensor_mul(self.s_t1[:, t2slot, :], self.s_t1[:, tslot, :],
                                         self.psA[:, s2a, :])
                i = nc.vector.tensor_mul(self.s_mid[:, e * 8 + hc, :],
                                         self.s_t1[:, t2slot, :], self.s_gb[:, e, :])
                ev_m = P.inc(i, "dv")
                self.rA.release(s2a, ev_m)
                t1rot.release(tslot, ev_m)
                t1rot.release(t2slot, ev_m)
            ev_wdone = P.ev("mm")
            wrot.release(sl1, ev_wdone)
            wrot.release(sl2, ev_wdone)
            if nxt < 32:
                e2, bp2 = order[nxt]
                sl1, evf1 = wrot.acquire()
                sl2, evf2 = wrot.acquire()
                pend[nxt] = ((sl1, wload4("w12", e2 * 8 + bp2, sl1, evf1)),
                             (sl2, wload4("w12", e2 * 8 + 4 + bp2, sl2, evf2)))
                nxt += 1

        if layer == 1:
            self._hnd2 = self._ln_stats_begin("A")
        ev_mid_d = P.ev("dv")
        pend = {}
        for ii in range(2):
            slot, evf = wrot.acquire()
            pend[ii] = (slot, wload4("w3", ii, slot, evf))
        nxt = 2
        xin_rot = Rot(2)
        for m in range(16):
            for half in range(2):
                ii = m * 2 + half
                slot, ev_w = pend.pop(ii)
                wv = self.s_w[:, slot, :].rearrange("p (k m) -> p k m", m=128)
                if half == 0:
                    sB, evB = self.rB.acquire()
                    P.wait("tensor", [evB, ev_mid_d])
                P.wait("tensor", ev_w)
                last = None
                for idx2 in range(32):
                    e4, kc = idx2 // 8, idx2 % 8
                    e = half * 4 + e4
                    last = nc.tensor.matmul(self.psB[:, sB, :], wv[:, idx2, :],
                                            self.s_mid[:, e * 8 + kc, :],
                                            start=(half == 0 and idx2 == 0),
                                            stop=(half == 1 and idx2 == 31))
                ev_mm = P.inc(last, "mm")
                wrot.release(slot, ev_mm)
                if nxt < 32:
                    slot2, evf2 = wrot.acquire()
                    pend[nxt] = (slot2, wload4("w3", nxt, slot2, evf2))
                    nxt += 1
            xs, evx = xin_rot.acquire()
            P.wait("sync", evx)
            P.wait("sync", P.ev("ls"))
            i = nc.sync.dma_start(out=self.s_xc[:, xs, :], in_=self.xr_d[:, m, ts])
            P.inc(i, "lx", 16)
            P.wait("vector", [ev_mm, P.ev("lx"), self._xout_free])
            i = nc.vector.scalar_tensor_tensor(
                self.s_xc[:, 2, :], self.psB[:, sB, :], 1.0,
                self.s_oacc[:, m, :], ALU.mult, ALU.add)
            i = nc.vector.tensor_add(self.s_xc[:, 3, :], self.s_xc[:, 2, :],
                                     self.s_xc[:, xs, :])
            ev_x2 = P.inc(i, "dv")
            self.rB.release(sB, ev_x2)
            xin_rot.release(xs, ev_x2)
            relx = [ev_x2]
            if layer == 1:
                relx = self._ln_stat_chunk2(m, self._hnd2, self.s_xc[:, 3, :], ev_x2)
            P.wait("scalar", ev_x2)
            dst = self.x2_d[:, m, ts] if layer == 1 else self.out[:, m, ts]
            i = nc.scalar.dma_start(out=dst, in_=self.s_xc[:, 3, :])
            ev_sp = P.inc(i, "ls", 16)
            self._xout_free = relx + [ev_sp]
        self._oacc_free = [P.ev("mm"), P.ev("dv")]
        self._mid_free = P.ev("mm")
        self._ao_free = [P.ev("mm"), P.ev("dv")]

    def _router_gating(self, ev_y):
        nc, P = self.nc, self.P
        P.wait("tensor", [self._psZ_free, self.ev_router, ev_y])
        last = None
        for k in range(16):
            last = nc.tensor.matmul(self.psZ[0:8, 0, :], self.c_router[:, k * 8:(k + 1) * 8],
                                    self.s_y[:, k, :], start=(k == 0), stop=(k == 15))
        ev_mm = P.inc(last, "mm")
        P.wait("scalar", ev_mm)
        i = nc.scalar.activation(self.s_gr[:, 1, :], self.psZ[0:8, 0, :], AF.Tanh,
                                 scale=FP(0.5))
        ev_t = P.inc(i, "ac")
        P.wait("vector", ev_t)
        i = nc.vector.tensor_scalar(self.s_gr[:, 0, :], self.s_gr[:, 1, :], 0.5, 0.5,
                                    ALU.mult, ALU.add)
        ev_sc = P.inc(i, "dv")
        self._psZ_free = ev_sc
        i = nc.vector.tensor_scalar_add(self.s_gr[:, 1, :], self.s_gr[:, 0, :],
                                        self.c_ebias[:, 0:1])
        nc.vector.tensor_copy(self.s_gr16[:, 0, :], self.s_gr[:, 0, :])
        i = nc.vector.tensor_copy(self.s_gr16[:, 1, :], self.s_gr[:, 1, :])
        ev_16 = P.inc(i, "dv")
        P.wait("tensor", [ev_16, self._psT_free])
        with nc.allow_low_precision(reason="router transpose staging"):
            for d in range(4):
                nc.tensor.transpose(self.psT[:, d, 0:8], self.s_gr16[0:8, 1, 128 * d:128 * (d + 1)],
                                    self.c_ident[0:8, 0:8])
            last = None
            for d in range(4):
                last = nc.tensor.transpose(self.psT[:, 4 + d, 0:8],
                                           self.s_gr16[0:8, 0, 128 * d:128 * (d + 1)],
                                           self.c_ident[0:8, 0:8])
        ev_tr = P.inc(last, "mm")
        P.wait("vector", ev_tr)
        i = nc.vector.tensor_copy(self.s_gtm16[:, 0:8, :], self.psT[:, 0:8, 0:8])
        ev_tm = P.inc(i, "dv")
        self._psT_free = ev_tm
        nc.vector.tensor_copy(self.s_dbg[:, 0:8, :], self.s_gtm16[:, 0:8, :])
        g = self.s_gtm

        def _sync(i):
            ev = P.inc(i, "dv")
            P.wait("vector", ev)
            return ev

        for d in range(4):
            sel = self.s_gtm16[:, d, :]
            sco = self.s_gtm16[:, 4 + d, :]
            _sync(nc.vector.reduce_max(g[:, 0, 0:1], sel, AX.X))
            _sync(nc.vector.tensor_scalar(g[:, 1, :], sel, g[:, 0, 0:1], None, ALU.is_ge))
            _sync(nc.vector.scalar_tensor_tensor(g[:, 2, :], g[:, 1, :], -1e30, sel,
                                                 ALU.mult, ALU.add))
            _sync(nc.vector.reduce_max(g[:, 3, 0:1], g[:, 2, :], AX.X))
            _sync(nc.vector.tensor_scalar(g[:, 4, :], sel, g[:, 3, 0:1], None, ALU.is_ge))
            _sync(nc.vector.tensor_mul(g[:, 5, :], g[:, 4, :], sco))
            _sync(nc.vector.reduce_sum(g[:, 6, 0:1], g[:, 5, :], AX.X))
            _sync(nc.vector.tensor_scalar_add(g[:, 6, 0:1], g[:, 6, 0:1], 1e-9))
            _sync(nc.vector.reciprocal(g[:, 7, 0:1], g[:, 6, 0:1]))
            ev_g = _sync(nc.vector.tensor_scalar(self.s_gtm16[:, d, :], g[:, 5, :],
                                                 g[:, 7, 0:1], None, ALU.mult))
            if d == 0:
                _sync(nc.vector.tensor_copy(self.s_dbg[:, 8:16, :], g[:, :, :]))
        nc.vector.tensor_copy(self.s_dbg[:, 8:12, :], self.s_gtm16[:, 0:4, :])
        P.wait("tensor", [ev_g, self._psT_free])
        with nc.allow_low_precision(reason="router transpose staging"):
            last = None
            for d in range(4):
                last = nc.tensor.transpose(self.psT[0:8, d, :],
                                           self.s_gtm16[:, d, :],
                                           self.c_ident[:, :])
        ev_tr2 = P.inc(last, "mm")
        P.wait("vector", ev_tr2)
        last = None
        for d in range(4):
            last = nc.vector.tensor_copy(self.s_gf[0:8, 128 * d:128 * (d + 1)],
                                         self.psT[0:8, d, 0:128])
        ev_gf = P.inc(last, "dv")
        self._psT_free = ev_gf
        for e in range(8):
            sA, evA = self.rA.acquire()
            P.wait("tensor", [ev_gf, evA])
            i = nc.tensor.matmul(self.psA[:, sA, :], self.c_onehot[0:8, e * 128:(e + 1) * 128],
                                 self.s_gf[0:8, :], start=True, stop=True)
            ev_b = P.inc(i, "mm")
            P.wait("vector", [ev_b, self._gb_free])
            i = nc.vector.tensor_copy(self.s_gb[:, e, :], self.psA[:, sA, :])
            ev_c = P.inc(i, "dv")
            self.rA.release(sA, ev_c)
        self._gb_ready = P.ev("dv")
        self._gb_free = None

    def allgather(self, which):
        nc, P = self.nc, self.P
        groups = [[0, 1, 2, 3], [4, 5, 6, 7]]
        P.wait("gpsimd", P.ev("ls"))
        if which == 1:
            i = nc.gpsimd.collective_compute(
                "AllGather", ALU.bypass, replica_groups=groups,
                ins=[self.ag1_in.opt()], outs=[self.ag1_out.opt()])
            return P.inc(i, "ag", 1)
        self.ev_ag2g = []
        for gi in range(4):
            rs = slice(512 * gi, 512 * (gi + 1))
            i = nc.gpsimd.collective_compute(
                "AllGather", ALU.bypass, replica_groups=groups,
                ins=[self.ag2_in[rs, :].opt()], outs=[self.agk_out[gi].opt()])
            P.inc(i, "ag", 1)
            rs2 = slice(2048 + 512 * gi, 2048 + 512 * (gi + 1))
            i = nc.gpsimd.collective_compute(
                "AllGather", ALU.bypass, replica_groups=groups,
                ins=[self.ag2_in[rs2, :].opt()], outs=[self.agv_out[gi].opt()])
            ev = P.inc(i, "ag", 1)
            self.ev_ag2g.append(ev)
        return self.ev_ag2g[-1]

    def tap(self, name, src, shape, dt):
        self.taps.append((name, src, shape, dt))

    def finish(self):
        nc, P = self.nc, self.P
        if getattr(self, "_pend_conv", None):
            self._emit_conv_chunks(10000)
        for sname in list(P.sems):
            if sname != "z":
                P.wait("gpsimd", P.ev(sname))
        for name, src, shape, dt in self.taps:
            o = nc.dram_tensor("tap_" + name, shape, dt, kind="ExternalOutput").ap()
            i = nc.gpsimd.dma_start(out=o, in_=src)
            P.inc(i, "lg", 16)
        P.wait("gpsimd", P.ev("lg"))
        for sh in P.sems.values():
            nc.gpsimd.sem_clear(sh)
        self.ctx.close()
        return self.nc

    def build(self):
        self._materialize_sbuf()
        self.load_consts()
        self.ev_conv = {}
        self.convert_weights(["wqkv", "router"])
        nc, P = self.nc, self.P
        P.wait("sync", self.ev_conv["router"])
        i = nc.sync.dma_start(out=self.c_router[:], in_=self.wb["router"][0])
        P.inc(i, "lx", 16)
        self.ev_router = P.ev("lx")
        self.phase0()
        ev_ag1 = self.allgather(1)
        self._pend_conv = []
        self.convert_weights(["wfc", "sw1", "sw2", "sw3", "w12", "w3",
                              "wq", "wk", "wv", "wo"], defer=True)
        if self.upto == "p0":
            self.tap("qa", self.qa_d, [128, 16, 2048], BF16)
            self.tap("ag1", self.ag1_out, [1024, 2048], BF16)
            return self.finish()
        P = self.P
        for e in ("sync", "tensor", "scalar", "vector"):
            P.wait(e, ev_ag1)
        self.flash_layer("a")
        self._emit_conv_chunks(10000)
        if self.upto == "p1a":
            self.tap("aoa", self.aoa_d, [128, 16, 2048], BF16)
            return self.finish()
        self.phase_mid(1)
        if self.upto == "p1b":
            self.tap("x2", self.x2_d, [128, 16, 2048], F32)
            self.tap("qb", self.qb_d, [128, 16, 2048], BF16)
            self.tap("ag2in", self.ag2_in, [4096, 2048], BF16)
            self.tap("xr", self.xr_d, [128, 16, 2048], F32)
            self.tap("mid", self.s_mid[:, :, :], [128, 64, 512], BF16)
            self.tap("oacc", self.s_oacc[:, :, :], [128, 16, 512], BF16)
            self.tap("gb", self.s_gb[:, :, :], [128, 8, 512], BF16)
            self.tap("ylast", self.s_y[:, :, :], [128, 16, 512], BF16)
            self.tap("gf", self.s_gf[:, :], [8, 512], BF16)
            self.tap("gtm16", self.s_gtm16[:, :, :], [128, 8, 8], BF16)
            self.tap("gr", self.s_gr[:, :, :], [8, 2, 512], F32)
            self.tap("gr16", self.s_gr16[:, :, :], [8, 2, 512], BF16)
            self.tap("dbg", self.s_dbg[:, :, :], [128, 16, 8], BF16)
            self.tap("dbg2", self.s_dbg2[:, :, :], [128, 4, 512], BF16)
            self.tap("wsw1b0", self.wb["sw1"][0:1], [1, 128, 4096], BF16)
            self.tap("wsw3b0", self.wb["sw3"][0:1], [1, 128, 4096], BF16)
            return self.finish()
        self.allgather(2)
        self.flash_layer("b")
        if self.upto == "p2a":
            self.tap("aob", self.aob_d, [128, 16, 2048], BF16)
            return self.finish()
        self.phase_mid(2)
        return self.finish()


_CACHE = {}

def _get_nc(upto):
    if upto not in _CACHE:
        _CACHE[upto] = Builder(upto=upto).build()
    return _CACHE[upto]


def kernel(**inputs):
    upto = os.environ.get("ATHENA_UPTO", "full")
    nc = _get_nc(upto)
    in_maps = make_in_maps(inputs)
    res = run_bass_kernel_spmd(nc, in_maps, core_ids=list(range(8)))
    return reassemble(res.results)


# version 05
# speedup vs baseline: 1.2034x; 1.0019x over previous
import os
from collections import defaultdict
from contextlib import ExitStack

import numpy as np
import ml_dtypes

import concourse.bass as bass
import concourse.mybir as mybir
from concourse.bass_utils import run_bass_kernel_spmd

F32 = mybir.dt.float32
BF16 = mybir.dt.bfloat16
AF = mybir.ActivationFunctionType
ALU = mybir.AluOpType
AX = mybir.AxisListType
BF = ml_dtypes.bfloat16

NCH = 16
NT = 4
NH = 16
KCE = [16, 32, 48, 64]
SLOPES = [0.5 ** ((i + 1) * 0.5) for i in range(NH)]
WH = [128, 256, 256] + [512] * 13
NEG = -1.0e9
INV_SQRT_HD = np.float32(1.0 / np.sqrt(128.0))
FP = np.float32

def sc_of(j):
    return [j, 7 - j, 8 + j, 15 - j]

_SC_MAP = {}
for _j in range(4):
    for _u, _s in enumerate(sc_of(_j)):
        _SC_MAP[_s] = (_j, _u)

def flash_blocks(layer):
    col = 0
    out = []
    acol = {}
    for h in range(NH):
        ws = 512 if layer == "a" else WH[h]
        for u in range(4):
            for i in range(512 // ws):
                npos, ndiag = KCE[u], ws // 128
                if layer == "a":
                    if (u, i) not in acol:
                        acol[(u, i)] = col
                        col += npos + ndiag
                    b0 = acol[(u, i)]
                else:
                    b0 = col
                    col += npos + ndiag
                out.append(dict(h=h, u=u, i=i, W=ws, npos=npos, ndiag=ndiag, bias0=b0))
    return out, col

FB_A, BIASA_COLS = flash_blocks("a")
FB_B, BIASB_COLS = flash_blocks("b")

MASK_OFF = {}
_c = 0
for _wd in ((512, 0), (512, 1), (512, 2), (512, 3), (256, 0), (256, 1), (128, 0)):
    MASK_OFF[_wd] = _c
    _c += _wd[0]
MASK_COLS = _c

WSPEC = {
    "wqkv": (18, 2048), "wfc": (16, 2048), "wq": (16, 2048), "wk": (16, 2048),
    "wv": (16, 2048), "wo": (16, 2048), "sw1": (32, 4096), "sw2": (32, 4096),
    "sw3": (32, 4096), "w12": (64, 4096), "w3": (32, 4096), "router": (1, 128),
}


def _mk_masks():
    m = np.zeros((128, MASK_COLS), dtype=BF)
    for (w, d), off in MASK_OFF.items():
        k = np.arange(128)[:, None]
        q = np.arange(w)[None, :]
        m[:, off:off + w] = (128 * d + k <= q).astype(BF)
    return m

def _mk_bias(layer, j):
    fb, ncols = (FB_A, BIASA_COLS) if layer == "a" else (FB_B, BIASB_COLS)
    tab = np.zeros((128, ncols), dtype=np.float32)
    p = np.arange(128, dtype=np.float32)
    for b in fb:
        sc = sc_of(j)[b["u"]]
        q0 = sc * 512 + b["i"] * b["W"]
        slope = 0.0 if layer == "a" else SLOPES[b["h"]]
        R = q0 + b["W"] / 2.0
        for kc in range(b["npos"]):
            c = b["bias0"] + kc
            if (kc + 1) * 128 <= q0:
                tab[:, c] = slope * (128.0 * kc + p - R)
            else:
                tab[:, c] = NEG
        for dd in range(b["ndiag"]):
            c = b["bias0"] + b["npos"] + dd
            kc = q0 // 128 + dd
            tab[:, c] = slope * (128.0 * kc + p - R)
    return tab

def _wblocks(Wt, kcs, mblk):
    K, M = Wt.shape
    nb = M // mblk
    a = Wt.reshape(kcs, 128, nb, mblk).transpose(2, 1, 0, 3)
    return np.ascontiguousarray(a.reshape(nb, 128, kcs * mblk), dtype=np.float32)

def _host_weight_arrays(inp):
    w = {}
    w["wqkv"] = _wblocks(np.asarray(inp["L0_attn_a_w_qkv"], np.float32), 16, 128)
    w["wfc"] = _wblocks(np.asarray(inp["L0_attn_a_w_fc"], np.float32), 16, 128)
    for n, k in (("wq", "L0_attn_b_Wq"), ("wk", "L0_attn_b_Wk"),
                 ("wv", "L0_attn_b_Wv"), ("wo", "L0_attn_b_Wo")):
        w[n] = _wblocks(np.ascontiguousarray(np.asarray(inp[k], np.float32).T), 16, 128)
    w["sw1"] = _wblocks(np.ascontiguousarray(np.asarray(inp["L0_ffn_sw1"], np.float32).T), 16, 256)
    w["sw2"] = _wblocks(np.ascontiguousarray(np.asarray(inp["L0_ffn_sw2"], np.float32).T), 16, 256)
    s3 = np.ascontiguousarray(np.asarray(inp["L0_ffn_sw3"], np.float32).T)
    a = s3.reshape(64, 128, 16, 128).transpose(2, 0, 1, 3)
    a = a.reshape(16, 2, 32, 128, 128).transpose(0, 1, 3, 2, 4)
    w["sw3"] = np.ascontiguousarray(a.reshape(32, 128, 32 * 128), dtype=np.float32)
    w12t = np.asarray(inp["L0_ffn_w12"], np.float32).transpose(0, 2, 1)
    a = w12t.reshape(8, 16, 128, 8, 256).transpose(0, 3, 2, 1, 4)
    w["w12"] = np.ascontiguousarray(a.reshape(64, 128, 16 * 256), dtype=np.float32)
    w3t = np.asarray(inp["L0_ffn_w3"], np.float32).transpose(0, 2, 1)
    a = w3t.reshape(8, 8, 128, 16, 128).transpose(3, 0, 1, 2, 4)
    a = a.reshape(16, 2, 4, 8, 128, 128).transpose(0, 1, 4, 2, 3, 5)
    w["w3"] = np.ascontiguousarray(a.reshape(32, 128, 4 * 8 * 128), dtype=np.float32)
    rt = np.ascontiguousarray(np.asarray(inp["L0_ffn_router_w"], np.float32).T)
    a = rt.reshape(16, 128, 1, 8).transpose(2, 1, 0, 3)
    w["router"] = np.ascontiguousarray(a.reshape(1, 128, 16 * 8), dtype=np.float32)
    return w

def _host_x(x0, core):
    g, j = core // 4, core % 4
    toks = np.concatenate([s * 512 + np.arange(512) for s in sc_of(j)])
    xs = np.asarray(x0, np.float32)[g][toks]
    arr = xs.T.reshape(16, 128, 2048).transpose(1, 0, 2)
    return np.ascontiguousarray(arr, dtype=np.float32)

def make_in_maps(inputs):
    w = _host_weight_arrays(inputs)
    masks = _mk_masks().astype(np.float32)
    ident = np.eye(128, dtype=np.float32)
    ones = np.ones((128, 128), dtype=np.float32)
    onehot = np.zeros((8, 1024), dtype=np.float32)
    for e in range(8):
        onehot[e, e * 128:(e + 1) * 128] = 1
    bqkv = np.ascontiguousarray(np.asarray(inputs["L0_attn_a_b_qkv"], np.float32).reshape(18, 128).T)
    bfc = np.ascontiguousarray(np.asarray(inputs["L0_attn_a_b_fc"], np.float32).reshape(16, 128).T)
    ebias = np.ascontiguousarray(np.asarray(inputs["L0_ffn_expert_bias"], np.float32).reshape(8, 1))
    in_maps = []
    for core in range(8):
        j = core % 4
        m = {
            "x_in": _host_x(inputs["L0_attn_a_x"], core),
            "bqkv": bqkv, "bfc": bfc, "ebias": ebias,
            "ident": ident, "onesin": ones, "onehot": onehot, "masks": masks,
            "biasa": _mk_bias("a", j), "biasb": _mk_bias("b", j),
        }
        for n in WSPEC:
            m["wf_" + n] = w[n]
        in_maps.append(m)
    return in_maps

def reassemble(results):
    out = np.empty((2, 8192, 2048), dtype=np.float32)
    for core in range(8):
        g, j = core // 4, core % 4
        toks = np.concatenate([s * 512 + np.arange(512) for s in sc_of(j)])
        shard = results[core]["out"]
        xs = shard.transpose(1, 0, 2).reshape(2048, 2048)
        out[g, toks, :] = xs.T
    return out


class Prog:
    def __init__(self, nc, ctx):
        self.nc = nc
        self.sems = {}
        self.cnt = defaultdict(int)
        self.lastw = defaultdict(dict)
        names = ["z", "lw0", "lw1", "lw2", "lw3", "lx", "lq0", "lq1", "lq2", "lq3",
                 "ka0", "ka1", "ls", "lg", "mm", "ac", "dv", "ag"]
        names += ["cv_" + n for n in WSPEC]
        for s in names:
            self.sems[s] = ctx.enter_context(nc.semaphore(s))
        self.dma_sems = {n for n in names if n not in ("mm", "ac", "dv", "ag")}
        self._fence_a = nc.dram_tensor("fence_a", [1, 16], F32, kind="Internal").ap()
        self._fence_b = nc.dram_tensor("fence_b", [1, 16], F32, kind="Internal").ap()
        mynums = sorted(h.num for h in self.sems.values())
        myrange = range(mynums[0], mynums[-1] + 1)
        nc.gpsimd.dma_reset(myrange)
        for s in self.sems.values():
            nc.gpsimd.sem_clear(s)
        for _ in range(4):
            nc.gpsimd.nop()
        nc.gpsimd.dma_start(out=self._fence_b, in_=self._fence_a).then_inc(self.sems["z"], 16)
        self.cnt["z"] = 16
        for eng in ("sync", "tensor", "scalar", "vector"):
            e = getattr(nc, eng)
            for _ in range(24):
                e.nop()
            e.wait_ge(self.sems["z"], 16)
            self.lastw[eng]["z"] = 16
        nc.gpsimd.wait_ge(self.sems["z"], 16)
        self.lastw["gpsimd"]["z"] = 16

    def inc(self, inst, sem, n=1):
        inst.then_inc(self.sems[sem], n)
        self.cnt[sem] += n
        return (sem, self.cnt[sem])

    def ev(self, sem):
        return (sem, self.cnt[sem])

    def barrier(self):
        pass

    def wait(self, engname, ev):
        if ev is None:
            return
        if isinstance(ev, list):
            for e in ev:
                self.wait(engname, e)
            return
        sem, v = ev
        if sem in self.dma_sems:
            v = self.cnt[sem]
        if v <= 0 or self.lastw[engname].get(sem, 0) >= v:
            return
        getattr(self.nc, engname).wait_ge(self.sems[sem], v)
        self.lastw[engname][sem] = v


class Rot:
    def __init__(self, n):
        self.n = n
        self.free_ev = [None] * n
        self.i = 0

    def acquire(self):
        idx = self.i
        self.i = (self.i + 1) % self.n
        return idx, self.free_ev[idx]

    def release(self, idx, ev):
        self.free_ev[idx] = ev


def _ceil32(x):
    return (x + 31) // 32 * 32


SERIAL = int(os.environ.get("ATHENA_SERIAL", "0"))

class Builder:
    def __init__(self, upto="full"):
        self.upto = upto
        self.taps = []
        nc = bass.Bass("TRN2", target_bir_lowering=False, debug=False)
        self.nc = nc
        self.ctx = ExitStack()
        self.P = Prog(nc, self.ctx)
        self._declare_dram()
        self._plan_sbuf()
        self._psum()

    def _declare_dram(self):
        nc = self.nc
        self.d_in = {}
        self.d_in["x"] = nc.dram_tensor("x_in", [128, 16, 2048], F32, kind="ExternalInput").ap()
        for n, (nb, bl) in WSPEC.items():
            self.d_in["wf_" + n] = nc.dram_tensor("wf_" + n, [nb, 128, bl], F32, kind="ExternalInput").ap()
        for nm, shp, dt in (("bqkv", [128, 18], F32), ("bfc", [128, 16], F32),
                            ("ebias", [8, 1], F32), ("ident", [128, 128], F32),
                            ("onesin", [128, 128], F32), ("onehot", [8, 1024], F32),
                            ("masks", [128, MASK_COLS], F32),
                            ("biasa", [128, BIASA_COLS], F32),
                            ("biasb", [128, BIASB_COLS], F32)):
            self.d_in[nm] = nc.dram_tensor(nm, shp, dt, kind="ExternalInput").ap()
        self.out = nc.dram_tensor("out", [128, 16, 2048], F32, kind="ExternalOutput").ap()

        self.wb = {}
        for n, (nb, bl) in WSPEC.items():
            self.wb[n] = nc.dram_tensor("wb_" + n, [nb, 128, bl], BF16, kind="Internal").ap()
        self.qa_d = nc.dram_tensor("qa_d", [128, 16, 2048], BF16, kind="Internal").ap()
        self.qb_d = nc.dram_tensor("qb_d", [128, 16, 2048], BF16, kind="Internal").ap()
        self.aoa_d = nc.dram_tensor("aoa_d", [128, 16, 2048], BF16, kind="Internal").ap()
        self.aob_d = nc.dram_tensor("aob_d", [128, 16, 2048], BF16, kind="Internal").ap()
        self.xr_d = nc.dram_tensor("xr_d", [128, 16, 2048], F32, kind="Internal").ap()
        self.x2_d = nc.dram_tensor("x2_d", [128, 16, 2048], F32, kind="Internal").ap()
        self.ag1_in = nc.dram_tensor("ag1_in", [256, 2048], BF16, kind="Internal").ap()
        self.ag1_out = nc.dram_tensor("ag1_out", [1024, 2048], BF16, kind="Internal").ap()
        self.ag2_in = nc.dram_tensor("ag2_in", [4096, 2048], BF16, kind="Internal").ap()
        self.agk_out = nc.dram_tensor("agk_out", [4, 2048, 2048], BF16, kind="Internal").ap()
        self.agv_out = nc.dram_tensor("agv_out", [4, 2048, 2048], BF16, kind="Internal").ap()

    def _plan_sbuf(self):
        nc = self.nc
        self._sb_virt = []
        self._off = 0

        def A(name, shape, dtype):
            per = 1
            for s in shape[1:]:
                per *= s
            per *= mybir.dt.size(dtype)
            h = nc.virtual_sbuf_tensor(name, shape, dtype)
            self._sb_virt.append((h, self._off))
            self._off = _ceil32(self._off + per)
            return h

        self.c_ident = A("c_ident", [128, 128], BF16)
        self.c_identf = A("c_identf", [128, 128], F32)
        self.c_ones = A("c_ones", [128, 128], BF16)
        self.c_onehot = A("c_onehot", [8, 1024], BF16)
        self.c_masks = A("c_masks", [128, MASK_COLS], BF16)
        self.c_biasa = A("c_biasa", [128, BIASA_COLS], F32)
        self.c_bqkv = A("c_bqkv", [128, 18], F32)
        self.c_bqs = A("c_bqs", [128, 16], F32)
        self.c_bfc = A("c_bfc", [128, 16], F32)
        self.c_ebias = A("c_ebias", [8, 1], F32)
        self.c_router = A("c_router", [128, 128], BF16)
        self.s_xc = A("s_xc", [128, 4, 512], F32)
        self.s_st = A("s_st", [1, 4, 512], F32)
        self.s_st16 = A("s_st16", [1, 2, 512], BF16)
        self.s_ev = A("s_ev", [128, 6, 512], BF16)
        self.s_t1 = A("s_t1", [128, 6, 512], BF16)
        self.s_mb = A("s_mb", [128, 512], BF16)
        self.s_rb = A("s_rb", [128, 512], BF16)

        g1 = self._off
        self.s_y = A("s_y", [128, 16, 512], BF16)
        self.s_xb = A("s_xb", [128, 16, 512], BF16)
        self.s_mid = A("s_mid", [128, 64, 512], BF16)
        g1a = self._off
        self.s_ao = A("s_ao", [128, 16, 512], BF16)
        end_g1a = self._off
        self._off = g1a
        self.s_oacc = A("s_oacc", [128, 16, 512], BF16)
        self._off = max(self._off, end_g1a)
        self.s_w = A("s_w", [128, 4, 4096], BF16)
        self.s_gb = A("s_gb", [128, 8, 512], BF16)
        self.s_gr = A("s_gr", [8, 2, 512], F32)
        self.s_gr16 = A("s_gr16", [8, 2, 512], BF16)
        self.s_gf = A("s_gf", [8, 512], BF16)
        self.s_gtm = A("s_gtm", [128, 8, 8], F32)
        self.s_dbg = A("s_dbg", [128, 16, 8], BF16)
        self.s_dbg2 = A("s_dbg2", [128, 4, 512], BF16)
        self.s_gtm16 = A("s_gtm16", [128, 8, 8], BF16)
        end_g1 = self._off

        self._off = g1
        self.c_biasb = A("c_biasb", [128, BIASB_COLS], F32)
        self.s_ka = A("s_ka", [128, 2, 80, 128], BF16)
        self.s_va = A("s_va", [128, 2, 80, 128], BF16)
        self.s_vt = A("s_vt", [128, 2, 80, 128], BF16)
        self.s_q = A("s_q", [128, 2, 2048], BF16)
        self.s_pt = A("s_pt", [128, 4, 512], BF16)
        self.s_zr = A("s_zr", [1, 4, 512], BF16)
        self.s_rbb = A("s_rbb", [128, 2, 512], BF16)
        self.s_aos = A("s_aos", [128, 4, 512], BF16)
        end_g2 = self._off

        self._off = max(end_g1, end_g2)
        assert self._off <= 207 * 1024, f"SBUF over budget: {self._off}"
        self.sbuf_total = self._off

    def _materialize_sbuf(self):
        nc = self.nc
        self.arena = self.ctx.enter_context(
            nc.sbuf_tensor("arena", [128, self._off], mybir.dt.uint8))
        base = nc.lookup_mloc(self.arena).addr
        for h, off in self._sb_virt:
            nc._tensor(h.name, list(h.shape), h.dtype, type="SB")
            mloc = nc.lookup_mloc(h)
            mloc.allocated = not nc.target_bir_lowering
            mloc.addr = base + off

    def _psum(self):
        nc = self.nc
        self.psA = self.ctx.enter_context(nc.psum_tensor("psA", [128, 4, 512], F32))
        self.psB = self.ctx.enter_context(nc.psum_tensor("psB", [128, 2, 512], F32))
        self.psZ = self.ctx.enter_context(nc.psum_tensor("psZ", [128, 1, 512], F32))
        self.psT = self.ctx.enter_context(nc.psum_tensor("psT", [128, 8, 128], BF16))
        self.rA = Rot(4)
        self.rB = Rot(2)
        self.rT = Rot(8)
        self._psZ_free = None
        self._psT_free = None
        self._vt_guard = [None, None]
        self._ao_free = None
        self._mid_free = None
        self._oacc_free = None
        self._gb_free = None
        self._gb_ready = None
        self._y_free = None
        self._xout_free = None
        self.wrot = Rot(4)

    def load_consts(self):
        nc, P = self.nc, self.P
        for sb, dr in ((self.c_biasa, "biasa"),
                       (self.c_bqkv, "bqkv"), (self.c_bfc, "bfc"),
                       (self.c_ebias, "ebias"), (self.c_identf, "ident")):
            P.inc(nc.sync.dma_start(out=sb[:], in_=self.d_in[dr]), "lx", 16)
        ev_f32 = P.ev("lx")
        for sb, dr in ((self.c_ident, "ident"), (self.c_ones, "onesin"),
                       (self.c_onehot, "onehot"), (self.c_masks, "masks")):
            P.inc(nc.gpsimd.dma_start(out=sb[:], in_=self.d_in[dr]), "lg", 16)
        self.ev_consts = [ev_f32, P.ev("lg")]
        P.wait("vector", self.ev_consts)
        i = nc.vector.tensor_scalar_mul(self.c_bqs[:, :], self.c_bqkv[:, 0:16], float(INV_SQRT_HD))
        self.ev_bqs = P.inc(i, "dv")

    def convert_weights(self, order, defer=False):
        nc, P = self.nc, self.P
        for n in order:
            nb, bl = WSPEC[n]
            step = max(1, min(nb, (8 * 1024 * 1024) // (128 * bl * 4)))
            ev = None
            for b0 in range(0, nb, step):
                b1 = min(nb, b0 + step)
                if defer:
                    self._pend_conv.append((n, b0, b1))
                    P.cnt["cv_" + n] += 16
                else:
                    i = nc.gpsimd.dma_start(out=self.wb[n][b0:b1],
                                            in_=self.d_in["wf_" + n][b0:b1])
                    ev = P.inc(i, "cv_" + n, 16)
            self.ev_conv[n] = ("cv_" + n, P.cnt["cv_" + n])

    def _emit_conv_chunks(self, k):
        nc, P = self.nc, self.P
        for _ in range(k):
            if not self._pend_conv:
                return
            n, b0, b1 = self._pend_conv.pop(0)
            P.lastw["gpsimd"].pop("mm", None)
            P.wait("gpsimd", ("mm", P.cnt["mm"]))
            i = nc.gpsimd.dma_start(out=self.wb[n][b0:b1], in_=self.d_in["wf_" + n][b0:b1])
            i.then_inc(P.sems["cv_" + n], 16)

    def wload(self, n, b, slot, prior_ev):
        nc, P = self.nc, self.P
        nb, bl = WSPEC[n]
        P.wait("sync", self.ev_conv[n])
        P.wait("sync", prior_ev)
        i = nc.sync.dma_start(out=self.s_w[:, slot, 0:bl], in_=self.wb[n][b])
        return P.inc(i, "lw%d" % slot, 16)

    def ln_stats_start(self):
        P = self.P
        sB, evB = self.rB.acquire()
        P.wait("tensor", evB)
        P.wait("tensor", self._psZ_free)
        return sB

    def ln_stat_chunk(self, c, sB, f32_ap, ev_ready):
        nc, P = self.nc, self.P
        P.wait("vector", ev_ready)
        i = nc.vector.tensor_copy(self.s_xb[:, c, :], f32_ap)
        ev_cast = P.inc(i, "dv")
        qslot = c % 4
        P.wait("scalar", ev_ready)
        P.wait("scalar", self._sq_free[qslot])
        i = nc.scalar.activation(self.s_t1[:, qslot, :], f32_ap, AF.Square)
        ev_sq = P.inc(i, "ac")
        P.wait("tensor", ev_cast)
        nc.tensor.matmul(self.psZ[0:1, 0, :], self.c_ones[:, 0:1], self.s_xb[:, c, :],
                         start=(c == 0), stop=(c == 15))
        P.wait("tensor", ev_sq)
        i = nc.tensor.matmul(self.psB[0:1, sB, :], self.c_ones[:, 0:1],
                             self.s_t1[:, qslot, :], start=(c == 0), stop=(c == 15))
        ev_mm = P.inc(i, "mm")
        self._sq_free[qslot] = ev_mm
        return [ev_cast, ev_sq]

    def ln_finish_stats(self, sB):
        nc, P = self.nc, self.P
        ev_mm = P.ev("mm")
        P.wait("vector", ev_mm)
        nc.vector.tensor_copy(self.s_st[:, 0, :], self.psZ[0:1, 0, :])
        i = nc.vector.tensor_copy(self.s_st[:, 1, :], self.psB[0:1, sB, :])
        ev_stats = P.inc(i, "dv")
        self.rB.release(sB, ev_stats)
        self._psZ_free = ev_stats
        return ev_stats

    def ln_apply(self, ev_stats, y_free_ev=None):
        nc, P = self.nc, self.P
        st, st16 = self.s_st, self.s_st16
        P.wait("vector", ev_stats)
        nc.vector.tensor_scalar_mul(st[:, 2, :], st[:, 0, :], 1.0 / 2048.0)
        nc.vector.tensor_scalar_mul(st[:, 3, :], st[:, 1, :], 1.0 / 2048.0)
        nc.vector.tensor_mul(st[:, 0, :], st[:, 2, :], st[:, 2, :])
        i = nc.vector.tensor_sub(st[:, 1, :], st[:, 3, :], st[:, 0, :])
        ev_var = P.inc(i, "dv")
        P.wait("scalar", ev_var)
        nc.scalar.activation(st[:, 0, :], st[:, 1, :], AF.Ln, bias=FP(1e-5))
        i = nc.scalar.activation(st16[:, 1, :], st[:, 0, :], AF.Exp, scale=FP(-0.5))
        ev_rstd = P.inc(i, "ac")
        P.wait("vector", ev_rstd)
        i = nc.vector.tensor_copy(st16[:, 0, :], st[:, 2, :])
        ev_m16 = P.inc(i, "dv")
        sA, evA = self.rA.acquire()
        sA2, evA2 = self.rA.acquire()
        P.wait("tensor", [ev_m16, ev_rstd, evA, evA2])
        nc.tensor.matmul(self.psA[:, sA, :], self.c_ones[0:1, :], st16[:, 0, :],
                         start=True, stop=True)
        i = nc.tensor.matmul(self.psA[:, sA2, :], self.c_ones[0:1, :], st16[:, 1, :],
                             start=True, stop=True)
        ev_bc = P.inc(i, "mm")
        P.wait("vector", ev_bc)
        nc.vector.tensor_copy(self.s_mb[:, :], self.psA[:, sA, :])
        i = nc.vector.tensor_copy(self.s_rb[:, :], self.psA[:, sA2, :])
        ev_bcs = P.inc(i, "dv")
        self.rA.release(sA, ev_bcs)
        self.rA.release(sA2, ev_bcs)
        P.wait("vector", y_free_ev)
        ev = None
        for c in range(16):
            sl = 4 + (c % 2)
            nc.vector.tensor_sub(self.s_t1[:, sl, :], self.s_xb[:, c, :], self.s_mb[:, :])
            i = nc.vector.tensor_mul(self.s_y[:, c, :], self.s_t1[:, sl, :], self.s_rb[:, :])
            if c == 15:
                ev = P.inc(i, "dv")
        return ev

    def ln_from_dram(self, src, ts, y_free_ev=None):
        nc, P = self.nc, self.P
        self._sq_free = [[P.ev("mm"), P.ev("dv")]] * 4
        sB = self.ln_stats_start()
        xrot = Rot(4)
        pend = {}
        for c in range(16):
            slot, evfree = xrot.acquire()
            P.wait("sync", evfree)
            i = nc.sync.dma_start(out=self.s_xc[:, slot, :], in_=src[:, c, ts])
            pend[c] = (slot, P.inc(i, "lx", 16))
            if c >= 3:
                cc = c - 3
                sl, ev = pend.pop(cc)
                rel = self.ln_stat_chunk(cc, sB, self.s_xc[:, sl, :], ev)
                xrot.release(sl, rel)
        for cc in range(13, 16):
            sl, ev = pend.pop(cc)
            rel = self.ln_stat_chunk(cc, sB, self.s_xc[:, sl, :], ev)
            xrot.release(sl, rel)
        ev_stats = self.ln_finish_stats(sB)
        self.P.barrier()
        ev = self.ln_apply(ev_stats, y_free_ev)
        self.P.barrier()
        return ev

    def proj(self, wname, y_ap_fn, mlist, evac_fn, extra_rhs_ev=None):
        nc, P = self.nc, self.P
        wrot = self.wrot
        pend = {}
        for idx in range(min(2, len(mlist))):
            slot, evfree = wrot.acquire()
            pend[idx] = (slot, self.wload(wname, mlist[idx], slot, evfree))
        nxt = min(2, len(mlist))
        for idx, m in enumerate(mlist):
            slot, ev_w = pend.pop(idx)
            sB, evB = self.rB.acquire()
            P.wait("tensor", [ev_w, evB, extra_rhs_ev])
            wv = self.s_w[:, slot, 0:2048].rearrange("p (k m) -> p k m", m=128)
            last = None
            for k in range(16):
                last = nc.tensor.matmul(self.psB[:, sB, :], wv[:, k, :], y_ap_fn(k),
                                        start=(k == 0), stop=(k == 15))
            ev_mm = P.inc(last, "mm")
            wrot.release(slot, ev_mm)
            if nxt < len(mlist):
                s2, evf2 = wrot.acquire()
                pend[nxt] = (s2, self.wload(wname, mlist[nxt], s2, evf2))
                nxt += 1
            ev_done = evac_fn(idx, m, self.psB[:, sB, :], ev_mm)
            self.rB.release(sB, ev_done)
            self.P.barrier()

    def phase0(self):
        nc, P = self.nc, self.P
        P.wait("tensor", self.ev_consts)
        evrot = Rot(6)
        y_free = None
        for t in range(NT):
            ts = slice(t * 512, (t + 1) * 512)
            ev_y = self.ln_from_dram(self.d_in["x"], ts, y_free)

            def evac(idx, m, ps, ev_mm, ts=ts):
                slot, evfree = evrot.acquire()
                P.wait("scalar", [ev_mm, evfree, self.ev_bqs])
                if m < 16:
                    i = nc.scalar.activation(self.s_ev[:, slot, :], ps, AF.Identity,
                                             bias=self.c_bqs[:, m:m + 1], scale=INV_SQRT_HD)
                else:
                    i = nc.scalar.activation(self.s_ev[:, slot, :], ps, AF.Identity,
                                             bias=self.c_bqkv[:, m:m + 1], scale=FP(1.0))
                ev_a = P.inc(i, "ac")
                if m < 16:
                    dst = self.qa_d[:, m, ts]
                elif m == 16:
                    dst = self.ag1_in[0:128, ts]
                else:
                    dst = self.ag1_in[128:256, ts]
                i = nc.scalar.dma_start(out=dst, in_=self.s_ev[:, slot, :])
                ev_st = P.inc(i, "ls", 16)
                evrot.release(slot, ev_st)
                return ev_a

            P.wait("tensor", ev_y)
            self.proj("wqkv", lambda k: self.s_y[:, k, :], list(range(18)), evac)
            y_free = P.ev("mm")

    def _span_loads(self, layer, h, buf, guard=None):
        nc, P = self.nc, self.P
        P.wait("sync", guard)
        if layer == "a":
            P.wait("sync", ("ag", 1))
            plan = ((self.s_ka, self.ag1_out, 256, 0, self.ag1_in, 0),
                    (self.s_va, self.ag1_out, 256, 128, self.ag1_in, 128))
        else:
            gi, hr = h // 4, h % 4
            P.wait("sync", self.ev_ag2g[gi])
            plan = ((self.s_ka, self.agk_out[gi], 512, hr * 128, self.ag2_in, h * 128),
                    (self.s_va, self.agv_out[gi], 512, hr * 128, self.ag2_in, 2048 + h * 128))
        for dst_t, agout, stride, row, agin, irow in plan:
            for sgc in range(16):
                rho, sig = _SC_MAP[sgc]
                src = agout[rho * stride + row: rho * stride + row + 128,
                            sig * 512:(sig + 1) * 512]
                i = nc.sync.dma_start(
                    out=dst_t[:, buf, 4 * sgc:4 * sgc + 4, :],
                    in_=src.rearrange("p (c m) -> p c m", m=128))
                P.inc(i, "ka%d" % buf, 16)
            for u in range(4):
                src = agin[irow:irow + 128, u * 512:(u + 1) * 512]
                i = nc.sync.dma_start(
                    out=dst_t[:, buf, 64 + 4 * u:64 + 4 * u + 4, :],
                    in_=src.rearrange("p (c m) -> p c m", m=128))
                P.inc(i, "ka%d" % buf, 16)
        return P.ev("ka%d" % buf)

    def _emit_transpose_quad(self, buf, q):
        nc, P = self.nc, self.P
        P.wait("tensor", self._psT_free)
        P.wait("tensor", self._vt_guard[buf])
        P.wait("tensor", P.ev("ka%d" % buf))
        last = None
        with nc.allow_low_precision(reason="bf16 transpose staging, no accumulation"):
            for d in range(4):
                last = nc.tensor.transpose(self.psT[:, d, :], self.s_va[:, buf, 4 * q + d, :],
                                           self.c_ident[:, :])
        ev_t = P.inc(last, "mm")
        P.wait("vector", ev_t)
        i = nc.vector.tensor_copy(
            self.s_vt[:, buf, 4 * q:4 * q + 4, :],
            self.psT[:, 0:4, :])
        self._psT_free = P.inc(i, "dv")

    def flash_layer(self, layer):
        nc, P = self.nc, self.P
        P.wait("tensor", self.ev_consts)
        P.wait("sync", P.ev("ls"))
        if layer == "b":
            i = nc.sync.dma_start(out=self.c_biasb[:], in_=self.d_in["biasb"])
            P.inc(i, "lx", 16)
        self._bias_ev = P.ev("lx")
        fb = FB_A if layer == "a" else FB_B
        qsrc = self.qa_d if layer == "a" else self.qb_d
        aod = self.aoa_d if layer == "a" else self.aob_d
        btab = self.c_biasa if layer == "a" else self.c_biasb
        self._q_guard = [None, None]
        ptrot = Rot(4)
        zrot = Rot(4)
        rbrot = Rot(2)
        aorot = Rot(4)
        by_head = defaultdict(list)
        for b in fb:
            by_head[b["h"]].append(b)

        def load_qspan(h):
            qb = h % 2
            P.wait("sync", self._q_guard[qb])
            i = nc.sync.dma_start(out=self.s_q[:, qb, :], in_=qsrc[:, h, :])
            P.inc(i, "lq%d" % qb, 16)

        load_qspan(0)

        if layer == "a":
            self._span_loads("a", 0, 0, guard=self._vt_guard[0])
            for q in range(20):
                self._emit_transpose_quad(0, q)
            head_buf = {h: 0 for h in range(NH)}
        else:
            self._span_loads("b", 0, 0, guard=self._vt_guard[0])
            for q in range(20):
                self._emit_transpose_quad(0, q)
            head_buf = {h: h % 2 for h in range(NH)}

        pend_tq = []
        for h in range(NH):
            buf = head_buf[h]
            pieces = by_head[h]
            if h + 1 < NH:
                load_qspan(h + 1)
            if layer == "b" and h + 1 < NH:
                self._span_loads("b", h + 1, (h + 1) % 2,
                                 guard=self._vt_guard[(h + 1) % 2])
                pend_tq = [((h + 1) % 2, q) for q in range(20)]
            nq = max(1, (len(pend_tq) + len(pieces) - 1) // max(1, len(pieces)))
            for idx, b in enumerate(pieces):
                self._flash_piece(layer, b, buf, h % 2, ptrot, zrot, rbrot,
                                  aorot, btab, aod)
                if layer == "a" and getattr(self, "_pend_conv", None):
                    self._emit_conv_chunks(2)
                for _ in range(nq):
                    if pend_tq:
                        tbuf, tq = pend_tq.pop(0)
                        self._emit_transpose_quad(tbuf, tq)
            while pend_tq:
                tbuf, tq = pend_tq.pop(0)
                self._emit_transpose_quad(tbuf, tq)
            self._vt_guard[buf] = P.ev("mm")
            self._q_guard[h % 2] = P.ev("mm")

    def _flash_piece(self, layer, b, buf, qbuf, ptrot, zrot, rbrot, aorot,
                     btab, aod):
        nc, P = self.nc, self.P
        W = b["W"]
        n_all = b["npos"] + b["ndiag"]
        wdiv = W // 128
        q_ap = self.s_q[:, qbuf, b["u"] * 512 + b["i"] * W:
                        b["u"] * 512 + (b["i"] + 1) * W]
        sO, evO = self.rB.acquire()
        P.wait("tensor", [evO, self._psZ_free, P.ev("ka%d" % buf),
                          P.ev("lq%d" % qbuf)])

        def chunk_info(ci):
            if ci < b["npos"]:
                return ci, b["bias0"] + ci, None
            dd = ci - b["npos"]
            return (64 + 4 * b["u"] + b["i"] * wdiv + dd,
                    b["bias0"] + b["npos"] + dd, dd)

        pairs = [list(range(p0, min(p0 + 2, n_all))) for p0 in range(0, n_all, 2)]
        state = []
        for cis in pairs:
            sas = []
            last = None
            for ci in cis:
                pos, bias_c, diag = chunk_info(ci)
                sA, evA = self.rA.acquire()
                P.wait("tensor", evA)
                last = nc.tensor.matmul(self.psA[:, sA, 0:W], self.s_ka[:, buf, pos, :],
                                        q_ap, start=True, stop=True)
                sas.append(sA)
            ev_qk = P.inc(last, "mm")
            pts = []
            last = None
            for k2, ci in enumerate(cis):
                pos, bias_c, diag = chunk_info(ci)
                pt, evp = ptrot.acquire()
                P.wait("scalar", [ev_qk, evp, self._bias_ev])
                last = nc.scalar.activation(self.s_pt[:, pt, 0:W], self.psA[:, sas[k2], 0:W],
                                            AF.Exp, bias=btab[:, bias_c:bias_c + 1],
                                            scale=FP(1.0))
                pts.append(pt)
            ev_exp = P.inc(last, "ac")
            for sA in sas:
                self.rA.release(sA, ev_exp)
            ev_ready = ev_exp
            for k2, ci in enumerate(cis):
                pos, bias_c, diag = chunk_info(ci)
                if diag is not None:
                    mo = MASK_OFF[(W, diag)]
                    P.wait("vector", ev_exp)
                    i = nc.vector.tensor_mul(self.s_pt[:, pts[k2], 0:W],
                                             self.s_pt[:, pts[k2], 0:W],
                                             self.c_masks[:, mo:mo + W])
                    ev_ready = P.inc(i, "dv")
            state.append((cis, pts, ev_ready))
            if len(state) > 1:
                self._flash_zav_pair(state.pop(0), b, buf, sO, W, n_all, ptrot)
        while state:
            self._flash_zav_pair(state.pop(0), b, buf, sO, W, n_all, ptrot)
        ev_mm = P.ev("mm")
        zs, evz = zrot.acquire()
        P.wait("vector", [ev_mm, evz])
        with nc.allow_low_precision(reason="softmax normalizer to bf16 for broadcast"):
            i = nc.vector.reciprocal(self.s_zr[0:1, zs, 0:W], self.psZ[0:1, 0, 0:W])
        ev_zr = P.inc(i, "dv")
        self._psZ_free = ev_zr
        sA, evA = self.rA.acquire()
        P.wait("tensor", [ev_zr, evA])
        i = nc.tensor.matmul(self.psA[:, sA, 0:W], self.c_ones[0:1, :],
                             self.s_zr[0:1, zs, 0:W], start=True, stop=True)
        ev_bc = P.inc(i, "mm")
        zrot.release(zs, ev_bc)
        rb, evrb = rbrot.acquire()
        P.wait("vector", [ev_bc, evrb])
        i = nc.vector.tensor_copy(self.s_rbb[:, rb, 0:W], self.psA[:, sA, 0:W])
        ev_rbc = P.inc(i, "dv")
        self.rA.release(sA, ev_rbc)
        ao, evao = aorot.acquire()
        P.wait("vector", evao)
        i = nc.vector.tensor_mul(self.s_aos[:, ao, 0:W], self.psB[:, sO, 0:W],
                                 self.s_rbb[:, rb, 0:W])
        ev_n = P.inc(i, "dv")
        rbrot.release(rb, ev_n)
        self.rB.release(sO, ev_n)
        P.wait("scalar", ev_n)
        c0 = b["u"] * 512 + b["i"] * W
        i = nc.scalar.dma_start(out=aod[:, b["h"], c0:c0 + W],
                                in_=self.s_aos[:, ao, 0:W])
        ev_st = P.inc(i, "ls", 16)
        aorot.release(ao, ev_st)

    def _flash_zav_pair(self, st, b, buf, sO, W, n_all, ptrot):
        nc, P = self.nc, self.P
        cis, pts, ev_ready = st
        P.wait("tensor", ev_ready)
        i = None
        for k2, ci in enumerate(cis):
            if ci < b["npos"]:
                pos = ci
            else:
                pos = 64 + 4 * b["u"] + b["i"] * (W // 128) + (ci - b["npos"])
            nc.tensor.matmul(self.psZ[0:1, 0, 0:W], self.c_ones[:, 0:1],
                             self.s_pt[:, pts[k2], 0:W], start=(ci == 0),
                             stop=(ci == n_all - 1))
            i = nc.tensor.matmul(self.psB[:, sO, 0:W], self.s_vt[:, buf, pos, :],
                                 self.s_pt[:, pts[k2], 0:W], start=(ci == 0),
                                 stop=(ci == n_all - 1))
        ev_av = P.inc(i, "mm")
        for pt in pts:
            ptrot.release(pt, ev_av)

    def _ln_stats_begin(self, pool):
        P = self.P
        self._sq_free = [[P.ev("mm"), P.ev("dv")]] * 4
        if pool == "B":
            sl, ev = self.rB.acquire()
            ap = self.psB[0:1, sl, :]
        else:
            sl, ev = self.rA.acquire()
            ap = self.psA[0:1, sl, :]
        P.wait("tensor", [ev, self._psZ_free])
        return (pool, sl, ap)

    def _ln_stats_finish(self, hnd):
        nc, P = self.nc, self.P
        pool, sl, ap = hnd
        ev_mm = P.ev("mm")
        P.wait("vector", ev_mm)
        nc.vector.tensor_copy(self.s_st[:, 0, :], self.psZ[0:1, 0, :])
        i = nc.vector.tensor_copy(self.s_st[:, 1, :], ap)
        ev_stats = P.inc(i, "dv")
        (self.rB if pool == "B" else self.rA).release(sl, ev_stats)
        self._psZ_free = ev_stats
        return ev_stats

    def _ln_stat_chunk2(self, c, hnd, f32_ap, ev_ready):
        nc, P = self.nc, self.P
        pool, sl, ap = hnd
        P.wait("vector", ev_ready)
        i = nc.vector.tensor_copy(self.s_xb[:, c, :], f32_ap)
        ev_cast = P.inc(i, "dv")
        qslot = c % 4
        P.wait("scalar", [ev_ready, self._sq_free[qslot]])
        i = nc.scalar.activation(self.s_t1[:, qslot, :], f32_ap, AF.Square)
        ev_sq = P.inc(i, "ac")
        P.wait("tensor", ev_cast)
        nc.tensor.matmul(self.psZ[0:1, 0, :], self.c_ones[:, 0:1], self.s_xb[:, c, :],
                         start=(c == 0), stop=(c == 15))
        P.wait("tensor", ev_sq)
        i = nc.tensor.matmul(ap, self.c_ones[:, 0:1],
                             self.s_t1[:, qslot, :], start=(c == 0), stop=(c == 15))
        ev_mm = P.inc(i, "mm")
        self._sq_free[qslot] = ev_mm
        return [ev_cast, ev_sq]

    def phase_mid(self, layer):
        nc, P = self.nc, self.P
        for t in range(NT):
            ts = slice(t * 512, (t + 1) * 512)
            self._mid_tile(layer, t, ts)

    def _mid_tile(self, layer, t, ts):
        nc, P = self.nc, self.P
        ao_d = self.aoa_d if layer == 1 else self.aob_d
        res_src = self.d_in["x"] if layer == 1 else self.x2_d
        wname = "wfc" if layer == 1 else "wo"
        P.wait("sync", self._ao_free)
        P.wait("sync", P.ev("ls"))
        i = nc.sync.dma_start(out=self.s_ao[:, :, :], in_=ao_d[:, :, ts])
        P.inc(i, "lx", 16)
        ev_ao = P.ev("lx")
        hnd = self._ln_stats_begin("A")
        xin_rot = Rot(2)

        def evac(idx, m, ps, ev_mm, ts=ts, layer=layer):
            xs, evx = xin_rot.acquire()
            P.wait("sync", evx)
            P.wait("sync", P.ev("ls"))
            i = nc.sync.dma_start(out=self.s_xc[:, xs, :], in_=res_src[:, m, ts])
            P.inc(i, "lx", 16)
            P.wait("vector", [ev_mm, self._xout_free])
            P.wait("vector", P.ev("lx"))
            if layer == 1:
                i = nc.vector.scalar_tensor_tensor(
                    self.s_xc[:, 3, :], ps, self.c_bfc[:, m:m + 1],
                    self.s_xc[:, xs, :], ALU.add, ALU.add)
            else:
                i = nc.vector.scalar_tensor_tensor(
                    self.s_xc[:, 3, :], ps, 1.0,
                    self.s_xc[:, xs, :], ALU.mult, ALU.add)
            ev_x1 = P.inc(i, "dv")
            xin_rot.release(xs, ev_x1)
            rel = self._ln_stat_chunk2(m, hnd, self.s_xc[:, 3, :], ev_x1)
            P.wait("scalar", ev_x1)
            i = nc.scalar.dma_start(out=self.xr_d[:, m, ts], in_=self.s_xc[:, 3, :])
            ev_sp = P.inc(i, "ls", 16)
            self._xout_free = [rel[0], rel[1], ev_sp]
            return ev_x1

        self.proj(wname, lambda k: self.s_ao[:, k, :], list(range(16)), evac,
                  extra_rhs_ev=ev_ao)
        ev_stats = self._ln_stats_finish(hnd)
        ev_y = self.ln_apply(ev_stats, self._y_free)
        self._ffn_tile(t, ts, ev_y, layer)
        if layer == 1:
            ev_stats2 = self._ln_stats_finish(self._hnd2)
            ev_y2 = self.ln_apply(ev_stats2, [P.ev("mm"), P.ev("dv")])
            for wn, kind in (("wq", "q"), ("wk", "k"), ("wv", "v")):
                evr = Rot(3)

                def evac3(idx, m, ps, ev_mm, wn=wn, kind=kind, ts=ts, evr=evr):
                    slot, evfree = evr.acquire()
                    P.wait("scalar", [ev_mm, evfree])
                    sc = INV_SQRT_HD if kind == "q" else FP(1.0)
                    i = nc.scalar.activation(self.s_ev[:, slot, :], ps, AF.Identity,
                                             scale=sc)
                    ev_a = P.inc(i, "ac")
                    P.wait("scalar", ev_a)
                    if kind == "q":
                        dst = self.qb_d[:, m, ts]
                    elif kind == "k":
                        dst = self.ag2_in[m * 128:(m + 1) * 128, ts]
                    else:
                        dst = self.ag2_in[2048 + m * 128:2048 + (m + 1) * 128, ts]
                    i = nc.scalar.dma_start(out=dst, in_=self.s_ev[:, slot, :])
                    ev_st = P.inc(i, "ls", 16)
                    evr.release(slot, ev_st)
                    return ev_a

                P.wait("tensor", ev_y2)
                self.proj(wn, lambda k: self.s_y[:, k, :], list(range(16)), evac3)
            self._y_free = P.ev("mm")
        else:
            self._y_free = P.ev("mm")

    def _ffn_tile(self, t, ts, ev_y, layer):
        nc, P = self.nc, self.P
        P.wait("tensor", ev_y)
        wrot = self.wrot
        t1rot = Rot(4)
        t1rot.free_ev = [[P.ev("mm"), P.ev("dv")]] * 4

        mid_guard = self._mid_free
        pend = {}
        wload4 = self.wload

        for bi in range(2):
            for wi, n in enumerate(("sw1", "sw2")):
                slot, evf = wrot.acquire()
                pend[(bi, wi)] = (slot, wload4(n, bi, slot, evf))
        nxt = 2
        for bi in range(32):
            s1, ev1 = pend.pop((bi, 0))
            s2, ev2 = pend.pop((bi, 1))
            wv1 = self.s_w[:, s1, :].rearrange("p (k m) -> p k m", m=256)
            wv2 = self.s_w[:, s2, :].rearrange("p (k m) -> p k m", m=256)
            for c2 in range(2):
                hc = bi * 2 + c2
                sg, evg = self.rA.acquire()
                su, evu = self.rA.acquire()
                P.wait("tensor", [ev1, ev2, evg, evu])
                for k in range(16):
                    nc.tensor.matmul(self.psA[:, sg, :], wv1[:, k, 128 * c2:128 * c2 + 128],
                                     self.s_y[:, k, :], start=(k == 0), stop=(k == 15))
                last = None
                for k in range(16):
                    last = nc.tensor.matmul(self.psA[:, su, :], wv2[:, k, 128 * c2:128 * c2 + 128],
                                            self.s_y[:, k, :], start=(k == 0), stop=(k == 15))
                ev_u = P.inc(last, "mm")
                tslot, evt = t1rot.acquire()
                P.wait("scalar", [ev_u, evt])
                i = nc.scalar.activation(self.s_t1[:, tslot, :], self.psA[:, sg, :], AF.Silu)
                ev_si = P.inc(i, "ac")
                P.wait("vector", [ev_si, mid_guard])
                if hc == 0:
                    nc.vector.tensor_copy(self.s_dbg2[:, 0, :], self.psA[:, sg, :])
                    nc.vector.tensor_copy(self.s_dbg2[:, 1, :], self.psA[:, su, :])
                    nc.vector.tensor_copy(self.s_dbg2[:, 2, :], self.s_t1[:, tslot, :])
                    nc.vector.tensor_copy(self.s_dbg2[:, 3, :], self.s_y[:, 0, :])
                self.rA.release(sg, ev_si)
                i = nc.vector.tensor_mul(self.s_mid[:, hc, :], self.s_t1[:, tslot, :],
                                         self.psA[:, su, :])
                ev_m = P.inc(i, "dv")
                self.rA.release(su, ev_m)
                t1rot.release(tslot, ev_m)
            ev_wdone = P.ev("mm")
            wrot.release(s1, ev_wdone)
            wrot.release(s2, ev_wdone)
            if nxt < 32:
                for wi, n in enumerate(("sw1", "sw2")):
                    slot, evf = wrot.acquire()
                    pend[(nxt, wi)] = (slot, wload4(n, nxt, slot, evf))
                nxt += 1

        self._router_gating(ev_y)

        ev_mid_a = P.ev("dv")
        pend = {}
        for ii in range(2):
            slot, evf = wrot.acquire()
            pend[ii] = (slot, wload4("sw3", ii, slot, evf))
        nxt = 2
        for m in range(16):
            outs = []
            for half in range(2):
                ii = m * 2 + half
                slot, ev_w = pend.pop(ii)
                wv = self.s_w[:, slot, :].rearrange("p (k m) -> p k m", m=128)
                if half == 0:
                    sB, evB = self.rB.acquire()
                    P.wait("tensor", [evB, ev_mid_a])
                P.wait("tensor", ev_w)
                last = None
                for kc in range(32):
                    last = nc.tensor.matmul(self.psB[:, sB, :], wv[:, kc, :],
                                            self.s_mid[:, half * 32 + kc, :],
                                            start=(half == 0 and kc == 0),
                                            stop=(half == 1 and kc == 31))
                ev_mm = P.inc(last, "mm")
                wrot.release(slot, ev_mm)
                if nxt < 32:
                    slot2, evf2 = wrot.acquire()
                    pend[nxt] = (slot2, wload4("sw3", nxt, slot2, evf2))
                    nxt += 1
            P.wait("vector", [ev_mm, self._oacc_free])
            i = nc.vector.tensor_copy(self.s_oacc[:, m, :], self.psB[:, sB, :])
            ev_oc = P.inc(i, "dv")
            self.rB.release(sB, ev_oc)

        ev_mid_rd = P.ev("mm")
        pend = {}
        order = []
        for e in range(8):
            for bp in range(4):
                order.append((e, bp))
        for ii in range(2):
            e, bp = order[ii]
            sl1, evf1 = wrot.acquire()
            sl2, evf2 = wrot.acquire()
            pend[ii] = ((sl1, wload4("w12", e * 8 + bp, sl1, evf1)),
                        (sl2, wload4("w12", e * 8 + 4 + bp, sl2, evf2)))
        nxt = 2
        for ii, (e, bp) in enumerate(order):
            (sl1, ev1), (sl2, ev2) = pend.pop(ii)
            wv1 = self.s_w[:, sl1, :].rearrange("p (k m) -> p k m", m=256)
            wv2 = self.s_w[:, sl2, :].rearrange("p (k m) -> p k m", m=256)
            for c2 in range(2):
                hc = 2 * bp + c2
                s1a, ev1a = self.rA.acquire()
                s2a, ev2a = self.rA.acquire()
                P.wait("tensor", [ev1, ev2, ev1a, ev2a])
                for k in range(16):
                    nc.tensor.matmul(self.psA[:, s1a, :], wv1[:, k, 128 * c2:128 * c2 + 128],
                                     self.s_y[:, k, :], start=(k == 0), stop=(k == 15))
                last = None
                for k in range(16):
                    last = nc.tensor.matmul(self.psA[:, s2a, :], wv2[:, k, 128 * c2:128 * c2 + 128],
                                            self.s_y[:, k, :], start=(k == 0), stop=(k == 15))
                ev_u = P.inc(last, "mm")
                tslot, evt = t1rot.acquire()
                P.wait("scalar", [ev_u, evt])
                i = nc.scalar.activation(self.s_t1[:, tslot, :], self.psA[:, s1a, :], AF.Silu)
                ev_si = P.inc(i, "ac")
                self.rA.release(s1a, ev_si)
                t2slot, evt2 = t1rot.acquire()
                P.wait("vector", [ev_si, evt2, ev_mid_rd, self._gb_ready])
                i = nc.vector.tensor_mul(self.s_t1[:, t2slot, :], self.s_t1[:, tslot, :],
                                         self.psA[:, s2a, :])
                i = nc.vector.tensor_mul(self.s_mid[:, e * 8 + hc, :],
                                         self.s_t1[:, t2slot, :], self.s_gb[:, e, :])
                ev_m = P.inc(i, "dv")
                self.rA.release(s2a, ev_m)
                t1rot.release(tslot, ev_m)
                t1rot.release(t2slot, ev_m)
            ev_wdone = P.ev("mm")
            wrot.release(sl1, ev_wdone)
            wrot.release(sl2, ev_wdone)
            if nxt < 32:
                e2, bp2 = order[nxt]
                sl1, evf1 = wrot.acquire()
                sl2, evf2 = wrot.acquire()
                pend[nxt] = ((sl1, wload4("w12", e2 * 8 + bp2, sl1, evf1)),
                             (sl2, wload4("w12", e2 * 8 + 4 + bp2, sl2, evf2)))
                nxt += 1

        if layer == 1:
            self._hnd2 = self._ln_stats_begin("A")
        ev_mid_d = P.ev("dv")
        pend = {}
        for ii in range(2):
            slot, evf = wrot.acquire()
            pend[ii] = (slot, wload4("w3", ii, slot, evf))
        nxt = 2
        xin_rot = Rot(2)
        for m in range(16):
            for half in range(2):
                ii = m * 2 + half
                slot, ev_w = pend.pop(ii)
                wv = self.s_w[:, slot, :].rearrange("p (k m) -> p k m", m=128)
                if half == 0:
                    sB, evB = self.rB.acquire()
                    P.wait("tensor", [evB, ev_mid_d])
                P.wait("tensor", ev_w)
                last = None
                for idx2 in range(32):
                    e4, kc = idx2 // 8, idx2 % 8
                    e = half * 4 + e4
                    last = nc.tensor.matmul(self.psB[:, sB, :], wv[:, idx2, :],
                                            self.s_mid[:, e * 8 + kc, :],
                                            start=(half == 0 and idx2 == 0),
                                            stop=(half == 1 and idx2 == 31))
                ev_mm = P.inc(last, "mm")
                wrot.release(slot, ev_mm)
                if nxt < 32:
                    slot2, evf2 = wrot.acquire()
                    pend[nxt] = (slot2, wload4("w3", nxt, slot2, evf2))
                    nxt += 1
            xs, evx = xin_rot.acquire()
            P.wait("sync", evx)
            P.wait("sync", P.ev("ls"))
            i = nc.sync.dma_start(out=self.s_xc[:, xs, :], in_=self.xr_d[:, m, ts])
            P.inc(i, "lx", 16)
            P.wait("vector", [ev_mm, P.ev("lx"), self._xout_free])
            i = nc.vector.scalar_tensor_tensor(
                self.s_xc[:, 2, :], self.psB[:, sB, :], 1.0,
                self.s_oacc[:, m, :], ALU.mult, ALU.add)
            i = nc.vector.tensor_add(self.s_xc[:, 3, :], self.s_xc[:, 2, :],
                                     self.s_xc[:, xs, :])
            ev_x2 = P.inc(i, "dv")
            self.rB.release(sB, ev_x2)
            xin_rot.release(xs, ev_x2)
            relx = [ev_x2]
            if layer == 1:
                relx = self._ln_stat_chunk2(m, self._hnd2, self.s_xc[:, 3, :], ev_x2)
            P.wait("scalar", ev_x2)
            dst = self.x2_d[:, m, ts] if layer == 1 else self.out[:, m, ts]
            i = nc.scalar.dma_start(out=dst, in_=self.s_xc[:, 3, :])
            ev_sp = P.inc(i, "ls", 16)
            self._xout_free = relx + [ev_sp]
        self._oacc_free = [P.ev("mm"), P.ev("dv")]
        self._mid_free = P.ev("mm")
        self._ao_free = [P.ev("mm"), P.ev("dv")]

    def _router_gating(self, ev_y):
        nc, P = self.nc, self.P
        P.wait("tensor", [self._psZ_free, self.ev_router, ev_y])
        last = None
        for k in range(16):
            last = nc.tensor.matmul(self.psZ[0:8, 0, :], self.c_router[:, k * 8:(k + 1) * 8],
                                    self.s_y[:, k, :], start=(k == 0), stop=(k == 15))
        ev_mm = P.inc(last, "mm")
        P.wait("scalar", ev_mm)
        i = nc.scalar.activation(self.s_gr[:, 1, :], self.psZ[0:8, 0, :], AF.Tanh,
                                 scale=FP(0.5))
        ev_t = P.inc(i, "ac")
        P.wait("vector", ev_t)
        i = nc.vector.tensor_scalar(self.s_gr[:, 0, :], self.s_gr[:, 1, :], 0.5, 0.5,
                                    ALU.mult, ALU.add)
        ev_sc = P.inc(i, "dv")
        self._psZ_free = ev_sc
        i = nc.vector.tensor_scalar_add(self.s_gr[:, 1, :], self.s_gr[:, 0, :],
                                        self.c_ebias[:, 0:1])
        nc.vector.tensor_copy(self.s_gr16[:, 0, :], self.s_gr[:, 0, :])
        i = nc.vector.tensor_copy(self.s_gr16[:, 1, :], self.s_gr[:, 1, :])
        ev_16 = P.inc(i, "dv")
        P.wait("tensor", [ev_16, self._psT_free])
        with nc.allow_low_precision(reason="router transpose staging"):
            for d in range(4):
                nc.tensor.transpose(self.psT[:, d, 0:8], self.s_gr16[0:8, 1, 128 * d:128 * (d + 1)],
                                    self.c_ident[0:8, 0:8])
            last = None
            for d in range(4):
                last = nc.tensor.transpose(self.psT[:, 4 + d, 0:8],
                                           self.s_gr16[0:8, 0, 128 * d:128 * (d + 1)],
                                           self.c_ident[0:8, 0:8])
        ev_tr = P.inc(last, "mm")
        P.wait("vector", ev_tr)
        i = nc.vector.tensor_copy(self.s_gtm16[:, 0:8, :], self.psT[:, 0:8, 0:8])
        ev_tm = P.inc(i, "dv")
        self._psT_free = ev_tm
        nc.vector.tensor_copy(self.s_dbg[:, 0:8, :], self.s_gtm16[:, 0:8, :])
        g = self.s_gtm

        def _sync(i):
            ev = P.inc(i, "dv")
            P.wait("vector", ev)
            return ev

        for d in range(4):
            sel = self.s_gtm16[:, d, :]
            sco = self.s_gtm16[:, 4 + d, :]
            _sync(nc.vector.reduce_max(g[:, 0, 0:1], sel, AX.X))
            _sync(nc.vector.tensor_scalar(g[:, 1, :], sel, g[:, 0, 0:1], None, ALU.is_ge))
            _sync(nc.vector.scalar_tensor_tensor(g[:, 2, :], g[:, 1, :], -1e30, sel,
                                                 ALU.mult, ALU.add))
            _sync(nc.vector.reduce_max(g[:, 3, 0:1], g[:, 2, :], AX.X))
            _sync(nc.vector.tensor_scalar(g[:, 4, :], sel, g[:, 3, 0:1], None, ALU.is_ge))
            _sync(nc.vector.tensor_mul(g[:, 5, :], g[:, 4, :], sco))
            _sync(nc.vector.reduce_sum(g[:, 6, 0:1], g[:, 5, :], AX.X))
            _sync(nc.vector.tensor_scalar_add(g[:, 6, 0:1], g[:, 6, 0:1], 1e-9))
            _sync(nc.vector.reciprocal(g[:, 7, 0:1], g[:, 6, 0:1]))
            ev_g = _sync(nc.vector.tensor_scalar(self.s_gtm16[:, d, :], g[:, 5, :],
                                                 g[:, 7, 0:1], None, ALU.mult))
            if d == 0:
                _sync(nc.vector.tensor_copy(self.s_dbg[:, 8:16, :], g[:, :, :]))
        nc.vector.tensor_copy(self.s_dbg[:, 8:12, :], self.s_gtm16[:, 0:4, :])
        P.wait("tensor", [ev_g, self._psT_free])
        with nc.allow_low_precision(reason="router transpose staging"):
            last = None
            for d in range(4):
                last = nc.tensor.transpose(self.psT[0:8, d, :],
                                           self.s_gtm16[:, d, :],
                                           self.c_ident[:, :])
        ev_tr2 = P.inc(last, "mm")
        P.wait("vector", ev_tr2)
        last = None
        for d in range(4):
            last = nc.vector.tensor_copy(self.s_gf[0:8, 128 * d:128 * (d + 1)],
                                         self.psT[0:8, d, 0:128])
        ev_gf = P.inc(last, "dv")
        self._psT_free = ev_gf
        for e in range(8):
            sA, evA = self.rA.acquire()
            P.wait("tensor", [ev_gf, evA])
            i = nc.tensor.matmul(self.psA[:, sA, :], self.c_onehot[0:8, e * 128:(e + 1) * 128],
                                 self.s_gf[0:8, :], start=True, stop=True)
            ev_b = P.inc(i, "mm")
            P.wait("vector", [ev_b, self._gb_free])
            i = nc.vector.tensor_copy(self.s_gb[:, e, :], self.psA[:, sA, :])
            ev_c = P.inc(i, "dv")
            self.rA.release(sA, ev_c)
        self._gb_ready = P.ev("dv")
        self._gb_free = None

    def allgather(self, which):
        nc, P = self.nc, self.P
        groups = [[0, 1, 2, 3], [4, 5, 6, 7]]
        P.wait("gpsimd", P.ev("ls"))
        if which == 1:
            i = nc.gpsimd.collective_compute(
                "AllGather", ALU.bypass, replica_groups=groups,
                ins=[self.ag1_in.opt()], outs=[self.ag1_out.opt()])
            return P.inc(i, "ag", 1)
        self.ev_ag2g = []
        for gi in range(4):
            rs = slice(512 * gi, 512 * (gi + 1))
            i = nc.gpsimd.collective_compute(
                "AllGather", ALU.bypass, replica_groups=groups,
                ins=[self.ag2_in[rs, :].opt()], outs=[self.agk_out[gi].opt()])
            P.inc(i, "ag", 1)
            rs2 = slice(2048 + 512 * gi, 2048 + 512 * (gi + 1))
            i = nc.gpsimd.collective_compute(
                "AllGather", ALU.bypass, replica_groups=groups,
                ins=[self.ag2_in[rs2, :].opt()], outs=[self.agv_out[gi].opt()])
            ev = P.inc(i, "ag", 1)
            self.ev_ag2g.append(ev)
        return self.ev_ag2g[-1]

    def tap(self, name, src, shape, dt):
        self.taps.append((name, src, shape, dt))

    def finish(self):
        nc, P = self.nc, self.P
        if getattr(self, "_pend_conv", None):
            self._emit_conv_chunks(10000)
        for sname in list(P.sems):
            if sname != "z":
                P.wait("gpsimd", P.ev(sname))
        for name, src, shape, dt in self.taps:
            o = nc.dram_tensor("tap_" + name, shape, dt, kind="ExternalOutput").ap()
            i = nc.gpsimd.dma_start(out=o, in_=src)
            P.inc(i, "lg", 16)
        P.wait("gpsimd", P.ev("lg"))
        for sh in P.sems.values():
            nc.gpsimd.sem_clear(sh)
        self.ctx.close()
        return self.nc

    def build(self):
        self._materialize_sbuf()
        self.load_consts()
        self.ev_conv = {}
        self.convert_weights(["wqkv", "router"])
        nc, P = self.nc, self.P
        P.wait("sync", self.ev_conv["router"])
        i = nc.sync.dma_start(out=self.c_router[:], in_=self.wb["router"][0])
        P.inc(i, "lx", 16)
        self.ev_router = P.ev("lx")
        self.phase0()
        ev_ag1 = self.allgather(1)
        self._pend_conv = []
        self.convert_weights(["wfc", "sw1", "sw2", "sw3", "w12", "w3",
                              "wq", "wk", "wv", "wo"], defer=True)
        if self.upto == "p0":
            self.tap("qa", self.qa_d, [128, 16, 2048], BF16)
            self.tap("ag1", self.ag1_out, [1024, 2048], BF16)
            return self.finish()
        P = self.P
        for e in ("sync", "tensor", "scalar", "vector"):
            P.wait(e, ev_ag1)
        self.flash_layer("a")
        self._emit_conv_chunks(10000)
        if self.upto == "p1a":
            self.tap("aoa", self.aoa_d, [128, 16, 2048], BF16)
            return self.finish()
        self.phase_mid(1)
        if self.upto == "p1b":
            self.tap("x2", self.x2_d, [128, 16, 2048], F32)
            self.tap("qb", self.qb_d, [128, 16, 2048], BF16)
            self.tap("ag2in", self.ag2_in, [4096, 2048], BF16)
            self.tap("xr", self.xr_d, [128, 16, 2048], F32)
            self.tap("mid", self.s_mid[:, :, :], [128, 64, 512], BF16)
            self.tap("oacc", self.s_oacc[:, :, :], [128, 16, 512], BF16)
            self.tap("gb", self.s_gb[:, :, :], [128, 8, 512], BF16)
            self.tap("ylast", self.s_y[:, :, :], [128, 16, 512], BF16)
            self.tap("gf", self.s_gf[:, :], [8, 512], BF16)
            self.tap("gtm16", self.s_gtm16[:, :, :], [128, 8, 8], BF16)
            self.tap("gr", self.s_gr[:, :, :], [8, 2, 512], F32)
            self.tap("gr16", self.s_gr16[:, :, :], [8, 2, 512], BF16)
            self.tap("dbg", self.s_dbg[:, :, :], [128, 16, 8], BF16)
            self.tap("dbg2", self.s_dbg2[:, :, :], [128, 4, 512], BF16)
            self.tap("wsw1b0", self.wb["sw1"][0:1], [1, 128, 4096], BF16)
            self.tap("wsw3b0", self.wb["sw3"][0:1], [1, 128, 4096], BF16)
            return self.finish()
        self.allgather(2)
        self.flash_layer("b")
        if self.upto == "p2a":
            self.tap("aob", self.aob_d, [128, 16, 2048], BF16)
            return self.finish()
        self.phase_mid(2)
        return self.finish()


_CACHE = {}

def _get_nc(upto):
    if upto not in _CACHE:
        _CACHE[upto] = Builder(upto=upto).build()
    return _CACHE[upto]


def kernel(**inputs):
    upto = os.environ.get("ATHENA_UPTO", "full")
    nc = _get_nc(upto)
    in_maps = make_in_maps(inputs)
    res = run_bass_kernel_spmd(nc, in_maps, core_ids=list(range(8)))
    return reassemble(res.results)
